# version 24
# baseline (speedup 1.0000x reference)
import contextlib
import numpy as np
import concourse.bass as bass
import concourse.mybir as mybir
from concourse.bass_utils import run_bass_kernel_spmd

F32 = mybir.dt.float32
BF16 = mybir.dt.bfloat16
AF = mybir.ActivationFunctionType
ALU = mybir.AluOpType
AX = mybir.AxisListType

S = 2048
D = 2048
NQ, NKV, HDIM = 32, 8, 64
LD = 1024
HID = [1024, 2048, 3072, 4096, 1024, 2048, 3072, 4096]
NH = [h // 128 for h in HID]
NEXP = 8
EPS = 1e-5
ESHIFT = 24.0
ACH = 256
NACH = S // ACH
FCH = 512
NFCH = S // FCH
COST_LAMBDA = 1e-7
UPTO = "all"
NCORES = 8


class _StopBuild(Exception):
    pass

ENGINES = ("tensor", "vector", "scalar", "sync", "gpsimd")
COMPUTE = ("tensor", "vector", "scalar")
QUEUES = ("sync", "gpsimd")
PSUM_RES = {("P0",), ("P1",), ("P2",), ("P3",), ("PW0",), ("PW1",)}

CONST_VALS = [0.0, 1.0, -1.0, 0.125, -ESHIFT, EPS, 1.0 / D, -0.5, 0.1, -1000.0]


class Sched:

    def __init__(self):
        self.barrier_fns = {}
        self.ops = {e: [] for e in ENGINES}
        self.counts = {e: 0 for e in ENGINES}
        self.know = {e: {} for e in ENGINES}
        self.snap = {e: [] for e in ENGINES}
        self.res = {}
        self.pending = {e: {} for e in ENGINES}
        self.wait_targets = {e: set() for e in COMPUTE}
        self.streams = {}
        self.meta = {}

    def _stream_add(self, eng, idx, sid, kind):
        st = self.streams.setdefault(sid, {"kind": kind, "ops": [], "queue": eng})
        assert st["kind"] == kind, sid
        if kind == "group":
            assert st["queue"] == eng, (sid, "group streams must be single-queue")
        k = len(st["ops"])
        st["ops"].append((eng, idx))
        self.meta[(eng, idx)] = (sid, k)

    def _need(self, eng, waits, te, ti, raw=False):
        if te in COMPUTE:
            if te == eng and eng == "tensor":
                return
            if self.know[eng].get(te, 0) >= ti + 1:
                return
            waits[te] = max(waits.get(te, 0), ti + 1)
        else:
            sid, k = self.meta[(te, ti)]
            if self.know[eng].get(sid, 0) >= k + 1:
                return
            waits[sid] = max(waits.get(sid, 0), k + 1)

    def _learn(self, eng, waits):
        for key, c in list(waits.items()):
            if self.know[eng].get(key, 0) >= c:
                del waits[key]
                continue
            self.know[eng][key] = c
            if key in COMPUTE:
                st = self.snap[key][c - 1]
            else:
                we, wi = self.streams[key]["ops"][c - 1]
                st = self.snap[we][wi]
            for f, v in st.items():
                if f != eng and v > self.know[eng].get(f, 0):
                    self.know[eng][f] = v
            if key in COMPUTE:
                self.wait_targets[key].add(c - 1)

    def op(self, eng, fn, reads=(), writes=(), stream=None):
        waits = dict(self.pending[eng])
        exact = set(waits.keys())
        self.pending[eng] = {}
        for r in reads:
            rec = self.res.get(r)
            if rec and rec["w"] is not None:
                self._need(eng, waits, *rec["w"], raw=True)
            if rec and r in PSUM_RES and eng in ("scalar", "vector"):
                for (re_, ri) in rec["r"]:
                    if re_ != eng and re_ in ("scalar", "vector"):
                        self._need(eng, waits, re_, ri)
        for r in writes:
            rec = self.res.get(r)
            if rec:
                if rec["w"] is not None:
                    self._need(eng, waits, *rec["w"])
                for (re_, ri) in rec["r"]:
                    self._need(eng, waits, re_, ri)
        exact &= set(waits.keys())
        self._learn(eng, waits)
        exact &= set(waits.keys())
        idx = self.counts[eng]
        self.ops[eng].append((fn, waits, exact))
        self.counts[eng] = idx + 1
        if eng in QUEUES:
            assert stream is not None, "queue ops need a stream"
            self._stream_add(eng, idx, stream[0], stream[1])
            st = dict(self.know[eng])
            sid, k = self.meta[(eng, idx)]
            if self.streams[sid]["kind"] == "slot":
                st[sid] = max(st.get(sid, 0), k + 1)
        else:
            st = dict(self.know[eng])
            st[eng] = idx + 1
        self.snap[eng].append(st)
        for r in reads:
            rec = self.res.setdefault(r, {"w": None, "r": []})
            rec["r"].append((eng, idx))
        for r in writes:
            self.res[r] = {"w": (eng, idx), "r": []}
        return (eng, idx)

    def barrier(self, engines=("tensor", "vector", "scalar", "sync")):
        ccur = {e: self.counts[e] for e in COMPUTE if self.counts[e] > 0}
        scur = {sid: len(st["ops"]) for sid, st in self.streams.items()
                if st["ops"] and st["queue"] in engines}
        for e in engines:
            if e in QUEUES:
                continue
            waits = {}
            for f, c in ccur.items():
                if f != e and self.know[e].get(f, 0) < c:
                    waits[f] = c
            for sid, c in scur.items():
                if self.know[e].get(sid, 0) < c:
                    waits[sid] = c
            exact = set(k for k in waits if k not in COMPUTE)
            self._learn(e, waits)
            exact &= set(waits.keys())
            idx = self.counts[e]
            bfn = self.barrier_fns.get(e, lambda en: en.nop())
            self.ops[e].append((bfn, waits, exact))
            self.counts[e] = idx + 1
            st = dict(self.know[e])
            st[e] = idx + 1
            self.snap[e].append(st)
        post = {e: self.counts[e] for e in COMPUTE if self.counts[e] > 0}
        for e in QUEUES:
            for f, c in post.items():
                self.pending[e][f] = max(self.pending[e].get(f, 0), c)

    def mark_final(self, eng_idx):
        pass

    def emit(self, block, nc, top):
        sems = {}
        for e in COMPUTE:
            sems[e] = top.enter_context(nc.semaphore(f"sem_{e}"))
        for sid in self.streams:
            sems[sid] = top.enter_context(nc.semaphore(f"sem_s_{sid}"))
        incmap = {}
        for e in COMPUTE:
            m = {}
            for j, oi in enumerate(sorted(self.wait_targets[e])):
                m[oi] = j + 1
            incmap[e] = m
        totals = {sid: len(st["ops"]) for sid, st in self.streams.items()}

        def replay(e, eng):
            for i, (fn, waits, exact) in enumerate(self.ops[e]):
                for key in sorted(waits, key=str):
                    c = waits[key]
                    if key in COMPUTE:
                        eng.wait_ge(sems[key], incmap[key][c - 1])
                    else:
                        st = self.streams[key]
                        if st["kind"] == "slot" or key in exact:
                            eng.wait_ge(sems[key], 16 * c)
                        elif st["queue"] == e:
                            m = sum(1 for (se, si) in st["ops"] if si < i)
                            assert m >= c, (key, c, m)
                            eng.wait_ge(sems[key], 16 * m)
                        else:
                            eng.wait_ge(sems[key], 16 * totals[key])
                inst = fn(eng)
                if e in COMPUTE:
                    if i in incmap[e]:
                        inst.then_inc(sems[e], 1)
                else:
                    sid, k = self.meta[(e, i)]
                    inst.then_inc(sems[sid], 16)
            if e in QUEUES:
                done = set()
                for i in range(len(self.ops[e])):
                    sid, k = self.meta[(e, i)]
                    done.add(sid)
                for sid in sorted(done, key=str):
                    eng.wait_ge(sems[sid], 16 * totals[sid])

        @block.tensor
        def _(eng):
            replay("tensor", eng)

        @block.vector
        def _(eng):
            replay("vector", eng)

        @block.scalar
        def _(eng):
            replay("scalar", eng)

        @block.sync
        def _(eng):
            replay("sync", eng)

        @block.gpsimd
        def _(eng):
            replay("gpsimd", eng)


def host_constants():
    c = {}
    k = np.arange(0, HDIM, 2)[: HDIM // 2].astype(np.float64)
    inv_freq = 1.0 / (10000.0 ** (k / HDIM))
    pos = np.arange(S, dtype=np.float64)
    ang = pos[:, None] * inv_freq[None, :]
    ang2 = np.concatenate([ang, ang], axis=1)
    cos64 = np.cos(ang2).T
    sin64 = np.sin(ang2).T
    sgn = np.ones((64, 1))
    sgn[:32] = -1.0
    sins = sin64 * sgn
    c["cosT"] = np.ascontiguousarray(np.concatenate([cos64, cos64], 0).astype(np.float32))
    c["sinT"] = np.ascontiguousarray(np.concatenate([sins, sins], 0).astype(np.float32))
    p = np.arange(128)[:, None]
    j = np.arange(896)[None, :] - 384
    m01 = (j >= p).astype(np.float32)
    c["m01"] = np.ascontiguousarray(m01)
    c["m01p"] = np.ascontiguousarray(
        np.concatenate([m01[:, 384:640], m01[:, 256:512]], axis=1))
    c["m01d384"] = np.ascontiguousarray(np.tile(m01[:, 384:640], (1, 2)))
    c["m01d256"] = np.ascontiguousarray(np.tile(m01[:, 256:512], (1, 2)))
    c["ident"] = np.eye(128, dtype=np.float32)
    pm = np.zeros((128, 128), dtype=np.float32)
    idx = np.arange(128)
    pm[idx ^ 32, idx] = 1.0
    c["permswap"] = pm
    c["lamcost"] = (np.float32(COST_LAMBDA) * np.array(
        [np.float32(LD * h + h * LD) for h in HID], dtype=np.float32)).reshape(1, 8)
    return c


def col_layout(b):
    return np.ascontiguousarray(np.asarray(b, np.float32).reshape(-1, 128).T)


def qplace(h):
    g = h // 4
    return (g // 2) * 4 + (h % 4), 64 * (g % 2)


def q_perm():
    pos = np.empty(D, dtype=np.int64)
    for h in range(NQ):
        t, off = qplace(h)
        pos[t * 128 + off: t * 128 + off + 64] = np.arange(h * 64, (h + 1) * 64)
    return pos


def build_program():
    nc = bass.Bass("TRN2", target_bir_lowering=False)
    s = Sched()

    def _ckpt(name):
        if UPTO == name:
            raise _StopBuild()
    top = contextlib.ExitStack()

    def sb(stk, name, shape, dtype):
        return stk.enter_context(nc.sbuf_tensor(name, shape, dtype))

    def dma(q, out, in_, reads=(), writes=(), stream=None):
        return s.op(q, (lambda o, i: lambda e: e.dma_start(out=o, in_=i))(out, in_),
                    reads=reads, writes=writes, stream=stream)

    def mm(out, lhsT, rhs, start, stop, reads=(), writes=()):
        return s.op("tensor",
                    (lambda o, l, r, st, sp: lambda e: e.matmul(o, l, r, start=st, stop=sp))(
                        out, lhsT, rhs, start, stop), reads=reads, writes=writes)

    def tr(out, in_, ident, reads=(), writes=()):
        return s.op("tensor", (lambda o, i, d: lambda e: e.transpose(o, i, d))(out, in_, ident),
                    reads=reads, writes=writes)

    def act(out, in_, func, bias=0.0, scale=1.0, reads=(), writes=()):
        return s.op("scalar",
                    (lambda o, i, f, b, sc: lambda e: e.activation(o, i, f, bias=b, scale=sc))(
                        out, in_, func, bias, scale), reads=reads, writes=writes)

    def scopy(out, in_, reads=(), writes=()):
        return s.op("scalar", (lambda o, i: lambda e: e.copy(o, i))(out, in_),
                    reads=reads, writes=writes)

    def vop(name, *args, reads=(), writes=()):
        return s.op("vector", (lambda n, a: lambda e: getattr(e, n)(*a))(name, args),
                    reads=reads, writes=writes)

    def memset(ap, val, writes=()):
        return s.op("vector", (lambda a, v: lambda e: e.memset(a, v))(ap, val), writes=writes)

    def din(name, shape):
        return nc.dram_tensor(name, list(shape), F32, kind="ExternalInput").ap()

    def dint(name, shape, dt):
        return nc.dram_tensor(name, list(shape), dt, kind="Internal").ap()

    x_d = din("x", (S, D))
    y_d = nc.dram_tensor("y", [S, D], F32, kind="ExternalOutput").ap()
    wqT_d = din("wqT", (D, D))
    wkT_d = din("wkT", (D, NKV * HDIM))
    wvT_d = din("wvT", (D, NKV * HDIM))
    woT_d = din("woT", (D, D))
    down_d = din("down_w", (D, LD))
    up_d = din("up_w", (LD, D))
    core_d = din("core_w", (D, D))
    sw1_d = din("sw1", (LD, LD))
    sw2_d = din("sw2", (LD, LD))
    w1_d = [din(f"w1_{e}", (LD, HID[e])) for e in range(NEXP)]
    w2_d = [din(f"w2_{e}", (HID[e], LD)) for e in range(NEXP)]
    downb_d = din("down_b_col", (128, LD // 128))
    upb_d = din("up_b_col", (128, D // 128))
    coreb_d = din("core_b_col", (128, D // 128))
    b1all_d = din("b1all_col", (128, sum(NH) + 8))
    b2row_d = din("b2_cat_row", (1, NEXP * LD))
    sb2row_d = din("sb2_row", (1, LD))
    rbrow_d = din("rb_row", (1, NEXP))
    lam_d = din("lamcost", (1, NEXP))
    rwt_d = din("router_w_t", (128, 16 * NEXP))
    cosT_d = din("cosT", (128, S))
    sinT_d = din("sinT", (128, S))
    m01_d = din("m01", (128, 896))
    m01p_d = din("m01p", (128, 512))
    m01d384_d = din("m01d384", (128, 512))
    m01d256_d = din("m01d256", (128, 512))
    ident_d = din("ident", (128, 128))
    perm_d = din("permswap", (128, 128))
    consts_d = din("constpool", (128, len(CONST_VALS)))

    x1_dram = dint("x1_scr", (S, D), F32)
    ln1T_dram = dint("ln1T_scr", (16, 128, S), BF16)
    ln3T_dram = dint("ln3T_scr", (16, 128, S), BF16)
    wq_c = dint("wq_c", (16, 128, 16, 128), BF16)
    wo_c = dint("wo_c", (16, 128, 16, 128), BF16)
    down_c = dint("down_c", (8, 128, 16, 128), BF16)
    up_c = dint("up_c", (16, 128, 8, 128), BF16)
    core_c = dint("core_c", (16, 128, 16, 128), BF16)
    sw1_c = dint("sw1_c", (8, 128, 8, 128), BF16)
    w1_c = [dint(f"w1c_{e}", (NH[e], 128, 8, 128), BF16) for e in range(NEXP)]
    sw2_c = dint("sw2_c", (LD, LD), BF16)
    w2_c = [dint(f"w2c_{e}", (HID[e], LD), BF16) for e in range(NEXP)]
    ustack_dram = dint("ustack_scr", (10, LD), BF16)

    cpool = sb(top, "cpool", [128, len(CONST_VALS)], F32)
    for i, v in enumerate(CONST_VALS):
        nc.const_aps.register(F32, float(v), cpool[:, i:i + 1])
    ident_b = sb(top, "ident_b", [128, 128], BF16)
    ident_f = sb(top, "ident_f", [128, 128], F32)
    perm_b = sb(top, "perm_b", [128, 128], BF16)
    ones1 = sb(top, "ones1", [1, 128], BF16)
    onesc = sb(top, "onesc", [128, 1], BF16)
    m01_b = sb(top, "m01_b", [128, 896], BF16)
    m01p_b = sb(top, "m01p_b", [128, 512], BF16)
    m01d_b = [sb(top, f"m01d{i}_b", [128, 512], BF16) for i in range(2)]
    cvt = [sb(top, f"cvt{i}", [128, 2048], BF16) for i in range(2)]
    cbst = [sb(top, f"cbst{i}", [128, 16 * 128], BF16) for i in range(3)]
    w2st = [sb(top, f"w2st{i}", [128, LD], BF16) for i in range(4)]
    downb = sb(top, "downb", [128, LD // 128], F32)
    bcol = sb(top, "bcol", [128, D // 128], F32)
    colt = sb(top, "colt", [128, 16], F32)
    b1all = sb(top, "b1all", [128, sum(NH) + 8], F32)
    gb1 = sb(top, "gb1", [128, sum(NH) + 8], BF16)
    rw_sb = sb(top, "rw_sb", [128, 16 * NEXP], F32)
    rw_hi = sb(top, "rw_hi", [128, 16 * NEXP], BF16)
    rw_lo = sb(top, "rw_lo", [128, 16 * NEXP], BF16)
    rbl = sb(top, "rbl", [1, 4 * NEXP], F32)
    rbl_hi = sb(top, "rbl_hi", [1, NEXP], BF16)
    rbl_lo = sb(top, "rbl_lo", [1, NEXP], BF16)
    w_tm = sb(top, "w_tm", [128, 16 * 9], F32)
    wT_sb = sb(top, "wT_sb", [10, S], BF16)
    ustack = sb(top, "ustack", [10, LD], BF16)
    crow = sb(top, "crow", [1, LD], F32)
    urow = sb(top, "urow", [1, LD], F32)

    barc = sb(top, "barc", [1, 2], F32)
    s.barrier_fns = {
        "vector": (lambda en: en.memset(barc[0:1, 0:1], 0.0)),
        "scalar": (lambda en: en.memzero(barc[0:1, 1:2])),
    }
    P0 = top.enter_context(nc.psum_tensor("P0", [128, 512], F32))
    P1 = top.enter_context(nc.psum_tensor("P1", [128, 512], F32))
    P2 = top.enter_context(nc.psum_tensor("P2", [128, 512], F32))
    P3 = top.enter_context(nc.psum_tensor("P3", [128, 512], F32))
    PW0 = top.enter_context(nc.psum_tensor("PW0", [128, 1024], F32))
    PW1 = top.enter_context(nc.psum_tensor("PW1", [128, 1024], F32))
    PS = [P0, P1, P2, P3]
    PN = [("P0",), ("P1",), ("P2",), ("P3",)]

    print('SBUF free after static:', nc.sbuf_bytes_remaining)
    _open = []

    def _mk():
        st = contextlib.ExitStack()
        _open.append(st)
        return st

    def _close(st):
        assert _open and _open[-1] is st
        _open.pop()
        st.close()

    try:
        init = _mk()
        rowtmp = sb(init, "rowtmp", [1, NEXP * LD], F32)
        dma("gpsimd", ident_b[:], ident_d, writes=[("identb",)], stream=("gi", "group"))
        dma("sync", ident_f[:], ident_d, writes=[("identf",)], stream=("si", "group"))
        dma("gpsimd", perm_b[:], perm_d, writes=[("permb",)], stream=("gi", "group"))
        dma("gpsimd", m01_b[:], m01_d, writes=[("m01",)], stream=("gi", "group"))
        dma("gpsimd", m01p_b[:], m01p_d, writes=[("m01p",)], stream=("gi", "group"))
        dma("gpsimd", m01d_b[0][:], m01d384_d, writes=[("m01d", 0)], stream=("gi", "group"))
        dma("gpsimd", m01d_b[1][:], m01d256_d, writes=[("m01d", 1)], stream=("gi", "group"))
        dma("sync", cpool[:], consts_d, writes=[("cpool",)], stream=("si", "group"))
        dma("sync", downb[:], downb_d, writes=[("downb",)], stream=("si", "group"))
        dma("sync", bcol[:], upb_d, writes=[("bcol",)], stream=("si", "group"))
        dma("sync", colt[:], coreb_d, writes=[("coltc",)], stream=("si", "group"))
        vop("tensor_add", bcol[:], bcol[:], colt[:], reads=[("bcol",), ("coltc",)],
            writes=[("bcol",)])
        dma("sync", b1all[:], b1all_d, writes=[("b1all",)], stream=("si", "group"))
        act(gb1[:], b1all[:], AF.Gelu, reads=[("b1all",)], writes=[("gb1",)])
        dma("sync", rowtmp[:], b2row_d, writes=[("rowtmp",)], stream=("si", "group"))
        dma("sync", rbl[0:1, 0:NEXP], rbrow_d, writes=[("rbl",)], stream=("si", "group"))
        dma("sync", rbl[0:1, 16:16 + NEXP], lam_d, writes=[("rbllam",)], stream=("si", "group"))
        vop("tensor_sub", rbl[0:1, 0:NEXP], rbl[0:1, 0:NEXP], rbl[0:1, 16:16 + NEXP],
            reads=[("rbl",), ("rbllam",)], writes=[("rbl",)])
        scopy(rbl_hi[:], rbl[0:1, 0:NEXP], reads=[("rbl",)], writes=[("rblhi",)])
        scopy(rbl[0:1, 8:8 + NEXP], rbl_hi[:], reads=[("rblhi",)], writes=[("rblhf",)])
        vop("tensor_sub", rbl_lo[:], rbl[0:1, 0:NEXP], rbl[0:1, 8:8 + NEXP],
            reads=[("rbl",), ("rblhf",)], writes=[("rbllo",)])
        dma("sync", rw_sb[:], rwt_d, writes=[("rw",)], stream=("si", "group"))
        scopy(rw_hi[:], rw_sb[:], reads=[("rw",)], writes=[("rwhi",)])
        vop("tensor_sub", rw_lo[:], rw_sb[:], rw_hi[:], reads=[("rw",), ("rwhi",)],
            writes=[("rwlo",)])
        memset(ones1[:], 1.0, writes=[("ones1",)])
        memset(onesc[:], 1.0, writes=[("onesc",)])
        memset(wT_sb[0:10, :], 1.0, writes=[("wTones",)])
        vop("tensor_add", rowtmp[0:1, 0:4096], rowtmp[0:1, 0:4096], rowtmp[0:1, 4096:8192],
            reads=[("rowtmp",)], writes=[("rowtmp",)])
        vop("tensor_add", rowtmp[0:1, 0:2048], rowtmp[0:1, 0:2048], rowtmp[0:1, 2048:4096],
            reads=[("rowtmp",)], writes=[("rowtmp",)])
        vop("tensor_add", crow[0:1, :], rowtmp[0:1, 0:1024], rowtmp[0:1, 1024:2048],
            reads=[("rowtmp",)], writes=[("crow",)])
        cvt_i = [0]

        def convert_tiled(src, dst4, R, C):
            for c0 in range(R // 128):
                for cc in range(0, C, 2048):
                    w = min(2048, C - cc)
                    sl = cvt_i[0] % 2
                    cvt_i[0] += 1
                    rid = ("cvt", sl)
                    dma("gpsimd", cvt[sl][:, 0:w], src[c0 * 128:(c0 + 1) * 128, cc:cc + w],
                        writes=[rid], stream=(f"cvt{sl}", "slot"))
                    dma("gpsimd", dst4[cc // 128:(cc + w) // 128, :, c0, :].transpose([1, 0, 2]),
                        cvt[sl][:, 0:w].rearrange("p (j o) -> p j o", o=128),
                        reads=[rid], writes=[("c_" + dst4.name,)],
                        stream=("c_" + dst4.name, "group"))

        def convert_plain(src, dst, R, C):
            for c0 in range(R // 128):
                sl = cvt_i[0] % 2
                cvt_i[0] += 1
                rid = ("cvt", sl)
                dma("gpsimd", cvt[sl][:, 0:C], src[c0 * 128:(c0 + 1) * 128, :], writes=[rid],
                    stream=(f"cvt{sl}", "slot"))
                dma("gpsimd", dst[c0 * 128:(c0 + 1) * 128, :], cvt[sl][:, 0:C],
                    reads=[rid], writes=[("c_" + dst.name,)], stream=("c_" + dst.name, "group"))

        convert_tiled(wqT_d, wq_c, D, D)
        convert_tiled(woT_d, wo_c, D, D)
        for e in range(NEXP):
            convert_plain(w2_d[e], w2_c[e], HID[e], LD)
        convert_tiled(down_d, down_c, D, LD)
        convert_plain(sw2_d, sw2_c, LD, LD)
        convert_tiled(sw1_d, sw1_c, LD, LD)
        for e in range(NEXP):
            convert_tiled(w1_d[e], w1_c[e], LD, HID[e])
        convert_tiled(up_d, up_c, LD, D)
        convert_tiled(core_d, core_c, D, D)
        s.barrier()
        _close(init)

        spanKV = _mk()
        K_sb = sb(spanKV, "K_sb", [128, 4 * S], BF16)
        V_sb = sb(spanKV, "V_sb", [128, 16 * 520], BF16)

        span01 = _mk()
        ln1T = sb(span01, "ln1T", [128, 16 * S], BF16)

        a0 = _mk()
        x_t = [sb(a0, f"x_t{i}", [128, D], F32) for i in range(2)]
        ln1f = [sb(a0, f"ln1f{i}", [128, D], BF16) for i in range(2)]
        sqt = sb(a0, "sqt", [128, D], F32)
        colsA = sb(a0, "colsA", [128, 8], F32)

        def ln_token(xap, sqap, mcol, rcol, m2col, statres, xtag, sqtag):
            vop("reduce_sum", mcol, xap, AX.X, reads=[xtag], writes=[statres])
            act(sqap, xap, AF.Square, reads=[xtag], writes=[sqtag])
            vop("reduce_sum", rcol, sqap, AX.X, reads=[sqtag], writes=[statres])
            vop("tensor_scalar", mcol, mcol, 1.0 / D, None, ALU.mult,
                reads=[statres], writes=[statres])
            vop("tensor_scalar", rcol, rcol, 1.0 / D, None, ALU.mult,
                reads=[statres], writes=[statres])
            vop("tensor_tensor", m2col, mcol, mcol, ALU.mult, reads=[statres], writes=[statres])
            vop("tensor_sub", rcol, rcol, m2col, reads=[statres], writes=[statres])
            act(rcol, rcol, AF.Ln, bias=EPS, reads=[statres], writes=[statres])
            act(rcol, rcol, AF.Exp, scale=-0.5, reads=[statres], writes=[statres])

        lnv = ln1T[:].rearrange("p (c t) -> p c t", c=16)
        for tt in range(16):
            sl = tt % 2
            xa, la = x_t[sl], ln1f[sl]
            xr = ("x_t", sl)
            dma("sync", xa[:], x_d[tt * 128:(tt + 1) * 128, :], writes=[xr],
                stream=(f"xt{sl}", "slot"))
            stat = ("lnA", sl)
            mcol, rcol, m2col = colsA[:, 3 * sl:3 * sl + 1], colsA[:, 3 * sl + 1:3 * sl + 2], \
                colsA[:, 3 * sl + 2:3 * sl + 3]
            ln_token(xa[:], sqt[:], mcol, rcol, m2col, stat, xr, ("sqA",))
            vop("tensor_scalar", la[:], xa[:], mcol, rcol, ALU.subtract, ALU.mult,
                reads=[xr, stat], writes=[("ln1f", sl)])
            for half in range(2):
                pp = PS[2 + half]
                ppb = pp[:].bitcast(BF16)
                pres = PN[2 + half]
                for j in range(8):
                    dc = half * 8 + j
                    tr(ppb[:, j * 128:(j + 1) * 128], la[:, dc * 128:(dc + 1) * 128], ident_b[:],
                       reads=[("ln1f", sl), ("identb",)], writes=[pres])
                scopy(lnv[:, half * 8:(half + 1) * 8, tt * 128:(tt + 1) * 128],
                      ppb[:, 0:1024].rearrange("p (c t) -> p c t", c=8),
                      reads=[pres], writes=[("ln1T",)])

        s.barrier()
        _close(a0)
        _ckpt("a0")

        print('SBUF free at A1(before allocs):', nc.sbuf_bytes_remaining)
        a1 = _mk()
        wvst = sb(a1, "wvst", [128, 16 * 512], BF16)
        wvstf = [sb(a1, f"wvstf{i}", [128, 512], F32) for i in range(2)]
        kstf = [sb(a1, f"kstf{i}", [128, 16 * 128], F32) for i in range(2)]
        cosc = sb(a1, "cosc", [128, 512], F32)
        sinc = sb(a1, "sinc", [128, 512], F32)
        kpb = [sb(a1, f"kpb{i}", [128, 512], BF16) for i in range(2)]
        rp1 = [sb(a1, f"rp1_{i}", [128, 512], F32) for i in range(2)]
        rp2 = [sb(a1, f"rp2_{i}", [128, 512], F32) for i in range(2)]

        memset(V_sb[:, 64::65], 1.0, writes=[("Vones",)])
        for dc in range(16):
            dma("sync", wvstf[dc % 2][:], wvT_d[dc * 128:(dc + 1) * 128, :],
                writes=[("wvstf", dc % 2)], stream=(f"wvst{dc % 2}", "slot"))
            vop("tensor_copy", wvst[:, dc * 512:(dc + 1) * 512], wvstf[dc % 2][:],
                reads=[("wvstf", dc % 2)], writes=[("wvst", dc)])
        vv = V_sb[:].rearrange("p (c x) -> p c x", c=16)
        for tt in range(16):
            pp = PS[tt % 2]
            pres = PN[tt % 2]
            for dc in range(16):
                mm(pp[:], lnv[:, dc, tt * 128:(tt + 1) * 128], wvst[:, dc * 512:(dc + 1) * 512],
                   dc == 0, dc == 15, reads=[("ln1T",), ("wvst", dc)], writes=[pres])
            scopy(vv[:, tt, :].rearrange("p (h z) -> p h z", h=8)[:, :, 0:64],
                  pp[:].rearrange("p (h z) -> p h z", h=8),
                  reads=[pres], writes=[("V", tt)])

        _ckpt("a1v")
        for ot in range(4):
            cb = cbst[ot % 2]
            cbres = ("cbst", ot % 2)
            dma("sync", kstf[ot % 2][:].rearrange("p (c o) -> p c o", c=16),
                wkT_d[:, ot * 128:(ot + 1) * 128].rearrange("(c p) o -> p c o", p=128),
                writes=[("kstf", ot % 2)], stream=(f"kcb{ot % 2}", "slot"))
            vop("tensor_copy", cb[:], kstf[ot % 2][:],
                reads=[("kstf", ot % 2)], writes=[cbres])
            _ckpt("a1kd")
            for tq in range(4):
                dma("sync", cosc[:], cosT_d[:, tq * 512:(tq + 1) * 512], writes=[("cosc",)],
                    stream=("cosc", "slot"))
                _ckpt("a1cA")
                dma("sync", sinc[:], sinT_d[:, tq * 512:(tq + 1) * 512], writes=[("sinc",)],
                    stream=("sinc", "slot"))
                _ckpt("a1c")
                pp = PS[tq % 2]
                pres = PN[tq % 2]
                psw = PS[2 + tq % 2]
                psres = PN[2 + tq % 2]
                for dc in range(16):
                    mm(pp[:], cb[:, dc * 128:(dc + 1) * 128], lnv[:, dc, tq * 512:(tq + 1) * 512],
                       dc == 0, dc == 15, reads=[cbres, ("ln1T",)], writes=[pres])
                sl = tq % 2
                _ckpt("a1m")
                scopy(kpb[sl][:], pp[:], reads=[pres], writes=[("kpb", sl)])
                _ckpt("a1s")
                mm(psw[:], perm_b[:], kpb[sl][:], True, True,
                   reads=[("permb",), ("kpb", sl)], writes=[psres])
                _ckpt("a1p")
                vop("tensor_mul", rp1[sl][:], pp[:], cosc[:], reads=[pres, ("cosc",)],
                    writes=[("rp1", sl)])
                _ckpt("a1r1")
                vop("tensor_mul", rp2[sl][:], psw[:], sinc[:], reads=[psres, ("sinc",)],
                    writes=[("rp2", sl)])
                _ckpt("a1r2")
                vop("tensor_add", K_sb[:, ot * S + tq * 512: ot * S + (tq + 1) * 512],
                    rp1[sl][:], rp2[sl][:], reads=[("rp1", sl), ("rp2", sl)], writes=[("K",)])
                _ckpt("a1r")

            _ckpt("a1k1")
        _ckpt("a1k")
        for q4 in range(4):
            dma("sync", ln1T_dram[q4 * 4:(q4 + 1) * 4, :, :].transpose([1, 0, 2]),
                lnv[:, q4 * 4:(q4 + 1) * 4, :], reads=[("ln1T",)], writes=[("ln1Td",)],
                stream=("ln1Td", "group"))
        s.barrier()
        _close(a1)
        _close(span01)
        _ckpt("a1")

        print('SBUF free at A2(before allocs):', nc.sbuf_bytes_remaining)
        a2 = _mk()
        ln1Tc = [sb(a2, "ln1Tc0", [128, 16 * ACH], BF16)] * 2
        cosc2 = [sb(a2, f"cosc2_{i}", [128, ACH], F32) for i in range(2)]
        sinc2 = [sb(a2, f"sinc2_{i}", [128, ACH], F32) for i in range(2)]
        QTc = sb(a2, "QTc", [128, 16 * ACH], BF16)
        qpb = [sb(a2, f"qpb{i}", [128, ACH], BF16) for i in range(2)]
        qr1 = [sb(a2, "qr1_0", [128, ACH], F32)] * 2
        qr2 = [sb(a2, "qr2_0", [128, ACH], F32)] * 2
        E_sl = [sb(a2, f"E{i}", [128, 2 * ACH], BF16) for i in range(4)]
        OTc = sb(a2, "OTc", [128, 16 * ACH], BF16)
        tmpO = [sb(a2, f"tmpO{i}", [64, 512], F32) for i in range(2)]
        aoTc = sb(a2, "aoTc", [128, 16 * ACH], F32)
        sqb = sb(a2, "sqb", [128, 16 * ACH], BF16)
        ntmp = [sb(a2, f"ntmp{i}", [128, ACH], F32) for i in range(2)]
        xa_t = sb(a2, "xa_t", [128, D], F32)
        x1_t = sb(a2, "x1_t", [128, D], F32)
        ln3f = sb(a2, "ln3f", [128, D], F32)
        ln3h = sb(a2, "ln3h", [128, D], BF16)
        ln3lo = sb(a2, "ln3lo", [128, D], BF16)
        lst = [sb(a2, f"lst{i}", [128, 8 * 128], BF16) for i in range(2)]
        lstl = [sb(a2, f"lstl{i}", [128, 8 * 128], BF16) for i in range(2)]
        colsB = sb(a2, "colsB", [128, 8], F32)
        rr = sb(a2, "rr", [1, 512], F32)
        rr2 = sb(a2, "rr2", [1, 512], F32)
        logit = sb(a2, "logit", [128, 8], F32)
        prob = sb(a2, "prob", [128, 8], F32)
        gsc = sb(a2, "gsc", [128, 24], F32)
        rows = sb(a2, "rows", [1, 2304], F32)
        rows_b = sb(a2, "rows_b", [1, 3072], BF16)

        def rows_hilo(src, n, hi_b, lo_b, ftmp, base, ftag):
            scopy(hi_b, src, reads=[base], writes=[(base, "hi")])
            scopy(ftmp, hi_b, reads=[(base, "hi")], writes=[(base, "hf"), ftag])
            vop("tensor_sub", lo_b, src, ftmp,
                reads=[base, (base, "hf"), ftag], writes=[(base, "lo")])

        def chunk_epilogue(c):
            t0 = c * ACH
            for tb in range(ACH // 128):
                tba = c * (ACH // 128) + tb
                dma("sync", xa_t[:], x_d[tba * 128:(tba + 1) * 128, :], writes=[("xa",)],
                    stream=("xa", "slot"))
                for half in range(2):
                    pw = PW0 if half == 0 else PW1
                    pwn = "PW0" if half == 0 else "PW1"
                    for j in range(8):
                        dt = half * 8 + j
                        tr(pw[:, j * 128:(j + 1) * 128],
                           aoTc[:, dt * ACH + tb * 128: dt * ACH + tb * 128 + 128], ident_f[:],
                           reads=[("aoTc",), ("identf",)], writes=[(pwn, j // 4)])
                    vop("tensor_add", x1_t[:, half * 1024:(half + 1) * 1024], pw[:],
                        xa_t[:, half * 1024:(half + 1) * 1024],
                        reads=[(pwn, 0), (pwn, 1), ("xa",)], writes=[("x1",)])
                dma("sync", x1_dram[tba * 128:(tba + 1) * 128, :], x1_t[:],
                    reads=[("x1",)], writes=[("x1d",)], stream=("x1d", "slot"))
                mcol, rcol, m2col = colsB[:, 0:1], colsB[:, 1:2], colsB[:, 2:3]
                ln_token(x1_t[:], xa_t[:], mcol, rcol, m2col, ("lnB",), ("x1",), ("xa",))
                vop("tensor_scalar", ln3f[:], x1_t[:], mcol, rcol, ALU.subtract, ALU.mult,
                    reads=[("x1",), ("lnB",)], writes=[("ln3f",)])
                scopy(ln3h[:], ln3f[:], reads=[("ln3f",)], writes=[("ln3h",)])
                scopy(xa_t[:], ln3h[:], reads=[("ln3h",)], writes=[("xa",)])
                vop("tensor_sub", ln3lo[:], ln3f[:], xa_t[:],
                    reads=[("ln3f",), ("xa",)], writes=[("ln3lo",)])
                for half in range(2):
                    pp = PS[2 + half]
                    ppb = pp[:].bitcast(BF16)
                    pres = PN[2 + half]
                    for j in range(8):
                        dc = half * 8 + j
                        tr(ppb[:, j * 128:(j + 1) * 128], ln3h[:, dc * 128:(dc + 1) * 128],
                           ident_b[:], reads=[("ln3h",), ("identb",)], writes=[pres])
                    scopy(lst[half][:], ppb[:, 0:1024], reads=[pres], writes=[("lst", half)])
                    dma("sync",
                        ln3T_dram[half * 8:(half + 1) * 8, :, tba * 128:(tba + 1) * 128]
                        .transpose([1, 0, 2]),
                        lst[half][:].rearrange("p (c t) -> p c t", c=8),
                        reads=[("lst", half)], writes=[("ln3Td", half)],
                        stream=(f"l3d{half}", "slot"))
                    for j in range(8):
                        dc = half * 8 + j
                        tr(ppb[:, j * 128:(j + 1) * 128], ln3lo[:, dc * 128:(dc + 1) * 128],
                           ident_b[:], reads=[("ln3lo",), ("identb",)], writes=[pres])
                    scopy(lstl[half][:], ppb[:, 0:1024], reads=[pres], writes=[("lstl", half)])
                rwv_hi = rw_hi[:].rearrange("p (c e) -> p c e", c=16)
                rwv_lo = rw_lo[:].rearrange("p (c e) -> p c e", c=16)
                for half in range(2):
                    for j in range(8):
                        dc = half * 8 + j
                        first = (half == 0 and j == 0)
                        mm(P0[:, 0:8], lst[half][:, j * 128:(j + 1) * 128], rwv_hi[:, dc, :],
                           first, False, reads=[("lst", half), ("rwhi",)], writes=[("P0",)])
                        mm(P0[:, 0:8], lst[half][:, j * 128:(j + 1) * 128], rwv_lo[:, dc, :],
                           False, False, reads=[("lst", half), ("rwlo",)], writes=[("P0",)])
                        mm(P0[:, 0:8], lstl[half][:, j * 128:(j + 1) * 128], rwv_hi[:, dc, :],
                           False, False, reads=[("lstl", half), ("rwhi",)], writes=[("P0",)])
                mm(P0[:, 0:8], ones1[:], rbl_hi[:], False, False,
                   reads=[("ones1",), ("rblhi",)], writes=[("P0",)])
                mm(P0[:, 0:8], ones1[:], rbl_lo[:], False, True,
                   reads=[("ones1",), ("rbllo",)], writes=[("P0",)])
                scopy(logit[:], P0[:, 0:8], reads=[("P0",)], writes=[("logit",)])
                g0, g1 = gsc[:, 0:1], gsc[:, 1:2]
                m1, m2 = gsc[:, 8:16], gsc[:, 16:24]
                vop("reduce_max", g0, logit[:], AX.X, reads=[("logit",)], writes=[("g0",)])
                vop("tensor_scalar", prob[:], logit[:], g0, None, ALU.subtract,
                    reads=[("logit",), ("g0",)], writes=[("prob",)])
                act(prob[:], prob[:], AF.Exp, reads=[("prob",)], writes=[("prob",)])
                vop("reduce_sum", g1, prob[:], AX.X, reads=[("prob",)], writes=[("g1",)])
                vop("reciprocal", g1, g1, reads=[("g1",)], writes=[("g1",)])
                vop("tensor_scalar", prob[:], prob[:], g1, None, ALU.mult,
                    reads=[("prob",), ("g1",)], writes=[("prob",)])
                vop("reduce_max", g0, prob[:], AX.X, reads=[("prob",)], writes=[("g0",)])
                vop("tensor_scalar", m1, prob[:], g0, None, ALU.is_ge,
                    reads=[("prob",), ("g0",)], writes=[("m1",)])
                vop("scalar_tensor_tensor", m2, m1, -1000.0, prob[:], ALU.mult, ALU.add,
                    reads=[("m1",), ("prob",)], writes=[("m2",)])
                vop("reduce_max", g0, m2, AX.X, reads=[("m2",)], writes=[("g0",)])
                vop("tensor_scalar", m2, m2, g0, None, ALU.is_ge,
                    reads=[("m2",), ("g0",)], writes=[("m2",)])
                vop("tensor_add", m1, m1, m2, reads=[("m1",), ("m2",)], writes=[("m1",)])
                act(prob[:], prob[:], AF.Exp, reads=[("prob",)], writes=[("prob",)])
                vop("tensor_mul", prob[:], prob[:], m1, reads=[("prob",), ("m1",)],
                    writes=[("prob",)])
                vop("reduce_sum", g1, prob[:], AX.X, reads=[("prob",)], writes=[("g1",)])
                vop("reciprocal", g1, g1, reads=[("g1",)], writes=[("g1",)])
                vop("tensor_scalar", w_tm[:, tba * 9: tba * 9 + 8], prob[:], g1, None, ALU.mult,
                    reads=[("prob",), ("g1",)], writes=[("w_tm",)])
                tr(P1[0:8, 0:128], w_tm[:, tba * 9: tba * 9 + 8], ident_f[:],
                   reads=[("w_tm",), ("identf",)], writes=[("P1",)])
                scopy(wT_sb[0:8, tba * 128:(tba + 1) * 128], P1[0:8, 0:128],
                      reads=[("P1",)], writes=[("wT",)])


        for c in range(NACH):
            t0 = c * ACH
            cpar = 0
            l1vc = ln1Tc[cpar][:].rearrange("p (c t) -> p c t", c=16)
            cosv, sinv = cosc2[cpar], sinc2[cpar]
            dma("sync", l1vc, ln1T_dram[:, :, t0:t0 + ACH].transpose([1, 0, 2]),
                reads=[("ln1Td",)], writes=[("ln1Tc", cpar)], stream=(f"l1c{cpar}", "slot"))
            dma("sync", cosv[:], cosT_d[:, t0:t0 + ACH], writes=[("cosc", cpar)],
                stream=(f"cosc{cpar}", "slot"))
            dma("sync", sinv[:], sinT_d[:, t0:t0 + ACH], writes=[("sinc", cpar)],
                stream=(f"sinc{cpar}", "slot"))

            def jit_mms(ot):
                cb = cbst[ot % 3]
                cbres = ("cbst", ot % 3)
                dma("sync", cb[:].rearrange("p (c o) -> p c o", c=16),
                    wq_c[ot, :, :, :],
                    reads=[("c_wq_c",)], writes=[cbres], stream=(f"cb{ot % 3}", "slot"))
                pp = PS[ot % 2]
                pres = PN[ot % 2]
                for dc in range(16):
                    mm(pp[:, 0:ACH], cb[:, dc * 128:(dc + 1) * 128], l1vc[:, dc, :],
                       dc == 0, dc == 15, reads=[cbres, ("ln1Tc", cpar)], writes=[pres])

            def jit_rope(ot):
                sl = ot % 2
                pp = PS[sl]
                pres = PN[sl]
                scopy(qpb[sl][:], pp[:, 0:ACH], reads=[pres], writes=[("qpb", sl)])
                psw = PS[2 + sl]
                psres = PN[2 + sl]
                mm(psw[:, 0:ACH], perm_b[:], qpb[sl][:], True, True,
                   reads=[("permb",), ("qpb", sl)], writes=[psres])
                vop("tensor_mul", qr1[sl][:], pp[:, 0:ACH], cosv[:],
                    reads=[pres, ("cosc", cpar)], writes=[("qr1",)])
                vop("tensor_mul", qr2[sl][:], psw[:, 0:ACH], sinv[:],
                    reads=[psres, ("sinc", cpar)], writes=[("qr2",)])
                vop("tensor_add", QTc[:, ot * ACH:(ot + 1) * ACH], qr1[sl][:], qr2[sl][:],
                    reads=[("qr1",), ("qr2",)], writes=[("QTc",)])

            for ot in range(16):
                jit_mms(ot)
                if ot > 0:
                    jit_rope(ot - 1)
            jit_rope(15)
            if c > 0:
                chunk_epilogue(c - 1)

            nkt = 2 * (c + 1)
            npair = nkt // 2

            def pair_scores_pv(p2i):
                pp2 = p2i % 2
                h0 = 2 * p2i
                kvh = h0 // 4
                ktile, koff = kvh // 2, 64 * (kvh % 2)
                pw = PW0 if pp2 == 0 else PW1
                pwn = "PW0" if pp2 == 0 else "PW1"
                qtile, qoff = qplace(h0)
                qrhs = QTc[qoff:qoff + 64, qtile * ACH:(qtile + 2) * ACH]

                def emit_s(kt):
                    pp = PS[kt % 4]
                    pres = PN[kt % 4]
                    mm(pp[:],
                       K_sb[koff:koff + 64,
                            ktile * S + kt * 128: ktile * S + (kt + 1) * 128],
                       qrhs, True, True, reads=[("K",), ("QTc",)], writes=[pres])
                    esl = kt % 4
                    act(E_sl[esl][:], pp[:], AF.Exp, bias=-ESHIFT, scale=0.125,
                        reads=[pres], writes=[("E", esl)])
                    if kt >= nkt - 2:
                        di = 0 if kt == nkt - 2 else 1
                        vop("tensor_mul", E_sl[esl][:], E_sl[esl][:], m01d_b[di][:],
                            reads=[("E", esl), ("m01d", di)], writes=[("E", esl)])

                def emit_pv(kt):
                    esl = kt % 4
                    mm(pw[0:65, 0:512],
                       V_sb[:, kt * 520 + kvh * 65: kt * 520 + (kvh + 1) * 65],
                       E_sl[esl][:], kt == 0, kt == nkt - 1,
                       reads=[("V", kt), ("Vones",), ("E", esl)], writes=[(pwn, 0)])

                return emit_s, emit_pv

            def pair_norm_calc(p2i):
                hp = p2i % 2
                pw = PW0 if hp == 0 else PW1
                pwn = "PW0" if hp == 0 else "PW1"
                rsrc = rr if hp == 0 else rr2
                rscr = rows[0:1, 1792:2304]
                act(rscr, pw[64:65, 0:512], AF.Ln, reads=[(pwn, 0)], writes=[("rscr",)])
                act(rsrc[0:1, 0:512], rscr, AF.Exp, scale=-1.0,
                    reads=[("rscr",)], writes=[("rr", hp)])
                vop("tensor_copy", rows_b[0:1, hp * 512:hp * 512 + 512], rsrc[0:1, 0:512],
                    reads=[("rr", hp)], writes=[(("rr", hp), "hi")])

            def pair_norm_apply(p2i):
                hp = p2i % 2
                h0 = 2 * p2i
                qtile, qoff = qplace(h0)
                pw = PW0 if hp == 0 else PW1
                pwn = "PW0" if hp == 0 else "PW1"
                mm(pw[0:64, 512:1024], ones1[0:1, 0:64],
                   rows_b[0:1, hp * 512:hp * 512 + 512], True, True,
                   reads=[("ones1",), (("rr", hp), "hi")], writes=[(pwn, 1)])
                scopy(tmpO[hp][:], pw[0:64, 0:512], reads=[(pwn, 0)], writes=[("tmpO", hp)])
                vop("tensor_mul", OTc[qoff:qoff + 64, qtile * ACH:(qtile + 2) * ACH],
                    tmpO[hp][:], pw[0:64, 512:1024],
                    reads=[("tmpO", hp), (pwn, 1)], writes=[("OTc",)])

            LA = 2
            for p2i in range(16):
                emit_s, emit_pv = pair_scores_pv(p2i)
                for kt in range(min(LA + 1, nkt)):
                    emit_s(kt)
                if p2i > 0:
                    pair_norm_calc(p2i - 1)
                late = max(0, nkt - 2)
                for kt in range(nkt):
                    if kt + LA + 1 < nkt:
                        emit_s(kt + LA + 1)
                    if kt == late and p2i > 0:
                        pair_norm_apply(p2i - 1)
                    emit_pv(kt)
            pair_norm_calc(15)
            pair_norm_apply(15)

            for dt in range(16):
                cb = cbst[dt % 3]
                cbres = ("cbst", dt % 3)
                dma("sync", cb[:].rearrange("p (c o) -> p c o", c=16), wo_c[dt, :, :, :],
                    reads=[("c_wo_c",)], writes=[cbres], stream=(f"cb{dt % 3}", "slot"))
                pp = PS[dt % 4]
                pres = PN[dt % 4]
                for oc in range(16):
                    mm(pp[:, 0:ACH], cb[:, oc * 128:(oc + 1) * 128], OTc[:, oc * ACH:(oc + 1) * ACH],
                       oc == 0, oc == 15, reads=[cbres, ("OTc",)], writes=[pres])
                scopy(aoTc[:, dt * ACH:(dt + 1) * ACH], pp[:, 0:ACH], reads=[pres],
                      writes=[("aoTc",)])
                vop("tensor_copy", QTc[:, dt * ACH:(dt + 1) * ACH],
                    aoTc[:, dt * ACH:(dt + 1) * ACH], reads=[("aoTc",)], writes=[("QTc",)])
                act(sqb[:, dt * ACH:(dt + 1) * ACH], aoTc[:, dt * ACH:(dt + 1) * ACH],
                    AF.Square, reads=[("aoTc",)], writes=[("sqb",)])

            for dc in range(16):
                mm(P2[0:1, 0:ACH], onesc[:], QTc[:, dc * ACH:(dc + 1) * ACH], dc == 0, dc == 15,
                   reads=[("onesc",), ("QTc",)], writes=[("P2",)])
                mm(P3[0:1, 0:ACH], onesc[:], sqb[:, dc * ACH:(dc + 1) * ACH], dc == 0, dc == 15,
                   reads=[("onesc",), ("sqb",)], writes=[("P3",)])
            mrow = rows[0:1, 1024:1024 + ACH]
            vrow = rows[0:1, 1024 + ACH:1024 + 2 * ACH]
            trow = rows[0:1, 1024 + 2 * ACH:1024 + 3 * ACH]
            vop("tensor_scalar", mrow, P2[0:1, 0:ACH], 1.0 / D, None, ALU.mult,
                reads=[("P2",)], writes=[("mrow",)])
            vop("tensor_scalar", vrow, P3[0:1, 0:ACH], 1.0 / D, None, ALU.mult,
                reads=[("P3",)], writes=[("vrow",)])
            vop("tensor_tensor", trow, mrow, mrow, ALU.mult, reads=[("mrow",)], writes=[("trow",)])
            vop("tensor_sub", vrow, vrow, trow, reads=[("vrow",), ("trow",)], writes=[("vrow",)])
            act(vrow, vrow, AF.Ln, bias=EPS, reads=[("vrow",)], writes=[("vrow",)])
            act(vrow, vrow, AF.Exp, scale=-0.5, reads=[("vrow",)], writes=[("vrow",)])
            rows_hilo(mrow, ACH, rows_b[0:1, 2048:2048 + ACH],
                      rows_b[0:1, 2048 + ACH:2048 + 2 * ACH], trow, ("mrow",), ("ftB",))
            mm(P2[:, 0:ACH], ones1[:], rows_b[0:1, 2048:2048 + ACH], True, False,
               reads=[("ones1",), (("mrow",), "hi")], writes=[("P2",)])
            mm(P2[:, 0:ACH], ones1[:], rows_b[0:1, 2048 + ACH:2048 + 2 * ACH], False, True,
               reads=[("ones1",), (("mrow",), "lo")], writes=[("P2",)])
            rows_hilo(vrow, ACH, rows_b[0:1, 2560:2560 + ACH],
                      rows_b[0:1, 2560 + ACH:2560 + 2 * ACH], trow, ("vrow",), ("ftB",))
            mm(P3[:, 0:ACH], ones1[:], rows_b[0:1, 2560:2560 + ACH], True, False,
               reads=[("ones1",), (("vrow",), "hi")], writes=[("P3",)])
            mm(P3[:, 0:ACH], ones1[:], rows_b[0:1, 2560 + ACH:2560 + 2 * ACH], False, True,
               reads=[("ones1",), (("vrow",), "lo")], writes=[("P3",)])
            for dt in range(16):
                sl = dt % 2
                vop("tensor_sub", ntmp[sl][:], aoTc[:, dt * ACH:(dt + 1) * ACH], P2[:, 0:ACH],
                    reads=[("aoTc",), ("P2",)], writes=[("ntmp", sl)])
                vop("tensor_mul", aoTc[:, dt * ACH:(dt + 1) * ACH], ntmp[sl][:], P3[:, 0:ACH],
                    reads=[("ntmp", sl), ("P3",)], writes=[("aoTc",)])

        chunk_epilogue(NACH - 1)
        s.barrier()
        _close(a2)
        _close(spanKV)
        _ckpt("a2")

        ups = _mk()
        ustg = sb(ups, "ustg", [1, LD], BF16)
        b1ofs = np.cumsum([0] + NH).tolist()
        for e in range(NEXP):
            for hcx in range(NH[e]):
                slw = hcx % 4
                uq = "sync" if slw < 2 else "gpsimd"
                usn = f"w2{slw}" if slw < 2 else f"gw2{slw}"
                dma(uq, w2st[slw][:], w2_c[e][hcx * 128:(hcx + 1) * 128, :],
                    reads=[("c_" + w2_c[e].name,)], writes=[("w2st", slw)],
                    stream=(usn, "slot"))
                for half in range(2):
                    pp = PS[half]
                    mm(pp[0:1, 0:512], gb1[:, b1ofs[e] + hcx: b1ofs[e] + hcx + 1],
                       w2st[slw][:, half * 512:(half + 1) * 512], hcx == 0, hcx == NH[e] - 1,
                       reads=[("gb1",), ("w2st", slw)], writes=[PN[half]])
            scopy(urow[0:1, 0:512], P0[0:1, 0:512], reads=[("P0",)], writes=[("urow",)])
            scopy(urow[0:1, 512:1024], P1[0:1, 0:512], reads=[("P1",)], writes=[("urow",)])
            vop("tensor_add", crow[0:1, :], crow[0:1, :], urow[0:1, :],
                reads=[("crow",), ("urow",)], writes=[("crow",)])
            s.op("scalar",
                 (lambda o, i: lambda en: en.activation(o, i, AF.Identity, bias=0.0, scale=-1.0))(
                     ustg[0:1, :], urow[0:1, :]),
                 reads=[("urow",)], writes=[("ustg",)])
            dma("gpsimd", ustack_dram[e:e + 1, :], ustg[0:1, :], reads=[("ustg",)],
                writes=[("ustackd",)], stream=("usd", "slot"))
        dma("sync", urow[0:1, :], sb2row_d, writes=[("sb2r",), ("urow",)],
            stream=("sb2", "group"))
        vop("scalar_tensor_tensor", crow[0:1, :], urow[0:1, :], 0.1, crow[0:1, :],
            ALU.mult, ALU.add, reads=[("sb2r",), ("crow",)], writes=[("crow",)])
        scopy(ustg[0:1, :], crow[0:1, :], reads=[("crow",)], writes=[("ustg",)])
        dma("gpsimd", ustack_dram[8:9, :], ustg[0:1, :], reads=[("ustg",)], writes=[("ustackd",)],
            stream=("usd", "slot"))
        scopy(urow[0:1, :], ustg[0:1, :], reads=[("ustg",)], writes=[("uhf",), ("sb2r",)])
        vop("tensor_sub", urow[0:1, :], crow[0:1, :], urow[0:1, :],
            reads=[("crow",), ("uhf",)], writes=[("ulo",)])
        scopy(ustg[0:1, :], urow[0:1, :], reads=[("ulo",)], writes=[("ustg",)])
        dma("gpsimd", ustack_dram[9:10, :], ustg[0:1, :], reads=[("ustg",)], writes=[("ustackd",)],
            stream=("usd", "slot"))
        dma("sync", ustack[:], ustack_dram, reads=[("ustackd",)], writes=[("ustack",)],
            stream=("usl", "group"))
        s.barrier()
        _close(ups)
        _ckpt("ups")

        print('SBUF free at FFN(before allocs):', nc.sbuf_bytes_remaining)
        ffn = _mk()
        ln3Tc = sb(ffn, "ln3Tc", [128, 16 * FCH], BF16)
        g_sb = sb(ffn, "g_sb", [128, 8 * FCH], BF16)
        gxT = sb(ffn, "gxT", [128, 16 * FCH], BF16)
        sc32 = sb(ffn, "sc32", [128, 32 * FCH], BF16)
        acc_sb = sb(ffn, "acc_sb", [128, 4 * LD], F32)
        accT = sb(ffn, "accT", [128, 8 * FCH], BF16)
        faoT = sb(ffn, "faoT", [128, 16 * FCH], F32)
        fntmp = [sb(ffn, f"fntmp{i}", [128, FCH], F32) for i in range(2)]
        fx1 = sb(ffn, "fx1", [128, D], F32)
        frows = sb(ffn, "frows", [1, 3 * FCH], F32)
        frows_b = sb(ffn, "frows_b", [1, 4 * FCH], BF16)

        l3v = ln3Tc[:].rearrange("p (c t) -> p c t", c=16)
        p2map = [(PW0[:, 0:512], PW0[:, 512:1024]), (PW1[:, 0:512], PW1[:, 512:1024]),
                 (P0[:], P1[:]), (P2[:], P3[:])]
        p2res = [[("PW0", 0), ("PW0", 1)], [("PW1", 0), ("PW1", 1)],
                 [("P0",), ("P1",)], [("P2",), ("P3",)]]
        p2full = [PW0[:], PW1[:], None, None]

        def ffn_final(c):
            for tb in range(4):
                tbg = c * 4 + tb
                dma("sync", fx1[:], x1_dram[tbg * 128:(tbg + 1) * 128, :],
                    reads=[("x1d",)], writes=[("fx1",)], stream=("fx1", "slot"))
                for half in range(2):
                    pw = PW0 if half == 0 else PW1
                    pwn = "PW0" if half == 0 else "PW1"
                    for j in range(8):
                        dt = half * 8 + j
                        tr(pw[:, j * 128:(j + 1) * 128],
                           faoT[:, dt * FCH + tb * 128: dt * FCH + tb * 128 + 128], ident_f[:],
                           reads=[("faoT",), ("identf",)], writes=[(pwn, j // 4)])
                    vop("scalar_tensor_tensor", fx1[:, half * 1024:(half + 1) * 1024],
                        pw[:], 1.0, fx1[:, half * 1024:(half + 1) * 1024], ALU.mult, ALU.add,
                        reads=[(pwn, 0), (pwn, 1), ("fx1",)], writes=[("fx1",)])
                dma("sync", y_d[tbg * 128:(tbg + 1) * 128, :], fx1[:],
                    reads=[("fx1",)], writes=[("y",)], stream=("y", "group"))


        for c in range(NFCH):
            t0 = c * FCH
            dma("gpsimd", l3v, ln3T_dram[:, :, t0:t0 + FCH].transpose([1, 0, 2]),
                reads=[("ln3Td", 0), ("ln3Td", 1)], writes=[("ln3Tc",)], stream=("gl3c", "slot"))
            for lt in range(8):
                cb = cbst[lt % 2]
                cbres = ("cbst", lt % 2)
                dma("sync", cb[:].rearrange("p (c o) -> p c o", c=16), down_c[lt, :, :, :],
                    reads=[("c_down_c",)], writes=[cbres], stream=(f"cb{lt % 2}", "slot"))
                pp = PS[lt % 4]
                pres = PN[lt % 4]
                for dc in range(16):
                    mm(pp[:], cb[:, dc * 128:(dc + 1) * 128], l3v[:, dc, :], dc == 0, dc == 15,
                       reads=[cbres, ("ln3Tc",)], writes=[pres])
                act(g_sb[:, lt * FCH:(lt + 1) * FCH], pp[:], AF.Gelu, bias=downb[:, lt:lt + 1],
                    reads=[pres, ("downb",)], writes=[("g", lt)])
            for dc in range(16):
                act(gxT[:, dc * FCH:(dc + 1) * FCH], l3v[:, dc, :], AF.Gelu,
                    reads=[("ln3Tc",)], writes=[("gxT",)])
            if c > 0:
                ffn_final(c - 1)
            for tsub in range(4):
                tbg = c * 4 + tsub
                for half in range(2):
                    mm(p2map[tsub][half], wT_sb[:, tbg * 128:(tbg + 1) * 128],
                       ustack[:, half * 512:(half + 1) * 512], True, True,
                       reads=[("wT",), ("wTones",), ("ustack",)],
                       writes=[p2res[tsub][half]])
                if p2full[tsub] is not None:
                    vop("tensor_copy", acc_sb[:, tsub * LD:(tsub + 1) * LD], p2full[tsub],
                        reads=p2res[tsub], writes=[("acc", tsub)])
                else:
                    vop("tensor_copy", acc_sb[:, tsub * LD:tsub * LD + 512], p2map[tsub][0],
                        reads=[p2res[tsub][0]], writes=[("acc", tsub)])
                    vop("tensor_copy", acc_sb[:, tsub * LD + 512:(tsub + 1) * LD], p2map[tsub][1],
                        reads=[p2res[tsub][1]], writes=[("acc", tsub)])

            def expert(nh, w1c4, w2cp, b1c0, gate_col, gate_const):
                for hcx in range(nh):
                    cb = cbst[hcx % 2]
                    cbres = ("cbst", hcx % 2)
                    w1q = "sync" if hcx % 2 == 0 else "gpsimd"
                    w1sn = "cb0" if hcx % 2 == 0 else "gcb1"
                    dma(w1q, cb[:, 0:8 * 128].rearrange("p (c o) -> p c o", c=8),
                        w1c4[hcx, :, :, :], reads=[("c_" + w1c4.name,)], writes=[cbres],
                        stream=(w1sn, "slot"))
                    pp = PS[hcx % 2]
                    pres = PN[hcx % 2]
                    for lt in range(8):
                        mm(pp[:], cb[:, lt * 128:(lt + 1) * 128], g_sb[:, lt * FCH:(lt + 1) * FCH],
                           lt == 0, lt == 7, reads=[cbres, ("g", lt)], writes=[pres])
                    act(sc32[:, hcx * FCH:(hcx + 1) * FCH], pp[:], AF.Gelu,
                        bias=b1all[:, b1c0 + hcx:b1c0 + hcx + 1],
                        reads=[pres, ("b1all",)], writes=[("sc32", hcx)])
                for hcx in range(nh):
                    slw = hcx % 4
                    wq2 = "sync" if slw < 2 else "gpsimd"
                    wsn = f"w2{slw}" if slw < 2 else f"gw2{slw}"
                    dma(wq2, w2st[slw][:], w2cp[hcx * 128:(hcx + 1) * 128, :],
                        reads=[("c_" + w2cp.name,)], writes=[("w2st", slw)],
                        stream=(wsn, "slot"))
                    for tsub in range(4):
                        for half in range(2):
                            mm(p2map[tsub][half],
                               sc32[:, hcx * FCH + tsub * 128: hcx * FCH + tsub * 128 + 128],
                               w2st[slw][:, half * 512:(half + 1) * 512],
                               hcx == 0, hcx == nh - 1,
                               reads=[("sc32", hcx), ("w2st", slw)],
                               writes=[p2res[tsub][half]])
                for tsub in (2, 3, 0, 1):
                    tbg = c * 4 + tsub
                    gc = (w_tm[:, tbg * 9 + gate_col: tbg * 9 + gate_col + 1]
                          if gate_col is not None else gate_const)
                    grd = [("w_tm",)] if gate_col is not None else []
                    if p2full[tsub] is not None:
                        vop("scalar_tensor_tensor", acc_sb[:, tsub * LD:(tsub + 1) * LD],
                            p2full[tsub], gc, acc_sb[:, tsub * LD:(tsub + 1) * LD],
                            ALU.mult, ALU.add,
                            reads=p2res[tsub] + grd + [("acc", tsub)], writes=[("acc", tsub)])
                    else:
                        for half in range(2):
                            vop("scalar_tensor_tensor",
                                acc_sb[:, tsub * LD + half * 512: tsub * LD + (half + 1) * 512],
                                p2map[tsub][half], gc,
                                acc_sb[:, tsub * LD + half * 512: tsub * LD + (half + 1) * 512],
                                ALU.mult, ALU.add,
                                reads=[p2res[tsub][half]] + grd + [("acc", tsub)],
                                writes=[("acc", tsub)])

            for e in range(NEXP):
                expert(NH[e], w1_c[e], w2_c[e], b1ofs[e], e, None)
            expert(8, sw1_c, sw2_c, b1ofs[NEXP], None, 0.1)

            for lt in range(8):
                pp = PS[2 + lt % 2]
                pres = PN[2 + lt % 2]
                for tsub in range(4):
                    tr(pp[:, tsub * 128:(tsub + 1) * 128],
                       acc_sb[:, tsub * LD + lt * 128: tsub * LD + lt * 128 + 128], ident_f[:],
                       reads=[("acc", tsub), ("identf",)], writes=[pres])
                scopy(accT[:, lt * FCH:(lt + 1) * FCH], pp[:], reads=[pres], writes=[("accT",)])

            for dt in range(16):
                cbu = cbst[dt % 2]
                cbures = ("cbst", dt % 2)
                dma("sync", cbu[:, 0:8 * 128].rearrange("p (c o) -> p c o", c=8),
                    up_c[dt, :, :, :], reads=[("c_up_c",)], writes=[cbures],
                    stream=(f"cb{dt % 2}", "slot"))
                cbc = cbst[2]
                cbcres = ("cbst", 2)
                dma("gpsimd", cbc[:].rearrange("p (c o) -> p c o", c=16), core_c[dt, :, :, :],
                    reads=[("c_core_c",)], writes=[cbcres], stream=("gcb2", "slot"))
                pp = PS[dt % 4]
                pres = PN[dt % 4]
                for lt in range(8):
                    mm(pp[:], cbu[:, lt * 128:(lt + 1) * 128], accT[:, lt * FCH:(lt + 1) * FCH],
                       lt == 0, False, reads=[cbures, ("accT",)], writes=[pres])
                for dc in range(16):
                    mm(pp[:], cbc[:, dc * 128:(dc + 1) * 128], gxT[:, dc * FCH:(dc + 1) * FCH],
                       False, dc == 15, reads=[cbcres, ("gxT",)], writes=[pres])
                s.op("scalar",
                     (lambda o, i, b: lambda en: en.activation(o, i, AF.Identity, bias=b, scale=1.0))(
                         faoT[:, dt * FCH:(dt + 1) * FCH], pp[:], bcol[:, dt:dt + 1]),
                     reads=[pres, ("bcol",)], writes=[("faoT",)])
                act(sc32[:, (16 + dt) * FCH:(17 + dt) * FCH],
                    faoT[:, dt * FCH:(dt + 1) * FCH], AF.Square,
                    reads=[("faoT",)], writes=[("sc32", 16 + dt)])
                vop("tensor_copy", sc32[:, dt * FCH:(dt + 1) * FCH],
                    faoT[:, dt * FCH:(dt + 1) * FCH], reads=[("faoT",)], writes=[("sc32", dt)])

            for dc in range(16):
                mm(P2[0:1, 0:FCH], onesc[:], sc32[:, dc * FCH:(dc + 1) * FCH], dc == 0, dc == 15,
                   reads=[("onesc",), ("sc32", dc)], writes=[("P2",)])
                mm(P3[0:1, 0:FCH], onesc[:], sc32[:, (16 + dc) * FCH:(17 + dc) * FCH],
                   dc == 0, dc == 15, reads=[("onesc",), ("sc32", 16 + dc)], writes=[("P3",)])
            fm = frows[0:1, 0:FCH]
            fv = frows[0:1, FCH:2 * FCH]
            ft = frows[0:1, 2 * FCH:3 * FCH]
            vop("tensor_scalar", fm, P2[0:1, 0:FCH], 1.0 / D, None, ALU.mult,
                reads=[("P2",)], writes=[("fm",)])
            vop("tensor_scalar", fv, P3[0:1, 0:FCH], 1.0 / D, None, ALU.mult,
                reads=[("P3",)], writes=[("fv",)])
            vop("tensor_tensor", ft, fm, fm, ALU.mult, reads=[("fm",)], writes=[("ft",)])
            vop("tensor_sub", fv, fv, ft, reads=[("fv",), ("ft",)], writes=[("fv",)])
            act(fv, fv, AF.Ln, bias=EPS, reads=[("fv",)], writes=[("fv",)])
            act(fv, fv, AF.Exp, scale=-0.5, reads=[("fv",)], writes=[("fv",)])
            rows_hilo(fm, FCH, frows_b[0:1, 0:FCH], frows_b[0:1, FCH:2 * FCH], ft, ("fm",), ("ftF",))
            mm(P2[:, 0:FCH], ones1[:], frows_b[0:1, 0:FCH], True, False,
               reads=[("ones1",), (("fm",), "hi")], writes=[("P2",)])
            mm(P2[:, 0:FCH], ones1[:], frows_b[0:1, FCH:2 * FCH], False, True,
               reads=[("ones1",), (("fm",), "lo")], writes=[("P2",)])
            rows_hilo(fv, FCH, frows_b[0:1, 2 * FCH:3 * FCH], frows_b[0:1, 3 * FCH:4 * FCH],
                      ft, ("fv",), ("ftF",))
            mm(P3[:, 0:FCH], ones1[:], frows_b[0:1, 2 * FCH:3 * FCH], True, False,
               reads=[("ones1",), (("fv",), "hi")], writes=[("P3",)])
            mm(P3[:, 0:FCH], ones1[:], frows_b[0:1, 3 * FCH:4 * FCH], False, True,
               reads=[("ones1",), (("fv",), "lo")], writes=[("P3",)])
            for dt in range(16):
                sl = dt % 2
                vop("tensor_sub", fntmp[sl][:], faoT[:, dt * FCH:(dt + 1) * FCH], P2[:, 0:FCH],
                    reads=[("faoT",), ("P2",)], writes=[("fntmp", sl)])
                vop("tensor_mul", faoT[:, dt * FCH:(dt + 1) * FCH], fntmp[sl][:], P3[:, 0:FCH],
                    reads=[("fntmp", sl), ("P3",)], writes=[("faoT",)])

        ffn_final(NFCH - 1)
        _close(ffn)
    except _StopBuild:
        while _open:
            _open.pop().close()

    with nc.Block() as block:
        s.emit(block, nc, top)
    top.close()
    return nc


_NC_CACHE = []


def kernel(**inputs):
    ii = {k: np.asarray(v) for k, v in inputs.items()}
    hcs = host_constants()
    x = np.ascontiguousarray(ii["L0_attn_x"], dtype=np.float32)

    inm = {}
    qp = q_perm()
    inm["wqT"] = np.ascontiguousarray(np.asarray(ii["L0_attn_Wq"], np.float32).T[:, qp])
    inm["wkT"] = np.ascontiguousarray(ii["L0_attn_Wk"].T)
    inm["wvT"] = np.ascontiguousarray(ii["L0_attn_Wv"].T)
    inm["woT"] = np.ascontiguousarray(np.asarray(ii["L0_attn_Wo"], np.float32).T[qp, :])
    inm["down_w"] = np.ascontiguousarray(ii["L0_ffn_down_w"])
    inm["up_w"] = np.ascontiguousarray(ii["L0_ffn_up_w"])
    inm["core_w"] = np.ascontiguousarray(ii["L0_ffn_core_w"])
    inm["sw1"] = np.ascontiguousarray(ii["L0_ffn_shared_w1"])
    inm["sw2"] = np.ascontiguousarray(ii["L0_ffn_shared_w2"])
    for e in range(NEXP):
        inm[f"w1_{e}"] = np.ascontiguousarray(ii[f"L0_ffn_e{e}_w1"])
        inm[f"w2_{e}"] = np.ascontiguousarray(ii[f"L0_ffn_e{e}_w2"])
    inm["down_b_col"] = col_layout(ii["L0_ffn_down_b"])
    inm["up_b_col"] = col_layout(ii["L0_ffn_up_b"])
    inm["core_b_col"] = col_layout(ii["L0_ffn_core_b"])
    b1cols = [col_layout(ii[f"L0_ffn_e{e}_b1"]) for e in range(NEXP)]
    b1cols.append(col_layout(ii["L0_ffn_shared_b1"]))
    inm["b1all_col"] = np.ascontiguousarray(np.concatenate(b1cols, axis=1))
    inm["b2_cat_row"] = np.ascontiguousarray(
        np.concatenate([np.asarray(ii[f"L0_ffn_e{e}_b2"], np.float32)
                        for e in range(NEXP)]).reshape(1, -1))
    inm["sb2_row"] = np.ascontiguousarray(
        np.asarray(ii["L0_ffn_shared_b2"], np.float32).reshape(1, -1))
    inm["rb_row"] = np.ascontiguousarray(
        np.asarray(ii["L0_ffn_router_b"], np.float32).reshape(1, -1))
    inm["lamcost"] = hcs["lamcost"]
    rw = np.asarray(ii["L0_ffn_router_w"], np.float32)
    inm["router_w_t"] = np.ascontiguousarray(
        rw.reshape(16, 128, NEXP).transpose(1, 0, 2).reshape(128, 16 * NEXP))
    inm["cosT"] = hcs["cosT"]
    inm["sinT"] = hcs["sinT"]
    inm["m01"] = hcs["m01"]
    inm["m01p"] = hcs["m01p"]
    inm["m01d384"] = hcs["m01d384"]
    inm["m01d256"] = hcs["m01d256"]
    inm["ident"] = hcs["ident"]
    inm["permswap"] = hcs["permswap"]
    inm["constpool"] = np.ascontiguousarray(
        np.tile(np.array(CONST_VALS, np.float32)[None, :], (128, 1)))

    if not _NC_CACHE:
        _NC_CACHE.append(build_program())
    nc = _NC_CACHE[0]
    in_maps = []
    for core in range(8):
        m = dict(inm)
        m["x"] = np.ascontiguousarray(x[core])
        in_maps.append(m)
    res = run_bass_kernel_spmd(nc, in_maps, core_ids=list(range(8)))
    out = np.stack([np.asarray(res.results[i]["y"]) for i in range(8)], axis=0)
    return out.astype(np.float32)


# version 25
# speedup vs baseline: 1.0121x; 1.0121x over previous
import contextlib
import numpy as np
import concourse.bass as bass
import concourse.mybir as mybir
from concourse.bass_utils import run_bass_kernel_spmd

F32 = mybir.dt.float32
BF16 = mybir.dt.bfloat16
AF = mybir.ActivationFunctionType
ALU = mybir.AluOpType
AX = mybir.AxisListType

S = 2048
D = 2048
NQ, NKV, HDIM = 32, 8, 64
LD = 1024
HID = [1024, 2048, 3072, 4096, 1024, 2048, 3072, 4096]
NH = [h // 128 for h in HID]
NEXP = 8
EPS = 1e-5
ESHIFT = 24.0
ACH = 256
NACH = S // ACH
FCH = 512
NFCH = S // FCH
COST_LAMBDA = 1e-7
UPTO = "all"
NCORES = 8


class _StopBuild(Exception):
    pass

ENGINES = ("tensor", "vector", "scalar", "sync", "gpsimd")
COMPUTE = ("tensor", "vector", "scalar")
QUEUES = ("sync", "gpsimd")
PSUM_RES = {("P0",), ("P1",), ("P2",), ("P3",), ("PW0",), ("PW1",)}

CONST_VALS = [0.0, 1.0, -1.0, 0.125, -ESHIFT, EPS, 1.0 / D, -0.5, 0.1, -1000.0]


class Sched:

    def __init__(self):
        self.barrier_fns = {}
        self.ops = {e: [] for e in ENGINES}
        self.counts = {e: 0 for e in ENGINES}
        self.know = {e: {} for e in ENGINES}
        self.snap = {e: [] for e in ENGINES}
        self.res = {}
        self.pending = {e: {} for e in ENGINES}
        self.wait_targets = {e: set() for e in COMPUTE}
        self.streams = {}
        self.meta = {}

    def _stream_add(self, eng, idx, sid, kind):
        st = self.streams.setdefault(sid, {"kind": kind, "ops": [], "queue": eng})
        assert st["kind"] == kind, sid
        if kind == "group":
            assert st["queue"] == eng, (sid, "group streams must be single-queue")
        k = len(st["ops"])
        st["ops"].append((eng, idx))
        self.meta[(eng, idx)] = (sid, k)

    def _need(self, eng, waits, te, ti, raw=False):
        if te in COMPUTE:
            if te == eng and eng == "tensor":
                return
            if self.know[eng].get(te, 0) >= ti + 1:
                return
            waits[te] = max(waits.get(te, 0), ti + 1)
        else:
            sid, k = self.meta[(te, ti)]
            if self.know[eng].get(sid, 0) >= k + 1:
                return
            waits[sid] = max(waits.get(sid, 0), k + 1)

    def _learn(self, eng, waits):
        for key, c in list(waits.items()):
            if self.know[eng].get(key, 0) >= c:
                del waits[key]
                continue
            self.know[eng][key] = c
            if key in COMPUTE:
                st = self.snap[key][c - 1]
            else:
                we, wi = self.streams[key]["ops"][c - 1]
                st = self.snap[we][wi]
            for f, v in st.items():
                if f != eng and v > self.know[eng].get(f, 0):
                    self.know[eng][f] = v
            if key in COMPUTE:
                self.wait_targets[key].add(c - 1)

    def op(self, eng, fn, reads=(), writes=(), stream=None):
        waits = dict(self.pending[eng])
        exact = set(waits.keys())
        self.pending[eng] = {}
        for r in reads:
            rec = self.res.get(r)
            if rec and rec["w"] is not None:
                self._need(eng, waits, *rec["w"], raw=True)
            if rec and r in PSUM_RES and eng in ("scalar", "vector"):
                for (re_, ri) in rec["r"]:
                    if re_ != eng and re_ in ("scalar", "vector"):
                        self._need(eng, waits, re_, ri)
        for r in writes:
            rec = self.res.get(r)
            if rec:
                if rec["w"] is not None:
                    self._need(eng, waits, *rec["w"])
                for (re_, ri) in rec["r"]:
                    self._need(eng, waits, re_, ri)
        exact &= set(waits.keys())
        self._learn(eng, waits)
        exact &= set(waits.keys())
        idx = self.counts[eng]
        self.ops[eng].append((fn, waits, exact))
        self.counts[eng] = idx + 1
        if eng in QUEUES:
            assert stream is not None, "queue ops need a stream"
            self._stream_add(eng, idx, stream[0], stream[1])
            st = dict(self.know[eng])
            sid, k = self.meta[(eng, idx)]
            if self.streams[sid]["kind"] == "slot":
                st[sid] = max(st.get(sid, 0), k + 1)
        else:
            st = dict(self.know[eng])
            st[eng] = idx + 1
        self.snap[eng].append(st)
        for r in reads:
            rec = self.res.setdefault(r, {"w": None, "r": []})
            rec["r"].append((eng, idx))
        for r in writes:
            self.res[r] = {"w": (eng, idx), "r": []}
        return (eng, idx)

    def barrier(self, engines=("tensor", "vector", "scalar", "sync")):
        ccur = {e: self.counts[e] for e in COMPUTE if self.counts[e] > 0}
        scur = {sid: len(st["ops"]) for sid, st in self.streams.items()
                if st["ops"] and st["queue"] in engines}
        for e in engines:
            if e in QUEUES:
                continue
            waits = {}
            for f, c in ccur.items():
                if f != e and self.know[e].get(f, 0) < c:
                    waits[f] = c
            for sid, c in scur.items():
                if self.know[e].get(sid, 0) < c:
                    waits[sid] = c
            exact = set(k for k in waits if k not in COMPUTE)
            self._learn(e, waits)
            exact &= set(waits.keys())
            idx = self.counts[e]
            bfn = self.barrier_fns.get(e, lambda en: en.nop())
            self.ops[e].append((bfn, waits, exact))
            self.counts[e] = idx + 1
            st = dict(self.know[e])
            st[e] = idx + 1
            self.snap[e].append(st)
        post = {e: self.counts[e] for e in COMPUTE if self.counts[e] > 0}
        for e in QUEUES:
            for f, c in post.items():
                self.pending[e][f] = max(self.pending[e].get(f, 0), c)

    def mark_final(self, eng_idx):
        pass

    def emit(self, block, nc, top):
        sems = {}
        for e in COMPUTE:
            sems[e] = top.enter_context(nc.semaphore(f"sem_{e}"))
        for sid in self.streams:
            sems[sid] = top.enter_context(nc.semaphore(f"sem_s_{sid}"))
        incmap = {}
        for e in COMPUTE:
            m = {}
            for j, oi in enumerate(sorted(self.wait_targets[e])):
                m[oi] = j + 1
            incmap[e] = m
        totals = {sid: len(st["ops"]) for sid, st in self.streams.items()}

        def replay(e, eng):
            for i, (fn, waits, exact) in enumerate(self.ops[e]):
                for key in sorted(waits, key=str):
                    c = waits[key]
                    if key in COMPUTE:
                        eng.wait_ge(sems[key], incmap[key][c - 1])
                    else:
                        st = self.streams[key]
                        if st["kind"] == "slot" or key in exact:
                            eng.wait_ge(sems[key], 16 * c)
                        elif st["queue"] == e:
                            m = sum(1 for (se, si) in st["ops"] if si < i)
                            assert m >= c, (key, c, m)
                            eng.wait_ge(sems[key], 16 * m)
                        else:
                            eng.wait_ge(sems[key], 16 * totals[key])
                inst = fn(eng)
                if e in COMPUTE:
                    if i in incmap[e]:
                        inst.then_inc(sems[e], 1)
                else:
                    sid, k = self.meta[(e, i)]
                    inst.then_inc(sems[sid], 16)
            if e in QUEUES:
                done = set()
                for i in range(len(self.ops[e])):
                    sid, k = self.meta[(e, i)]
                    done.add(sid)
                for sid in sorted(done, key=str):
                    eng.wait_ge(sems[sid], 16 * totals[sid])

        @block.tensor
        def _(eng):
            replay("tensor", eng)

        @block.vector
        def _(eng):
            replay("vector", eng)

        @block.scalar
        def _(eng):
            replay("scalar", eng)

        @block.sync
        def _(eng):
            replay("sync", eng)

        @block.gpsimd
        def _(eng):
            replay("gpsimd", eng)


def host_constants():
    c = {}
    k = np.arange(0, HDIM, 2)[: HDIM // 2].astype(np.float64)
    inv_freq = 1.0 / (10000.0 ** (k / HDIM))
    pos = np.arange(S, dtype=np.float64)
    ang = pos[:, None] * inv_freq[None, :]
    ang2 = np.concatenate([ang, ang], axis=1)
    cos64 = np.cos(ang2).T
    sin64 = np.sin(ang2).T
    sgn = np.ones((64, 1))
    sgn[:32] = -1.0
    sins = sin64 * sgn
    c["cosT"] = np.ascontiguousarray(np.concatenate([cos64, cos64], 0).astype(np.float32))
    c["sinT"] = np.ascontiguousarray(np.concatenate([sins, sins], 0).astype(np.float32))
    p = np.arange(128)[:, None]
    j = np.arange(896)[None, :] - 384
    m01 = (j >= p).astype(np.float32)
    c["m01"] = np.ascontiguousarray(m01)
    c["m01p"] = np.ascontiguousarray(
        np.concatenate([m01[:, 384:640], m01[:, 256:512]], axis=1))
    c["m01d384"] = np.ascontiguousarray(np.tile(m01[:, 384:640], (1, 2)))
    c["m01d256"] = np.ascontiguousarray(np.tile(m01[:, 256:512], (1, 2)))
    c["ident"] = np.eye(128, dtype=np.float32)
    pm = np.zeros((128, 128), dtype=np.float32)
    idx = np.arange(128)
    pm[idx ^ 32, idx] = 1.0
    c["permswap"] = pm
    c["lamcost"] = (np.float32(COST_LAMBDA) * np.array(
        [np.float32(LD * h + h * LD) for h in HID], dtype=np.float32)).reshape(1, 8)
    return c


def col_layout(b):
    return np.ascontiguousarray(np.asarray(b, np.float32).reshape(-1, 128).T)


def qplace(h):
    g = h // 4
    return (g // 2) * 4 + (h % 4), 64 * (g % 2)


def q_perm():
    pos = np.empty(D, dtype=np.int64)
    for h in range(NQ):
        t, off = qplace(h)
        pos[t * 128 + off: t * 128 + off + 64] = np.arange(h * 64, (h + 1) * 64)
    return pos


def build_program():
    nc = bass.Bass("TRN2", target_bir_lowering=False)
    s = Sched()

    def _ckpt(name):
        if UPTO == name:
            raise _StopBuild()
    top = contextlib.ExitStack()

    def sb(stk, name, shape, dtype):
        return stk.enter_context(nc.sbuf_tensor(name, shape, dtype))

    def dma(q, out, in_, reads=(), writes=(), stream=None):
        return s.op(q, (lambda o, i: lambda e: e.dma_start(out=o, in_=i))(out, in_),
                    reads=reads, writes=writes, stream=stream)

    def mm(out, lhsT, rhs, start, stop, reads=(), writes=()):
        return s.op("tensor",
                    (lambda o, l, r, st, sp: lambda e: e.matmul(o, l, r, start=st, stop=sp))(
                        out, lhsT, rhs, start, stop), reads=reads, writes=writes)

    def tr(out, in_, ident, reads=(), writes=()):
        return s.op("tensor", (lambda o, i, d: lambda e: e.transpose(o, i, d))(out, in_, ident),
                    reads=reads, writes=writes)

    def act(out, in_, func, bias=0.0, scale=1.0, reads=(), writes=()):
        return s.op("scalar",
                    (lambda o, i, f, b, sc: lambda e: e.activation(o, i, f, bias=b, scale=sc))(
                        out, in_, func, bias, scale), reads=reads, writes=writes)

    def scopy(out, in_, reads=(), writes=()):
        return s.op("scalar", (lambda o, i: lambda e: e.copy(o, i))(out, in_),
                    reads=reads, writes=writes)

    def vop(name, *args, reads=(), writes=()):
        return s.op("vector", (lambda n, a: lambda e: getattr(e, n)(*a))(name, args),
                    reads=reads, writes=writes)

    def memset(ap, val, writes=()):
        return s.op("vector", (lambda a, v: lambda e: e.memset(a, v))(ap, val), writes=writes)

    def din(name, shape):
        return nc.dram_tensor(name, list(shape), F32, kind="ExternalInput").ap()

    def dint(name, shape, dt):
        return nc.dram_tensor(name, list(shape), dt, kind="Internal").ap()

    x_d = din("x", (S, D))
    y_d = nc.dram_tensor("y", [S, D], F32, kind="ExternalOutput").ap()
    wqT_d = din("wqT", (D, D))
    wkT_d = din("wkT", (D, NKV * HDIM))
    wvT_d = din("wvT", (D, NKV * HDIM))
    woT_d = din("woT", (D, D))
    down_d = din("down_w", (D, LD))
    up_d = din("up_w", (LD, D))
    core_d = din("core_w", (D, D))
    sw1_d = din("sw1", (LD, LD))
    sw2_d = din("sw2", (LD, LD))
    w1_d = [din(f"w1_{e}", (LD, HID[e])) for e in range(NEXP)]
    w2_d = [din(f"w2_{e}", (HID[e], LD)) for e in range(NEXP)]
    downb_d = din("down_b_col", (128, LD // 128))
    upb_d = din("up_b_col", (128, D // 128))
    coreb_d = din("core_b_col", (128, D // 128))
    b1all_d = din("b1all_col", (128, sum(NH) + 8))
    b2row_d = din("b2_cat_row", (1, NEXP * LD))
    sb2row_d = din("sb2_row", (1, LD))
    rbrow_d = din("rb_row", (1, NEXP))
    lam_d = din("lamcost", (1, NEXP))
    rwt_d = din("router_w_t", (128, 16 * NEXP))
    cosT_d = din("cosT", (128, S))
    sinT_d = din("sinT", (128, S))
    m01_d = din("m01", (128, 896))
    m01p_d = din("m01p", (128, 512))
    m01d384_d = din("m01d384", (128, 512))
    m01d256_d = din("m01d256", (128, 512))
    ident_d = din("ident", (128, 128))
    perm_d = din("permswap", (128, 128))
    consts_d = din("constpool", (128, len(CONST_VALS)))

    x1_dram = dint("x1_scr", (S, D), F32)
    ln1T_dram = dint("ln1T_scr", (16, 128, S), BF16)
    ln3T_dram = dint("ln3T_scr", (16, 128, S), BF16)
    wq_c = dint("wq_c", (16, 128, 16, 128), BF16)
    wo_c = dint("wo_c", (16, 128, 16, 128), BF16)
    down_c = dint("down_c", (8, 128, 16, 128), BF16)
    up_c = dint("up_c", (16, 128, 8, 128), BF16)
    core_c = dint("core_c", (16, 128, 16, 128), BF16)
    sw1_c = dint("sw1_c", (8, 128, 8, 128), BF16)
    w1_c = [dint(f"w1c_{e}", (NH[e], 128, 8, 128), BF16) for e in range(NEXP)]
    sw2_c = dint("sw2_c", (LD, LD), BF16)
    w2_c = [dint(f"w2c_{e}", (HID[e], LD), BF16) for e in range(NEXP)]
    ustack_dram = dint("ustack_scr", (10, LD), BF16)

    cpool = sb(top, "cpool", [128, len(CONST_VALS)], F32)
    for i, v in enumerate(CONST_VALS):
        nc.const_aps.register(F32, float(v), cpool[:, i:i + 1])
    ident_b = sb(top, "ident_b", [128, 128], BF16)
    ident_f = sb(top, "ident_f", [128, 128], F32)
    perm_b = sb(top, "perm_b", [128, 128], BF16)
    ones1 = sb(top, "ones1", [1, 128], BF16)
    onesc = sb(top, "onesc", [128, 1], BF16)
    m01_b = sb(top, "m01_b", [128, 896], BF16)
    m01p_b = sb(top, "m01p_b", [128, 512], BF16)
    m01d_b = [sb(top, f"m01d{i}_b", [128, 512], BF16) for i in range(2)]
    cvt = [sb(top, f"cvt{i}", [128, 2048], BF16) for i in range(2)]
    cbst = [sb(top, f"cbst{i}", [128, 16 * 128], BF16) for i in range(3)]
    w2st = [sb(top, f"w2st{i}", [128, LD], BF16) for i in range(4)]
    downb = sb(top, "downb", [128, LD // 128], F32)
    bcol = sb(top, "bcol", [128, D // 128], F32)
    colt = sb(top, "colt", [128, 16], F32)
    b1all = sb(top, "b1all", [128, sum(NH) + 8], F32)
    gb1 = sb(top, "gb1", [128, sum(NH) + 8], BF16)
    rw_sb = sb(top, "rw_sb", [128, 16 * NEXP], F32)
    rw_hi = sb(top, "rw_hi", [128, 16 * NEXP], BF16)
    rw_lo = sb(top, "rw_lo", [128, 16 * NEXP], BF16)
    rbl = sb(top, "rbl", [1, 4 * NEXP], F32)
    rbl_hi = sb(top, "rbl_hi", [1, NEXP], BF16)
    rbl_lo = sb(top, "rbl_lo", [1, NEXP], BF16)
    w_tm = sb(top, "w_tm", [128, 16 * 9], F32)
    wT_sb = sb(top, "wT_sb", [10, S], BF16)
    ustack = sb(top, "ustack", [10, LD], BF16)
    crow = sb(top, "crow", [1, LD], F32)
    urow = sb(top, "urow", [1, LD], F32)

    barc = sb(top, "barc", [1, 2], F32)
    s.barrier_fns = {
        "vector": (lambda en: en.memset(barc[0:1, 0:1], 0.0)),
        "scalar": (lambda en: en.memzero(barc[0:1, 1:2])),
    }
    P0 = top.enter_context(nc.psum_tensor("P0", [128, 512], F32))
    P1 = top.enter_context(nc.psum_tensor("P1", [128, 512], F32))
    P2 = top.enter_context(nc.psum_tensor("P2", [128, 512], F32))
    P3 = top.enter_context(nc.psum_tensor("P3", [128, 512], F32))
    PW0 = top.enter_context(nc.psum_tensor("PW0", [128, 1024], F32))
    PW1 = top.enter_context(nc.psum_tensor("PW1", [128, 1024], F32))
    PS = [P0, P1, P2, P3]
    PN = [("P0",), ("P1",), ("P2",), ("P3",)]

    print('SBUF free after static:', nc.sbuf_bytes_remaining)
    _open = []

    def _mk():
        st = contextlib.ExitStack()
        _open.append(st)
        return st

    def _close(st):
        assert _open and _open[-1] is st
        _open.pop()
        st.close()

    try:
        init = _mk()
        rowtmp = sb(init, "rowtmp", [1, NEXP * LD], F32)
        dma("gpsimd", ident_b[:], ident_d, writes=[("identb",)], stream=("gi", "group"))
        dma("sync", ident_f[:], ident_d, writes=[("identf",)], stream=("si", "group"))
        dma("gpsimd", perm_b[:], perm_d, writes=[("permb",)], stream=("gi", "group"))
        dma("gpsimd", m01_b[:], m01_d, writes=[("m01",)], stream=("gi", "group"))
        dma("gpsimd", m01p_b[:], m01p_d, writes=[("m01p",)], stream=("gi", "group"))
        dma("gpsimd", m01d_b[0][:], m01d384_d, writes=[("m01d", 0)], stream=("gi", "group"))
        dma("gpsimd", m01d_b[1][:], m01d256_d, writes=[("m01d", 1)], stream=("gi", "group"))
        dma("sync", cpool[:], consts_d, writes=[("cpool",)], stream=("si", "group"))
        dma("sync", downb[:], downb_d, writes=[("downb",)], stream=("si", "group"))
        dma("sync", bcol[:], upb_d, writes=[("bcol",)], stream=("si", "group"))
        dma("sync", colt[:], coreb_d, writes=[("coltc",)], stream=("si", "group"))
        vop("tensor_add", bcol[:], bcol[:], colt[:], reads=[("bcol",), ("coltc",)],
            writes=[("bcol",)])
        dma("sync", b1all[:], b1all_d, writes=[("b1all",)], stream=("si", "group"))
        act(gb1[:], b1all[:], AF.Gelu, reads=[("b1all",)], writes=[("gb1",)])
        dma("sync", rowtmp[:], b2row_d, writes=[("rowtmp",)], stream=("si", "group"))
        dma("sync", rbl[0:1, 0:NEXP], rbrow_d, writes=[("rbl",)], stream=("si", "group"))
        dma("sync", rbl[0:1, 16:16 + NEXP], lam_d, writes=[("rbllam",)], stream=("si", "group"))
        vop("tensor_sub", rbl[0:1, 0:NEXP], rbl[0:1, 0:NEXP], rbl[0:1, 16:16 + NEXP],
            reads=[("rbl",), ("rbllam",)], writes=[("rbl",)])
        scopy(rbl_hi[:], rbl[0:1, 0:NEXP], reads=[("rbl",)], writes=[("rblhi",)])
        scopy(rbl[0:1, 8:8 + NEXP], rbl_hi[:], reads=[("rblhi",)], writes=[("rblhf",)])
        vop("tensor_sub", rbl_lo[:], rbl[0:1, 0:NEXP], rbl[0:1, 8:8 + NEXP],
            reads=[("rbl",), ("rblhf",)], writes=[("rbllo",)])
        dma("sync", rw_sb[:], rwt_d, writes=[("rw",)], stream=("si", "group"))
        scopy(rw_hi[:], rw_sb[:], reads=[("rw",)], writes=[("rwhi",)])
        vop("tensor_sub", rw_lo[:], rw_sb[:], rw_hi[:], reads=[("rw",), ("rwhi",)],
            writes=[("rwlo",)])
        memset(ones1[:], 1.0, writes=[("ones1",)])
        memset(onesc[:], 1.0, writes=[("onesc",)])
        memset(wT_sb[0:10, :], 1.0, writes=[("wTones",)])
        vop("tensor_add", rowtmp[0:1, 0:4096], rowtmp[0:1, 0:4096], rowtmp[0:1, 4096:8192],
            reads=[("rowtmp",)], writes=[("rowtmp",)])
        vop("tensor_add", rowtmp[0:1, 0:2048], rowtmp[0:1, 0:2048], rowtmp[0:1, 2048:4096],
            reads=[("rowtmp",)], writes=[("rowtmp",)])
        vop("tensor_add", crow[0:1, :], rowtmp[0:1, 0:1024], rowtmp[0:1, 1024:2048],
            reads=[("rowtmp",)], writes=[("crow",)])
        cvt_i = [0]

        def convert_tiled(src, dst4, R, C):
            for c0 in range(R // 128):
                for cc in range(0, C, 2048):
                    w = min(2048, C - cc)
                    sl = cvt_i[0] % 2
                    cvt_i[0] += 1
                    rid = ("cvt", sl)
                    dma("gpsimd", cvt[sl][:, 0:w], src[c0 * 128:(c0 + 1) * 128, cc:cc + w],
                        writes=[rid], stream=(f"cvt{sl}", "slot"))
                    dma("gpsimd", dst4[cc // 128:(cc + w) // 128, :, c0, :].transpose([1, 0, 2]),
                        cvt[sl][:, 0:w].rearrange("p (j o) -> p j o", o=128),
                        reads=[rid], writes=[("c_" + dst4.name,)],
                        stream=("c_" + dst4.name, "group"))

        def convert_plain(src, dst, R, C):
            for c0 in range(R // 128):
                sl = cvt_i[0] % 2
                cvt_i[0] += 1
                rid = ("cvt", sl)
                dma("gpsimd", cvt[sl][:, 0:C], src[c0 * 128:(c0 + 1) * 128, :], writes=[rid],
                    stream=(f"cvt{sl}", "slot"))
                dma("gpsimd", dst[c0 * 128:(c0 + 1) * 128, :], cvt[sl][:, 0:C],
                    reads=[rid], writes=[("c_" + dst.name,)], stream=("c_" + dst.name, "group"))

        convert_tiled(wqT_d, wq_c, D, D)
        convert_tiled(woT_d, wo_c, D, D)
        for e in range(NEXP):
            convert_plain(w2_d[e], w2_c[e], HID[e], LD)
        convert_tiled(down_d, down_c, D, LD)
        convert_plain(sw2_d, sw2_c, LD, LD)
        convert_tiled(sw1_d, sw1_c, LD, LD)
        for e in range(NEXP):
            convert_tiled(w1_d[e], w1_c[e], LD, HID[e])
        convert_tiled(up_d, up_c, LD, D)
        convert_tiled(core_d, core_c, D, D)
        s.barrier()
        _close(init)

        spanKV = _mk()
        K_sb = sb(spanKV, "K_sb", [128, 4 * S], BF16)
        V_sb = sb(spanKV, "V_sb", [128, 16 * 520], BF16)

        span01 = _mk()
        ln1T = sb(span01, "ln1T", [128, 16 * S], BF16)

        a0 = _mk()
        x_t = [sb(a0, f"x_t{i}", [128, D], F32) for i in range(2)]
        ln1f = [sb(a0, f"ln1f{i}", [128, D], BF16) for i in range(2)]
        sqt = sb(a0, "sqt", [128, D], F32)
        colsA = sb(a0, "colsA", [128, 8], F32)

        def ln_token(xap, sqap, mcol, rcol, m2col, statres, xtag, sqtag):
            vop("reduce_sum", mcol, xap, AX.X, reads=[xtag], writes=[statres])
            act(sqap, xap, AF.Square, reads=[xtag], writes=[sqtag])
            vop("reduce_sum", rcol, sqap, AX.X, reads=[sqtag], writes=[statres])
            vop("tensor_scalar", mcol, mcol, 1.0 / D, None, ALU.mult,
                reads=[statres], writes=[statres])
            vop("tensor_scalar", rcol, rcol, 1.0 / D, None, ALU.mult,
                reads=[statres], writes=[statres])
            vop("tensor_tensor", m2col, mcol, mcol, ALU.mult, reads=[statres], writes=[statres])
            vop("tensor_sub", rcol, rcol, m2col, reads=[statres], writes=[statres])
            act(rcol, rcol, AF.Ln, bias=EPS, reads=[statres], writes=[statres])
            act(rcol, rcol, AF.Exp, scale=-0.5, reads=[statres], writes=[statres])

        lnv = ln1T[:].rearrange("p (c t) -> p c t", c=16)
        for tt in range(16):
            sl = tt % 2
            xa, la = x_t[sl], ln1f[sl]
            xr = ("x_t", sl)
            dma("sync", xa[:], x_d[tt * 128:(tt + 1) * 128, :], writes=[xr],
                stream=(f"xt{sl}", "slot"))
            stat = ("lnA", sl)
            mcol, rcol, m2col = colsA[:, 3 * sl:3 * sl + 1], colsA[:, 3 * sl + 1:3 * sl + 2], \
                colsA[:, 3 * sl + 2:3 * sl + 3]
            ln_token(xa[:], sqt[:], mcol, rcol, m2col, stat, xr, ("sqA",))
            vop("tensor_scalar", la[:], xa[:], mcol, rcol, ALU.subtract, ALU.mult,
                reads=[xr, stat], writes=[("ln1f", sl)])
            for half in range(2):
                pp = PS[2 + half]
                ppb = pp[:].bitcast(BF16)
                pres = PN[2 + half]
                for j in range(8):
                    dc = half * 8 + j
                    tr(ppb[:, j * 128:(j + 1) * 128], la[:, dc * 128:(dc + 1) * 128], ident_b[:],
                       reads=[("ln1f", sl), ("identb",)], writes=[pres])
                scopy(lnv[:, half * 8:(half + 1) * 8, tt * 128:(tt + 1) * 128],
                      ppb[:, 0:1024].rearrange("p (c t) -> p c t", c=8),
                      reads=[pres], writes=[("ln1T",)])

        s.barrier()
        _close(a0)
        _ckpt("a0")

        print('SBUF free at A1(before allocs):', nc.sbuf_bytes_remaining)
        a1 = _mk()
        wvst = sb(a1, "wvst", [128, 16 * 512], BF16)
        wvstf = [sb(a1, f"wvstf{i}", [128, 512], F32) for i in range(2)]
        kstf = [sb(a1, f"kstf{i}", [128, 16 * 128], F32) for i in range(2)]
        cosc = sb(a1, "cosc", [128, 512], F32)
        sinc = sb(a1, "sinc", [128, 512], F32)
        kpb = [sb(a1, f"kpb{i}", [128, 512], BF16) for i in range(2)]
        rp1 = [sb(a1, f"rp1_{i}", [128, 512], F32) for i in range(2)]
        rp2 = [sb(a1, f"rp2_{i}", [128, 512], F32) for i in range(2)]

        memset(V_sb[:, 64::65], 1.0, writes=[("Vones",)])
        for dc in range(16):
            dma("sync", wvstf[dc % 2][:], wvT_d[dc * 128:(dc + 1) * 128, :],
                writes=[("wvstf", dc % 2)], stream=(f"wvst{dc % 2}", "slot"))
            vop("tensor_copy", wvst[:, dc * 512:(dc + 1) * 512], wvstf[dc % 2][:],
                reads=[("wvstf", dc % 2)], writes=[("wvst", dc)])
        vv = V_sb[:].rearrange("p (c x) -> p c x", c=16)
        for tt in range(16):
            pp = PS[tt % 2]
            pres = PN[tt % 2]
            for dc in range(16):
                mm(pp[:], lnv[:, dc, tt * 128:(tt + 1) * 128], wvst[:, dc * 512:(dc + 1) * 512],
                   dc == 0, dc == 15, reads=[("ln1T",), ("wvst", dc)], writes=[pres])
            scopy(vv[:, tt, :].rearrange("p (h z) -> p h z", h=8)[:, :, 0:64],
                  pp[:].rearrange("p (h z) -> p h z", h=8),
                  reads=[pres], writes=[("V", tt)])

        _ckpt("a1v")
        for ot in range(4):
            cb = cbst[ot % 2]
            cbres = ("cbst", ot % 2)
            dma("sync", kstf[ot % 2][:].rearrange("p (c o) -> p c o", c=16),
                wkT_d[:, ot * 128:(ot + 1) * 128].rearrange("(c p) o -> p c o", p=128),
                writes=[("kstf", ot % 2)], stream=(f"kcb{ot % 2}", "slot"))
            vop("tensor_copy", cb[:], kstf[ot % 2][:],
                reads=[("kstf", ot % 2)], writes=[cbres])
            _ckpt("a1kd")
            for tq in range(4):
                dma("sync", cosc[:], cosT_d[:, tq * 512:(tq + 1) * 512], writes=[("cosc",)],
                    stream=("cosc", "slot"))
                _ckpt("a1cA")
                dma("sync", sinc[:], sinT_d[:, tq * 512:(tq + 1) * 512], writes=[("sinc",)],
                    stream=("sinc", "slot"))
                _ckpt("a1c")
                pp = PS[tq % 2]
                pres = PN[tq % 2]
                psw = PS[2 + tq % 2]
                psres = PN[2 + tq % 2]
                for dc in range(16):
                    mm(pp[:], cb[:, dc * 128:(dc + 1) * 128], lnv[:, dc, tq * 512:(tq + 1) * 512],
                       dc == 0, dc == 15, reads=[cbres, ("ln1T",)], writes=[pres])
                sl = tq % 2
                _ckpt("a1m")
                scopy(kpb[sl][:], pp[:], reads=[pres], writes=[("kpb", sl)])
                _ckpt("a1s")
                mm(psw[:], perm_b[:], kpb[sl][:], True, True,
                   reads=[("permb",), ("kpb", sl)], writes=[psres])
                _ckpt("a1p")
                vop("tensor_mul", rp1[sl][:], pp[:], cosc[:], reads=[pres, ("cosc",)],
                    writes=[("rp1", sl)])
                _ckpt("a1r1")
                vop("tensor_mul", rp2[sl][:], psw[:], sinc[:], reads=[psres, ("sinc",)],
                    writes=[("rp2", sl)])
                _ckpt("a1r2")
                vop("tensor_add", K_sb[:, ot * S + tq * 512: ot * S + (tq + 1) * 512],
                    rp1[sl][:], rp2[sl][:], reads=[("rp1", sl), ("rp2", sl)], writes=[("K",)])
                _ckpt("a1r")

            _ckpt("a1k1")
        _ckpt("a1k")
        for q4 in range(4):
            dma("sync", ln1T_dram[q4 * 4:(q4 + 1) * 4, :, :].transpose([1, 0, 2]),
                lnv[:, q4 * 4:(q4 + 1) * 4, :], reads=[("ln1T",)], writes=[("ln1Td",)],
                stream=("ln1Td", "group"))
        s.barrier()
        _close(a1)
        _close(span01)
        _ckpt("a1")

        print('SBUF free at A2(before allocs):', nc.sbuf_bytes_remaining)
        a2 = _mk()
        ln1Tc = [sb(a2, "ln1Tc0", [128, 16 * ACH], BF16)] * 2
        cosc2 = [sb(a2, f"cosc2_{i}", [128, ACH], F32) for i in range(2)]
        sinc2 = [sb(a2, f"sinc2_{i}", [128, ACH], F32) for i in range(2)]
        QTc = sb(a2, "QTc", [128, 16 * ACH], BF16)
        qpb = [sb(a2, f"qpb{i}", [128, ACH], BF16) for i in range(2)]
        qr1 = [sb(a2, "qr1_0", [128, ACH], F32)] * 2
        qr2 = [sb(a2, "qr2_0", [128, ACH], F32)] * 2
        E_sl = [sb(a2, f"E{i}", [128, 2 * ACH], BF16) for i in range(4)]
        OTc = sb(a2, "OTc", [128, 16 * ACH], BF16)
        tmpO = [sb(a2, f"tmpO{i}", [64, 512], F32) for i in range(2)]
        aoTc = sb(a2, "aoTc", [128, 16 * ACH], F32)
        sqb = sb(a2, "sqb", [128, 16 * ACH], BF16)
        ntmp = [sb(a2, f"ntmp{i}", [128, ACH], F32) for i in range(2)]
        xa_t = sb(a2, "xa_t", [128, D], F32)
        x1_t = sb(a2, "x1_t", [128, D], F32)
        ln3f = sb(a2, "ln3f", [128, D], F32)
        ln3h = sb(a2, "ln3h", [128, D], BF16)
        ln3lo = sb(a2, "ln3lo", [128, D], BF16)
        lst = [sb(a2, f"lst{i}", [128, 8 * 128], BF16) for i in range(2)]
        lstl = [sb(a2, f"lstl{i}", [128, 8 * 128], BF16) for i in range(2)]
        colsB = sb(a2, "colsB", [128, 8], F32)
        rr = sb(a2, "rr", [1, 512], F32)
        rr2 = sb(a2, "rr2", [1, 512], F32)
        logit = sb(a2, "logit", [128, 8], F32)
        prob = sb(a2, "prob", [128, 8], F32)
        gsc = sb(a2, "gsc", [128, 24], F32)
        rows = sb(a2, "rows", [1, 2304], F32)
        rows_b = sb(a2, "rows_b", [1, 3072], BF16)

        def rows_hilo(src, n, hi_b, lo_b, ftmp, base, ftag):
            scopy(hi_b, src, reads=[base], writes=[(base, "hi")])
            scopy(ftmp, hi_b, reads=[(base, "hi")], writes=[(base, "hf"), ftag])
            vop("tensor_sub", lo_b, src, ftmp,
                reads=[base, (base, "hf"), ftag], writes=[(base, "lo")])

        def chunk_epilogue(c):
            t0 = c * ACH
            for tb in range(ACH // 128):
                tba = c * (ACH // 128) + tb
                dma("sync", xa_t[:], x_d[tba * 128:(tba + 1) * 128, :], writes=[("xa",)],
                    stream=("xa", "slot"))
                for half in range(2):
                    pw = PW0 if half == 0 else PW1
                    pwn = "PW0" if half == 0 else "PW1"
                    for j in range(8):
                        dt = half * 8 + j
                        tr(pw[:, j * 128:(j + 1) * 128],
                           aoTc[:, dt * ACH + tb * 128: dt * ACH + tb * 128 + 128], ident_f[:],
                           reads=[("aoTc",), ("identf",)], writes=[(pwn, j // 4)])
                    vop("tensor_add", x1_t[:, half * 1024:(half + 1) * 1024], pw[:],
                        xa_t[:, half * 1024:(half + 1) * 1024],
                        reads=[(pwn, 0), (pwn, 1), ("xa",)], writes=[("x1",)])
                dma("sync", x1_dram[tba * 128:(tba + 1) * 128, :], x1_t[:],
                    reads=[("x1",)], writes=[("x1d",)], stream=("x1d", "slot"))
                mcol, rcol, m2col = colsB[:, 0:1], colsB[:, 1:2], colsB[:, 2:3]
                ln_token(x1_t[:], xa_t[:], mcol, rcol, m2col, ("lnB",), ("x1",), ("xa",))
                vop("tensor_scalar", ln3f[:], x1_t[:], mcol, rcol, ALU.subtract, ALU.mult,
                    reads=[("x1",), ("lnB",)], writes=[("ln3f",)])
                scopy(ln3h[:], ln3f[:], reads=[("ln3f",)], writes=[("ln3h",)])
                scopy(xa_t[:], ln3h[:], reads=[("ln3h",)], writes=[("xa",)])
                vop("tensor_sub", ln3lo[:], ln3f[:], xa_t[:],
                    reads=[("ln3f",), ("xa",)], writes=[("ln3lo",)])
                for half in range(2):
                    pp = PS[2 + half]
                    ppb = pp[:].bitcast(BF16)
                    pres = PN[2 + half]
                    for j in range(8):
                        dc = half * 8 + j
                        tr(ppb[:, j * 128:(j + 1) * 128], ln3h[:, dc * 128:(dc + 1) * 128],
                           ident_b[:], reads=[("ln3h",), ("identb",)], writes=[pres])
                    scopy(lst[half][:], ppb[:, 0:1024], reads=[pres], writes=[("lst", half)])
                    dma("sync",
                        ln3T_dram[half * 8:(half + 1) * 8, :, tba * 128:(tba + 1) * 128]
                        .transpose([1, 0, 2]),
                        lst[half][:].rearrange("p (c t) -> p c t", c=8),
                        reads=[("lst", half)], writes=[("ln3Td", half)],
                        stream=(f"l3d{half}", "slot"))
                    for j in range(8):
                        dc = half * 8 + j
                        tr(ppb[:, j * 128:(j + 1) * 128], ln3lo[:, dc * 128:(dc + 1) * 128],
                           ident_b[:], reads=[("ln3lo",), ("identb",)], writes=[pres])
                    scopy(lstl[half][:], ppb[:, 0:1024], reads=[pres], writes=[("lstl", half)])
                rwv_hi = rw_hi[:].rearrange("p (c e) -> p c e", c=16)
                rwv_lo = rw_lo[:].rearrange("p (c e) -> p c e", c=16)
                for half in range(2):
                    for j in range(8):
                        dc = half * 8 + j
                        first = (half == 0 and j == 0)
                        mm(P0[:, 0:8], lst[half][:, j * 128:(j + 1) * 128], rwv_hi[:, dc, :],
                           first, False, reads=[("lst", half), ("rwhi",)], writes=[("P0",)])
                        mm(P0[:, 0:8], lst[half][:, j * 128:(j + 1) * 128], rwv_lo[:, dc, :],
                           False, False, reads=[("lst", half), ("rwlo",)], writes=[("P0",)])
                        mm(P0[:, 0:8], lstl[half][:, j * 128:(j + 1) * 128], rwv_hi[:, dc, :],
                           False, False, reads=[("lstl", half), ("rwhi",)], writes=[("P0",)])
                mm(P0[:, 0:8], ones1[:], rbl_hi[:], False, False,
                   reads=[("ones1",), ("rblhi",)], writes=[("P0",)])
                mm(P0[:, 0:8], ones1[:], rbl_lo[:], False, True,
                   reads=[("ones1",), ("rbllo",)], writes=[("P0",)])
                scopy(logit[:], P0[:, 0:8], reads=[("P0",)], writes=[("logit",)])
                g0, g1 = gsc[:, 0:1], gsc[:, 1:2]
                m1, m2 = gsc[:, 8:16], gsc[:, 16:24]
                vop("reduce_max", g0, logit[:], AX.X, reads=[("logit",)], writes=[("g0",)])
                vop("tensor_scalar", prob[:], logit[:], g0, None, ALU.subtract,
                    reads=[("logit",), ("g0",)], writes=[("prob",)])
                act(prob[:], prob[:], AF.Exp, reads=[("prob",)], writes=[("prob",)])
                vop("reduce_sum", g1, prob[:], AX.X, reads=[("prob",)], writes=[("g1",)])
                vop("reciprocal", g1, g1, reads=[("g1",)], writes=[("g1",)])
                vop("tensor_scalar", prob[:], prob[:], g1, None, ALU.mult,
                    reads=[("prob",), ("g1",)], writes=[("prob",)])
                vop("reduce_max", g0, prob[:], AX.X, reads=[("prob",)], writes=[("g0",)])
                vop("tensor_scalar", m1, prob[:], g0, None, ALU.is_ge,
                    reads=[("prob",), ("g0",)], writes=[("m1",)])
                vop("scalar_tensor_tensor", m2, m1, -1000.0, prob[:], ALU.mult, ALU.add,
                    reads=[("m1",), ("prob",)], writes=[("m2",)])
                vop("reduce_max", g0, m2, AX.X, reads=[("m2",)], writes=[("g0",)])
                vop("tensor_scalar", m2, m2, g0, None, ALU.is_ge,
                    reads=[("m2",), ("g0",)], writes=[("m2",)])
                vop("tensor_add", m1, m1, m2, reads=[("m1",), ("m2",)], writes=[("m1",)])
                act(prob[:], prob[:], AF.Exp, reads=[("prob",)], writes=[("prob",)])
                vop("tensor_mul", prob[:], prob[:], m1, reads=[("prob",), ("m1",)],
                    writes=[("prob",)])
                vop("reduce_sum", g1, prob[:], AX.X, reads=[("prob",)], writes=[("g1",)])
                vop("reciprocal", g1, g1, reads=[("g1",)], writes=[("g1",)])
                vop("tensor_scalar", w_tm[:, tba * 9: tba * 9 + 8], prob[:], g1, None, ALU.mult,
                    reads=[("prob",), ("g1",)], writes=[("w_tm",)])
                tr(P1[0:8, 0:128], w_tm[:, tba * 9: tba * 9 + 8], ident_f[:],
                   reads=[("w_tm",), ("identf",)], writes=[("P1",)])
                scopy(wT_sb[0:8, tba * 128:(tba + 1) * 128], P1[0:8, 0:128],
                      reads=[("P1",)], writes=[("wT",)])


        for c in range(NACH):
            t0 = c * ACH
            cpar = 0
            l1vc = ln1Tc[cpar][:].rearrange("p (c t) -> p c t", c=16)
            cosv, sinv = cosc2[cpar], sinc2[cpar]
            dma("sync", l1vc, ln1T_dram[:, :, t0:t0 + ACH].transpose([1, 0, 2]),
                reads=[("ln1Td",)], writes=[("ln1Tc", cpar)], stream=(f"l1c{cpar}", "slot"))
            dma("sync", cosv[:], cosT_d[:, t0:t0 + ACH], writes=[("cosc", cpar)],
                stream=(f"cosc{cpar}", "slot"))
            dma("sync", sinv[:], sinT_d[:, t0:t0 + ACH], writes=[("sinc", cpar)],
                stream=(f"sinc{cpar}", "slot"))

            def jit_mms(ot):
                cb = cbst[ot % 3]
                cbres = ("cbst", ot % 3)
                dma("sync", cb[:].rearrange("p (c o) -> p c o", c=16),
                    wq_c[ot, :, :, :],
                    reads=[("c_wq_c",)], writes=[cbres], stream=(f"cb{ot % 3}", "slot"))
                pp = PS[ot % 2]
                pres = PN[ot % 2]
                for dc in range(16):
                    mm(pp[:, 0:ACH], cb[:, dc * 128:(dc + 1) * 128], l1vc[:, dc, :],
                       dc == 0, dc == 15, reads=[cbres, ("ln1Tc", cpar)], writes=[pres])

            def jit_rope(ot):
                sl = ot % 2
                pp = PS[sl]
                pres = PN[sl]
                scopy(qpb[sl][:], pp[:, 0:ACH], reads=[pres], writes=[("qpb", sl)])
                psw = PS[2 + sl]
                psres = PN[2 + sl]
                mm(psw[:, 0:ACH], perm_b[:], qpb[sl][:], True, True,
                   reads=[("permb",), ("qpb", sl)], writes=[psres])
                vop("tensor_mul", qr1[sl][:], pp[:, 0:ACH], cosv[:],
                    reads=[pres, ("cosc", cpar)], writes=[("qr1",)])
                vop("tensor_mul", qr2[sl][:], psw[:, 0:ACH], sinv[:],
                    reads=[psres, ("sinc", cpar)], writes=[("qr2",)])
                vop("tensor_add", QTc[:, ot * ACH:(ot + 1) * ACH], qr1[sl][:], qr2[sl][:],
                    reads=[("qr1",), ("qr2",)], writes=[("QTc",)])

            for ot in range(16):
                jit_mms(ot)
                if ot > 0:
                    jit_rope(ot - 1)
            jit_rope(15)
            if c > 0:
                chunk_epilogue(c - 1)

            nkt = 2 * (c + 1)
            npair = nkt // 2

            def pair_scores_pv(p2i):
                pp2 = p2i % 2
                h0 = 2 * p2i
                kvh = h0 // 4
                ktile, koff = kvh // 2, 64 * (kvh % 2)
                pw = PW0 if pp2 == 0 else PW1
                pwn = "PW0" if pp2 == 0 else "PW1"
                qtile, qoff = qplace(h0)
                qrhs = QTc[qoff:qoff + 64, qtile * ACH:(qtile + 2) * ACH]

                def emit_s(kt):
                    pp = PS[kt % 4]
                    pres = PN[kt % 4]
                    mm(pp[:],
                       K_sb[koff:koff + 64,
                            ktile * S + kt * 128: ktile * S + (kt + 1) * 128],
                       qrhs, True, True, reads=[("K",), ("QTc",)], writes=[pres])
                    esl = kt % 4
                    act(E_sl[esl][:], pp[:], AF.Exp, bias=-ESHIFT, scale=0.125,
                        reads=[pres], writes=[("E", esl)])
                    if kt >= nkt - 2:
                        di = 0 if kt == nkt - 2 else 1
                        vop("tensor_mul", E_sl[esl][:], E_sl[esl][:], m01d_b[di][:],
                            reads=[("E", esl), ("m01d", di)], writes=[("E", esl)])

                def emit_pv(kt):
                    esl = kt % 4
                    mm(pw[0:65, 0:512],
                       V_sb[:, kt * 520 + kvh * 65: kt * 520 + (kvh + 1) * 65],
                       E_sl[esl][:], kt == 0, kt == nkt - 1,
                       reads=[("V", kt), ("Vones",), ("E", esl)], writes=[(pwn, 0)])

                return emit_s, emit_pv

            def pair_norm_calc(p2i):
                hp = p2i % 2
                pw = PW0 if hp == 0 else PW1
                pwn = "PW0" if hp == 0 else "PW1"
                rsrc = rr if hp == 0 else rr2
                rscr = rows[0:1, 1792:2304]
                act(rscr, pw[64:65, 0:512], AF.Ln, reads=[(pwn, 0)], writes=[("rscr",)])
                act(rsrc[0:1, 0:512], rscr, AF.Exp, scale=-1.0,
                    reads=[("rscr",)], writes=[("rr", hp)])
                vop("tensor_copy", rows_b[0:1, hp * 512:hp * 512 + 512], rsrc[0:1, 0:512],
                    reads=[("rr", hp)], writes=[(("rr", hp), "hi")])

            def pair_norm_apply(p2i):
                hp = p2i % 2
                h0 = 2 * p2i
                qtile, qoff = qplace(h0)
                pw = PW0 if hp == 0 else PW1
                pwn = "PW0" if hp == 0 else "PW1"
                mm(pw[0:64, 512:1024], ones1[0:1, 0:64],
                   rows_b[0:1, hp * 512:hp * 512 + 512], True, True,
                   reads=[("ones1",), (("rr", hp), "hi")], writes=[(pwn, 1)])
                scopy(tmpO[hp][:], pw[0:64, 0:512], reads=[(pwn, 0)], writes=[("tmpO", hp)])
                vop("tensor_mul", OTc[qoff:qoff + 64, qtile * ACH:(qtile + 2) * ACH],
                    tmpO[hp][:], pw[0:64, 512:1024],
                    reads=[("tmpO", hp), (pwn, 1)], writes=[("OTc",)])

            LA = 2
            for p2i in range(16):
                emit_s, emit_pv = pair_scores_pv(p2i)
                for kt in range(min(LA + 1, nkt)):
                    emit_s(kt)
                if p2i > 0:
                    pair_norm_calc(p2i - 1)
                late = max(0, nkt - 2)
                for kt in range(nkt):
                    if kt + LA + 1 < nkt:
                        emit_s(kt + LA + 1)
                    if kt == late and p2i > 0:
                        pair_norm_apply(p2i - 1)
                    emit_pv(kt)
            pair_norm_calc(15)
            pair_norm_apply(15)

            for dt in range(16):
                cb = cbst[dt % 3]
                cbres = ("cbst", dt % 3)
                dma("sync", cb[:].rearrange("p (c o) -> p c o", c=16), wo_c[dt, :, :, :],
                    reads=[("c_wo_c",)], writes=[cbres], stream=(f"cb{dt % 3}", "slot"))
                pp = PS[dt % 4]
                pres = PN[dt % 4]
                for oc in range(16):
                    mm(pp[:, 0:ACH], cb[:, oc * 128:(oc + 1) * 128], OTc[:, oc * ACH:(oc + 1) * ACH],
                       oc == 0, oc == 15, reads=[cbres, ("OTc",)], writes=[pres])
                scopy(aoTc[:, dt * ACH:(dt + 1) * ACH], pp[:, 0:ACH], reads=[pres],
                      writes=[("aoTc",)])
                vop("tensor_copy", QTc[:, dt * ACH:(dt + 1) * ACH],
                    aoTc[:, dt * ACH:(dt + 1) * ACH], reads=[("aoTc",)], writes=[("QTc",)])
                act(sqb[:, dt * ACH:(dt + 1) * ACH], aoTc[:, dt * ACH:(dt + 1) * ACH],
                    AF.Square, reads=[("aoTc",)], writes=[("sqb",)])

            for dc in range(16):
                mm(P2[0:1, 0:ACH], onesc[:], QTc[:, dc * ACH:(dc + 1) * ACH], dc == 0, dc == 15,
                   reads=[("onesc",), ("QTc",)], writes=[("P2",)])
                mm(P3[0:1, 0:ACH], onesc[:], sqb[:, dc * ACH:(dc + 1) * ACH], dc == 0, dc == 15,
                   reads=[("onesc",), ("sqb",)], writes=[("P3",)])
            mrow = rows[0:1, 1024:1024 + ACH]
            vrow = rows[0:1, 1024 + ACH:1024 + 2 * ACH]
            trow = rows[0:1, 1024 + 2 * ACH:1024 + 3 * ACH]
            vop("tensor_scalar", mrow, P2[0:1, 0:ACH], 1.0 / D, None, ALU.mult,
                reads=[("P2",)], writes=[("mrow",)])
            vop("tensor_scalar", vrow, P3[0:1, 0:ACH], 1.0 / D, None, ALU.mult,
                reads=[("P3",)], writes=[("vrow",)])
            vop("tensor_tensor", trow, mrow, mrow, ALU.mult, reads=[("mrow",)], writes=[("trow",)])
            vop("tensor_sub", vrow, vrow, trow, reads=[("vrow",), ("trow",)], writes=[("vrow",)])
            act(vrow, vrow, AF.Ln, bias=EPS, reads=[("vrow",)], writes=[("vrow",)])
            act(vrow, vrow, AF.Exp, scale=-0.5, reads=[("vrow",)], writes=[("vrow",)])
            rows_hilo(mrow, ACH, rows_b[0:1, 2048:2048 + ACH],
                      rows_b[0:1, 2048 + ACH:2048 + 2 * ACH], trow, ("mrow",), ("ftB",))
            mm(P2[:, 0:ACH], ones1[:], rows_b[0:1, 2048:2048 + ACH], True, False,
               reads=[("ones1",), (("mrow",), "hi")], writes=[("P2",)])
            mm(P2[:, 0:ACH], ones1[:], rows_b[0:1, 2048 + ACH:2048 + 2 * ACH], False, True,
               reads=[("ones1",), (("mrow",), "lo")], writes=[("P2",)])
            rows_hilo(vrow, ACH, rows_b[0:1, 2560:2560 + ACH],
                      rows_b[0:1, 2560 + ACH:2560 + 2 * ACH], trow, ("vrow",), ("ftB",))
            mm(P3[:, 0:ACH], ones1[:], rows_b[0:1, 2560:2560 + ACH], True, False,
               reads=[("ones1",), (("vrow",), "hi")], writes=[("P3",)])
            mm(P3[:, 0:ACH], ones1[:], rows_b[0:1, 2560 + ACH:2560 + 2 * ACH], False, True,
               reads=[("ones1",), (("vrow",), "lo")], writes=[("P3",)])
            for dt in range(16):
                sl = dt % 2
                vop("tensor_sub", ntmp[sl][:], aoTc[:, dt * ACH:(dt + 1) * ACH], P2[:, 0:ACH],
                    reads=[("aoTc",), ("P2",)], writes=[("ntmp", sl)])
                vop("tensor_mul", aoTc[:, dt * ACH:(dt + 1) * ACH], ntmp[sl][:], P3[:, 0:ACH],
                    reads=[("ntmp", sl), ("P3",)], writes=[("aoTc",)])

        chunk_epilogue(NACH - 1)
        s.barrier()
        _close(a2)
        _close(spanKV)
        _ckpt("a2")

        ups = _mk()
        ustg = sb(ups, "ustg", [1, LD], BF16)
        b1ofs = np.cumsum([0] + NH).tolist()
        for e in range(NEXP):
            for hcx in range(NH[e]):
                slw = hcx % 4
                uq = "sync" if slw < 2 else "gpsimd"
                usn = f"w2{slw}" if slw < 2 else f"gw2{slw}"
                dma(uq, w2st[slw][:], w2_c[e][hcx * 128:(hcx + 1) * 128, :],
                    reads=[("c_" + w2_c[e].name,)], writes=[("w2st", slw)],
                    stream=(usn, "slot"))
                for half in range(2):
                    pp = PS[half]
                    mm(pp[0:1, 0:512], gb1[:, b1ofs[e] + hcx: b1ofs[e] + hcx + 1],
                       w2st[slw][:, half * 512:(half + 1) * 512], hcx == 0, hcx == NH[e] - 1,
                       reads=[("gb1",), ("w2st", slw)], writes=[PN[half]])
            scopy(urow[0:1, 0:512], P0[0:1, 0:512], reads=[("P0",)], writes=[("urow",)])
            scopy(urow[0:1, 512:1024], P1[0:1, 0:512], reads=[("P1",)], writes=[("urow",)])
            vop("tensor_add", crow[0:1, :], crow[0:1, :], urow[0:1, :],
                reads=[("crow",), ("urow",)], writes=[("crow",)])
            s.op("scalar",
                 (lambda o, i: lambda en: en.activation(o, i, AF.Identity, bias=0.0, scale=-1.0))(
                     ustg[0:1, :], urow[0:1, :]),
                 reads=[("urow",)], writes=[("ustg",)])
            dma("gpsimd", ustack_dram[e:e + 1, :], ustg[0:1, :], reads=[("ustg",)],
                writes=[("ustackd",)], stream=("usd", "slot"))
        dma("sync", urow[0:1, :], sb2row_d, writes=[("sb2r",), ("urow",)],
            stream=("sb2", "group"))
        vop("scalar_tensor_tensor", crow[0:1, :], urow[0:1, :], 0.1, crow[0:1, :],
            ALU.mult, ALU.add, reads=[("sb2r",), ("crow",)], writes=[("crow",)])
        scopy(ustg[0:1, :], crow[0:1, :], reads=[("crow",)], writes=[("ustg",)])
        dma("gpsimd", ustack_dram[8:9, :], ustg[0:1, :], reads=[("ustg",)], writes=[("ustackd",)],
            stream=("usd", "slot"))
        scopy(urow[0:1, :], ustg[0:1, :], reads=[("ustg",)], writes=[("uhf",), ("sb2r",)])
        vop("tensor_sub", urow[0:1, :], crow[0:1, :], urow[0:1, :],
            reads=[("crow",), ("uhf",)], writes=[("ulo",)])
        scopy(ustg[0:1, :], urow[0:1, :], reads=[("ulo",)], writes=[("ustg",)])
        dma("gpsimd", ustack_dram[9:10, :], ustg[0:1, :], reads=[("ustg",)], writes=[("ustackd",)],
            stream=("usd", "slot"))
        dma("sync", ustack[:], ustack_dram, reads=[("ustackd",)], writes=[("ustack",)],
            stream=("usl", "group"))
        s.barrier()
        _close(ups)
        _ckpt("ups")

        print('SBUF free at FFN(before allocs):', nc.sbuf_bytes_remaining)
        ffn = _mk()
        ln3Tc = sb(ffn, "ln3Tc", [128, 16 * FCH], BF16)
        g_sb = sb(ffn, "g_sb", [128, 8 * FCH], BF16)
        gxT = sb(ffn, "gxT", [128, 16 * FCH], BF16)
        sc32 = sb(ffn, "sc32", [128, 32 * FCH], BF16)
        acc_sb = sb(ffn, "acc_sb", [128, 4 * LD], F32)
        accT = sb(ffn, "accT", [128, 8 * FCH], BF16)
        faoT = sb(ffn, "faoT", [128, 16 * FCH], F32)
        fntmp = [sb(ffn, f"fntmp{i}", [128, FCH], F32) for i in range(2)]
        fx1 = sb(ffn, "fx1", [128, D], F32)
        frows = sb(ffn, "frows", [1, 3 * FCH], F32)
        frows_b = sb(ffn, "frows_b", [1, 4 * FCH], BF16)

        l3v = ln3Tc[:].rearrange("p (c t) -> p c t", c=16)
        p2map = [(PW0[:, 0:512], PW0[:, 512:1024]), (PW1[:, 0:512], PW1[:, 512:1024]),
                 (P0[:], P1[:]), (P2[:], P3[:])]
        p2res = [[("PW0", 0), ("PW0", 1)], [("PW1", 0), ("PW1", 1)],
                 [("P0",), ("P1",)], [("P2",), ("P3",)]]
        p2full = [PW0[:], PW1[:], None, None]

        def ffn_final(c):
            for tb in range(4):
                tbg = c * 4 + tb
                dma("sync", fx1[:], x1_dram[tbg * 128:(tbg + 1) * 128, :],
                    reads=[("x1d",)], writes=[("fx1",)], stream=("fx1", "slot"))
                for half in range(2):
                    pw = PW0 if half == 0 else PW1
                    pwn = "PW0" if half == 0 else "PW1"
                    for j in range(8):
                        dt = half * 8 + j
                        tr(pw[:, j * 128:(j + 1) * 128],
                           faoT[:, dt * FCH + tb * 128: dt * FCH + tb * 128 + 128], ident_f[:],
                           reads=[("faoT",), ("identf",)], writes=[(pwn, j // 4)])
                    vop("scalar_tensor_tensor", fx1[:, half * 1024:(half + 1) * 1024],
                        pw[:], 1.0, fx1[:, half * 1024:(half + 1) * 1024], ALU.mult, ALU.add,
                        reads=[(pwn, 0), (pwn, 1), ("fx1",)], writes=[("fx1",)])
                dma("sync", y_d[tbg * 128:(tbg + 1) * 128, :], fx1[:],
                    reads=[("fx1",)], writes=[("y",)], stream=("y", "group"))


        for c in range(NFCH):
            t0 = c * FCH
            dma("gpsimd", l3v, ln3T_dram[:, :, t0:t0 + FCH].transpose([1, 0, 2]),
                reads=[("ln3Td", 0), ("ln3Td", 1)], writes=[("ln3Tc",)], stream=("gl3c", "slot"))
            for lt in range(8):
                cb = cbst[lt % 2]
                cbres = ("cbst", lt % 2)
                dma("gpsimd", cb[:].rearrange("p (c o) -> p c o", c=16), down_c[lt, :, :, :],
                    reads=[("c_down_c",)], writes=[cbres], stream=(f"gdn{lt % 2}", "slot"))
                pp = PS[lt % 4]
                pres = PN[lt % 4]
                for dc in range(16):
                    mm(pp[:], cb[:, dc * 128:(dc + 1) * 128], l3v[:, dc, :], dc == 0, dc == 15,
                       reads=[cbres, ("ln3Tc",)], writes=[pres])
                act(g_sb[:, lt * FCH:(lt + 1) * FCH], pp[:], AF.Gelu, bias=downb[:, lt:lt + 1],
                    reads=[pres, ("downb",)], writes=[("g", lt)])
            for dc in range(16):
                act(gxT[:, dc * FCH:(dc + 1) * FCH], l3v[:, dc, :], AF.Gelu,
                    reads=[("ln3Tc",)], writes=[("gxT",)])
            if c > 0:
                ffn_final(c - 1)
            for tsub in range(4):
                tbg = c * 4 + tsub
                for half in range(2):
                    mm(p2map[tsub][half], wT_sb[:, tbg * 128:(tbg + 1) * 128],
                       ustack[:, half * 512:(half + 1) * 512], True, True,
                       reads=[("wT",), ("wTones",), ("ustack",)],
                       writes=[p2res[tsub][half]])
                if p2full[tsub] is not None:
                    vop("tensor_copy", acc_sb[:, tsub * LD:(tsub + 1) * LD], p2full[tsub],
                        reads=p2res[tsub], writes=[("acc", tsub)])
                else:
                    vop("tensor_copy", acc_sb[:, tsub * LD:tsub * LD + 512], p2map[tsub][0],
                        reads=[p2res[tsub][0]], writes=[("acc", tsub)])
                    vop("tensor_copy", acc_sb[:, tsub * LD + 512:(tsub + 1) * LD], p2map[tsub][1],
                        reads=[p2res[tsub][1]], writes=[("acc", tsub)])

            def expert(nh, w1c4, w2cp, b1c0, gate_col, gate_const):
                for hcx in range(nh):
                    cb = cbst[hcx % 2]
                    cbres = ("cbst", hcx % 2)
                    dma("sync", cb[:, 0:8 * 128].rearrange("p (c o) -> p c o", c=8),
                        w1c4[hcx, :, :, :], reads=[("c_" + w1c4.name,)], writes=[cbres],
                        stream=(f"cb{hcx % 2}", "slot"))
                    pp = PS[hcx % 2]
                    pres = PN[hcx % 2]
                    for lt in range(8):
                        mm(pp[:], cb[:, lt * 128:(lt + 1) * 128], g_sb[:, lt * FCH:(lt + 1) * FCH],
                           lt == 0, lt == 7, reads=[cbres, ("g", lt)], writes=[pres])
                    act(sc32[:, hcx * FCH:(hcx + 1) * FCH], pp[:], AF.Gelu,
                        bias=b1all[:, b1c0 + hcx:b1c0 + hcx + 1],
                        reads=[pres, ("b1all",)], writes=[("sc32", hcx)])
                for hcx in range(nh):
                    slw = hcx % 4
                    wq2 = "sync" if slw < 2 else "gpsimd"
                    wsn = f"w2{slw}" if slw < 2 else f"gw2{slw}"
                    dma(wq2, w2st[slw][:], w2cp[hcx * 128:(hcx + 1) * 128, :],
                        reads=[("c_" + w2cp.name,)], writes=[("w2st", slw)],
                        stream=(wsn, "slot"))
                    for tsub in range(4):
                        for half in range(2):
                            mm(p2map[tsub][half],
                               sc32[:, hcx * FCH + tsub * 128: hcx * FCH + tsub * 128 + 128],
                               w2st[slw][:, half * 512:(half + 1) * 512],
                               hcx == 0, hcx == nh - 1,
                               reads=[("sc32", hcx), ("w2st", slw)],
                               writes=[p2res[tsub][half]])
                for tsub in (2, 3, 0, 1):
                    tbg = c * 4 + tsub
                    gc = (w_tm[:, tbg * 9 + gate_col: tbg * 9 + gate_col + 1]
                          if gate_col is not None else gate_const)
                    grd = [("w_tm",)] if gate_col is not None else []
                    if p2full[tsub] is not None:
                        vop("scalar_tensor_tensor", acc_sb[:, tsub * LD:(tsub + 1) * LD],
                            p2full[tsub], gc, acc_sb[:, tsub * LD:(tsub + 1) * LD],
                            ALU.mult, ALU.add,
                            reads=p2res[tsub] + grd + [("acc", tsub)], writes=[("acc", tsub)])
                    else:
                        for half in range(2):
                            vop("scalar_tensor_tensor",
                                acc_sb[:, tsub * LD + half * 512: tsub * LD + (half + 1) * 512],
                                p2map[tsub][half], gc,
                                acc_sb[:, tsub * LD + half * 512: tsub * LD + (half + 1) * 512],
                                ALU.mult, ALU.add,
                                reads=[p2res[tsub][half]] + grd + [("acc", tsub)],
                                writes=[("acc", tsub)])

            for e in range(NEXP):
                expert(NH[e], w1_c[e], w2_c[e], b1ofs[e], e, None)
            expert(8, sw1_c, sw2_c, b1ofs[NEXP], None, 0.1)

            for lt in range(8):
                pp = PS[2 + lt % 2]
                pres = PN[2 + lt % 2]
                for tsub in range(4):
                    tr(pp[:, tsub * 128:(tsub + 1) * 128],
                       acc_sb[:, tsub * LD + lt * 128: tsub * LD + lt * 128 + 128], ident_f[:],
                       reads=[("acc", tsub), ("identf",)], writes=[pres])
                scopy(accT[:, lt * FCH:(lt + 1) * FCH], pp[:], reads=[pres], writes=[("accT",)])

            for dt in range(16):
                cbu = cbst[dt % 2]
                cbures = ("cbst", dt % 2)
                dma("gpsimd", cbu[:, 0:8 * 128].rearrange("p (c o) -> p c o", c=8),
                    up_c[dt, :, :, :], reads=[("c_up_c",)], writes=[cbures],
                    stream=(f"gup{dt % 2}", "slot"))
                cbc = cbst[2]
                cbcres = ("cbst", 2)
                dma("gpsimd", cbc[:].rearrange("p (c o) -> p c o", c=16), core_c[dt, :, :, :],
                    reads=[("c_core_c",)], writes=[cbcres], stream=("gcb2", "slot"))
                pp = PS[dt % 4]
                pres = PN[dt % 4]
                for lt in range(8):
                    mm(pp[:], cbu[:, lt * 128:(lt + 1) * 128], accT[:, lt * FCH:(lt + 1) * FCH],
                       lt == 0, False, reads=[cbures, ("accT",)], writes=[pres])
                for dc in range(16):
                    mm(pp[:], cbc[:, dc * 128:(dc + 1) * 128], gxT[:, dc * FCH:(dc + 1) * FCH],
                       False, dc == 15, reads=[cbcres, ("gxT",)], writes=[pres])
                s.op("scalar",
                     (lambda o, i, b: lambda en: en.activation(o, i, AF.Identity, bias=b, scale=1.0))(
                         faoT[:, dt * FCH:(dt + 1) * FCH], pp[:], bcol[:, dt:dt + 1]),
                     reads=[pres, ("bcol",)], writes=[("faoT",)])
                act(sc32[:, (16 + dt) * FCH:(17 + dt) * FCH],
                    faoT[:, dt * FCH:(dt + 1) * FCH], AF.Square,
                    reads=[("faoT",)], writes=[("sc32", 16 + dt)])
                vop("tensor_copy", sc32[:, dt * FCH:(dt + 1) * FCH],
                    faoT[:, dt * FCH:(dt + 1) * FCH], reads=[("faoT",)], writes=[("sc32", dt)])

            for dc in range(16):
                mm(P2[0:1, 0:FCH], onesc[:], sc32[:, dc * FCH:(dc + 1) * FCH], dc == 0, dc == 15,
                   reads=[("onesc",), ("sc32", dc)], writes=[("P2",)])
                mm(P3[0:1, 0:FCH], onesc[:], sc32[:, (16 + dc) * FCH:(17 + dc) * FCH],
                   dc == 0, dc == 15, reads=[("onesc",), ("sc32", 16 + dc)], writes=[("P3",)])
            fm = frows[0:1, 0:FCH]
            fv = frows[0:1, FCH:2 * FCH]
            ft = frows[0:1, 2 * FCH:3 * FCH]
            vop("tensor_scalar", fm, P2[0:1, 0:FCH], 1.0 / D, None, ALU.mult,
                reads=[("P2",)], writes=[("fm",)])
            vop("tensor_scalar", fv, P3[0:1, 0:FCH], 1.0 / D, None, ALU.mult,
                reads=[("P3",)], writes=[("fv",)])
            vop("tensor_tensor", ft, fm, fm, ALU.mult, reads=[("fm",)], writes=[("ft",)])
            vop("tensor_sub", fv, fv, ft, reads=[("fv",), ("ft",)], writes=[("fv",)])
            act(fv, fv, AF.Ln, bias=EPS, reads=[("fv",)], writes=[("fv",)])
            act(fv, fv, AF.Exp, scale=-0.5, reads=[("fv",)], writes=[("fv",)])
            rows_hilo(fm, FCH, frows_b[0:1, 0:FCH], frows_b[0:1, FCH:2 * FCH], ft, ("fm",), ("ftF",))
            mm(P2[:, 0:FCH], ones1[:], frows_b[0:1, 0:FCH], True, False,
               reads=[("ones1",), (("fm",), "hi")], writes=[("P2",)])
            mm(P2[:, 0:FCH], ones1[:], frows_b[0:1, FCH:2 * FCH], False, True,
               reads=[("ones1",), (("fm",), "lo")], writes=[("P2",)])
            rows_hilo(fv, FCH, frows_b[0:1, 2 * FCH:3 * FCH], frows_b[0:1, 3 * FCH:4 * FCH],
                      ft, ("fv",), ("ftF",))
            mm(P3[:, 0:FCH], ones1[:], frows_b[0:1, 2 * FCH:3 * FCH], True, False,
               reads=[("ones1",), (("fv",), "hi")], writes=[("P3",)])
            mm(P3[:, 0:FCH], ones1[:], frows_b[0:1, 3 * FCH:4 * FCH], False, True,
               reads=[("ones1",), (("fv",), "lo")], writes=[("P3",)])
            for dt in range(16):
                sl = dt % 2
                vop("tensor_sub", fntmp[sl][:], faoT[:, dt * FCH:(dt + 1) * FCH], P2[:, 0:FCH],
                    reads=[("faoT",), ("P2",)], writes=[("fntmp", sl)])
                vop("tensor_mul", faoT[:, dt * FCH:(dt + 1) * FCH], fntmp[sl][:], P3[:, 0:FCH],
                    reads=[("fntmp", sl), ("P3",)], writes=[("faoT",)])

        ffn_final(NFCH - 1)
        _close(ffn)
    except _StopBuild:
        while _open:
            _open.pop().close()

    with nc.Block() as block:
        s.emit(block, nc, top)
    top.close()
    return nc


_NC_CACHE = []


def kernel(**inputs):
    ii = {k: np.asarray(v) for k, v in inputs.items()}
    hcs = host_constants()
    x = np.ascontiguousarray(ii["L0_attn_x"], dtype=np.float32)

    inm = {}
    qp = q_perm()
    inm["wqT"] = np.ascontiguousarray(np.asarray(ii["L0_attn_Wq"], np.float32).T[:, qp])
    inm["wkT"] = np.ascontiguousarray(ii["L0_attn_Wk"].T)
    inm["wvT"] = np.ascontiguousarray(ii["L0_attn_Wv"].T)
    inm["woT"] = np.ascontiguousarray(np.asarray(ii["L0_attn_Wo"], np.float32).T[qp, :])
    inm["down_w"] = np.ascontiguousarray(ii["L0_ffn_down_w"])
    inm["up_w"] = np.ascontiguousarray(ii["L0_ffn_up_w"])
    inm["core_w"] = np.ascontiguousarray(ii["L0_ffn_core_w"])
    inm["sw1"] = np.ascontiguousarray(ii["L0_ffn_shared_w1"])
    inm["sw2"] = np.ascontiguousarray(ii["L0_ffn_shared_w2"])
    for e in range(NEXP):
        inm[f"w1_{e}"] = np.ascontiguousarray(ii[f"L0_ffn_e{e}_w1"])
        inm[f"w2_{e}"] = np.ascontiguousarray(ii[f"L0_ffn_e{e}_w2"])
    inm["down_b_col"] = col_layout(ii["L0_ffn_down_b"])
    inm["up_b_col"] = col_layout(ii["L0_ffn_up_b"])
    inm["core_b_col"] = col_layout(ii["L0_ffn_core_b"])
    b1cols = [col_layout(ii[f"L0_ffn_e{e}_b1"]) for e in range(NEXP)]
    b1cols.append(col_layout(ii["L0_ffn_shared_b1"]))
    inm["b1all_col"] = np.ascontiguousarray(np.concatenate(b1cols, axis=1))
    inm["b2_cat_row"] = np.ascontiguousarray(
        np.concatenate([np.asarray(ii[f"L0_ffn_e{e}_b2"], np.float32)
                        for e in range(NEXP)]).reshape(1, -1))
    inm["sb2_row"] = np.ascontiguousarray(
        np.asarray(ii["L0_ffn_shared_b2"], np.float32).reshape(1, -1))
    inm["rb_row"] = np.ascontiguousarray(
        np.asarray(ii["L0_ffn_router_b"], np.float32).reshape(1, -1))
    inm["lamcost"] = hcs["lamcost"]
    rw = np.asarray(ii["L0_ffn_router_w"], np.float32)
    inm["router_w_t"] = np.ascontiguousarray(
        rw.reshape(16, 128, NEXP).transpose(1, 0, 2).reshape(128, 16 * NEXP))
    inm["cosT"] = hcs["cosT"]
    inm["sinT"] = hcs["sinT"]
    inm["m01"] = hcs["m01"]
    inm["m01p"] = hcs["m01p"]
    inm["m01d384"] = hcs["m01d384"]
    inm["m01d256"] = hcs["m01d256"]
    inm["ident"] = hcs["ident"]
    inm["permswap"] = hcs["permswap"]
    inm["constpool"] = np.ascontiguousarray(
        np.tile(np.array(CONST_VALS, np.float32)[None, :], (128, 1)))

    if not _NC_CACHE:
        _NC_CACHE.append(build_program())
    nc = _NC_CACHE[0]
    in_maps = []
    for core in range(8):
        m = dict(inm)
        m["x"] = np.ascontiguousarray(x[core])
        in_maps.append(m)
    res = run_bass_kernel_spmd(nc, in_maps, core_ids=list(range(8)))
    out = np.stack([np.asarray(res.results[i]["y"]) for i in range(8)], axis=0)
    return out.astype(np.float32)


# version 26
# speedup vs baseline: 1.0975x; 1.0844x over previous
import contextlib
import numpy as np
import concourse.bass as bass
import concourse.mybir as mybir
from concourse.bass_utils import run_bass_kernel_spmd

F32 = mybir.dt.float32
BF16 = mybir.dt.bfloat16
AF = mybir.ActivationFunctionType
ALU = mybir.AluOpType
AX = mybir.AxisListType

S = 2048
D = 2048
NQ, NKV, HDIM = 32, 8, 64
LD = 1024
HID = [1024, 2048, 3072, 4096, 1024, 2048, 3072, 4096]
NH = [h // 128 for h in HID]
NEXP = 8
EPS = 1e-5
ESHIFT = 24.0
ACH = 256
NACH = S // ACH
FCH = 512
NFCH = S // FCH
COST_LAMBDA = 1e-7
UPTO = "all"
NCORES = 8


class _StopBuild(Exception):
    pass

ENGINES = ("tensor", "vector", "scalar", "sync", "gpsimd")
COMPUTE = ("tensor", "vector", "scalar")
QUEUES = ("sync", "gpsimd")
PSUM_RES = {("P0",), ("P1",), ("P2",), ("P3",), ("PW0",), ("PW1",)}

CONST_VALS = [0.0, 1.0, -1.0, 0.125, -ESHIFT, EPS, 1.0 / D, -0.5, 0.1, -1000.0]


class Sched:

    def __init__(self):
        self.barrier_fns = {}
        self.ops = {e: [] for e in ENGINES}
        self.counts = {e: 0 for e in ENGINES}
        self.know = {e: {} for e in ENGINES}
        self.snap = {e: [] for e in ENGINES}
        self.res = {}
        self.pending = {e: {} for e in ENGINES}
        self.wait_targets = {e: set() for e in COMPUTE}
        self.streams = {}
        self.meta = {}

    def _stream_add(self, eng, idx, sid, kind):
        st = self.streams.setdefault(sid, {"kind": kind, "ops": [], "queue": eng})
        assert st["kind"] == kind, sid
        if kind == "group":
            assert st["queue"] == eng, (sid, "group streams must be single-queue")
        k = len(st["ops"])
        st["ops"].append((eng, idx))
        self.meta[(eng, idx)] = (sid, k)

    def _need(self, eng, waits, te, ti, raw=False):
        if te in COMPUTE:
            if te == eng and eng == "tensor":
                return
            if self.know[eng].get(te, 0) >= ti + 1:
                return
            waits[te] = max(waits.get(te, 0), ti + 1)
        else:
            sid, k = self.meta[(te, ti)]
            if self.know[eng].get(sid, 0) >= k + 1:
                return
            waits[sid] = max(waits.get(sid, 0), k + 1)

    def _learn(self, eng, waits):
        for key, c in list(waits.items()):
            if self.know[eng].get(key, 0) >= c:
                del waits[key]
                continue
            self.know[eng][key] = c
            if key in COMPUTE:
                st = self.snap[key][c - 1]
            else:
                we, wi = self.streams[key]["ops"][c - 1]
                st = self.snap[we][wi]
            for f, v in st.items():
                if f != eng and v > self.know[eng].get(f, 0):
                    self.know[eng][f] = v
            if key in COMPUTE:
                self.wait_targets[key].add(c - 1)

    def op(self, eng, fn, reads=(), writes=(), stream=None):
        waits = dict(self.pending[eng])
        exact = set(waits.keys())
        self.pending[eng] = {}
        for r in reads:
            rec = self.res.get(r)
            if rec and rec["w"] is not None:
                self._need(eng, waits, *rec["w"], raw=True)
            if rec and r in PSUM_RES and eng in ("scalar", "vector"):
                for (re_, ri) in rec["r"]:
                    if re_ != eng and re_ in ("scalar", "vector"):
                        self._need(eng, waits, re_, ri)
        for r in writes:
            rec = self.res.get(r)
            if rec:
                if rec["w"] is not None:
                    self._need(eng, waits, *rec["w"])
                for (re_, ri) in rec["r"]:
                    self._need(eng, waits, re_, ri)
        exact &= set(waits.keys())
        self._learn(eng, waits)
        exact &= set(waits.keys())
        idx = self.counts[eng]
        self.ops[eng].append((fn, waits, exact))
        self.counts[eng] = idx + 1
        if eng in QUEUES:
            assert stream is not None, "queue ops need a stream"
            self._stream_add(eng, idx, stream[0], stream[1])
            st = dict(self.know[eng])
            sid, k = self.meta[(eng, idx)]
            if self.streams[sid]["kind"] == "slot":
                st[sid] = max(st.get(sid, 0), k + 1)
        else:
            st = dict(self.know[eng])
            st[eng] = idx + 1
        self.snap[eng].append(st)
        for r in reads:
            rec = self.res.setdefault(r, {"w": None, "r": []})
            rec["r"].append((eng, idx))
        for r in writes:
            self.res[r] = {"w": (eng, idx), "r": []}
        return (eng, idx)

    def barrier(self, engines=("tensor", "vector", "scalar", "sync")):
        ccur = {e: self.counts[e] for e in COMPUTE if self.counts[e] > 0}
        scur = {sid: len(st["ops"]) for sid, st in self.streams.items()
                if st["ops"] and st["queue"] in engines}
        for e in engines:
            if e in QUEUES:
                continue
            waits = {}
            for f, c in ccur.items():
                if f != e and self.know[e].get(f, 0) < c:
                    waits[f] = c
            for sid, c in scur.items():
                if self.know[e].get(sid, 0) < c:
                    waits[sid] = c
            exact = set(k for k in waits if k not in COMPUTE)
            self._learn(e, waits)
            exact &= set(waits.keys())
            idx = self.counts[e]
            bfn = self.barrier_fns.get(e, lambda en: en.nop())
            self.ops[e].append((bfn, waits, exact))
            self.counts[e] = idx + 1
            st = dict(self.know[e])
            st[e] = idx + 1
            self.snap[e].append(st)
        post = {e: self.counts[e] for e in COMPUTE if self.counts[e] > 0}
        for e in QUEUES:
            for f, c in post.items():
                self.pending[e][f] = max(self.pending[e].get(f, 0), c)

    def mark_final(self, eng_idx):
        pass

    def emit(self, block, nc, top):
        sems = {}
        for e in COMPUTE:
            sems[e] = top.enter_context(nc.semaphore(f"sem_{e}"))
        for sid in self.streams:
            sems[sid] = top.enter_context(nc.semaphore(f"sem_s_{sid}"))
        incmap = {}
        for e in COMPUTE:
            m = {}
            for j, oi in enumerate(sorted(self.wait_targets[e])):
                m[oi] = j + 1
            incmap[e] = m
        totals = {sid: len(st["ops"]) for sid, st in self.streams.items()}

        def replay(e, eng):
            for i, (fn, waits, exact) in enumerate(self.ops[e]):
                for key in sorted(waits, key=str):
                    c = waits[key]
                    if key in COMPUTE:
                        eng.wait_ge(sems[key], incmap[key][c - 1])
                    else:
                        st = self.streams[key]
                        if st["kind"] == "slot" or key in exact:
                            eng.wait_ge(sems[key], 16 * c)
                        elif st["queue"] == e:
                            m = sum(1 for (se, si) in st["ops"] if si < i)
                            assert m >= c, (key, c, m)
                            eng.wait_ge(sems[key], 16 * m)
                        else:
                            eng.wait_ge(sems[key], 16 * totals[key])
                inst = fn(eng)
                if e in COMPUTE:
                    if i in incmap[e]:
                        inst.then_inc(sems[e], 1)
                else:
                    sid, k = self.meta[(e, i)]
                    inst.then_inc(sems[sid], 16)
            if e in QUEUES:
                done = set()
                for i in range(len(self.ops[e])):
                    sid, k = self.meta[(e, i)]
                    done.add(sid)
                for sid in sorted(done, key=str):
                    eng.wait_ge(sems[sid], 16 * totals[sid])

        @block.tensor
        def _(eng):
            replay("tensor", eng)

        @block.vector
        def _(eng):
            replay("vector", eng)

        @block.scalar
        def _(eng):
            replay("scalar", eng)

        @block.sync
        def _(eng):
            replay("sync", eng)

        @block.gpsimd
        def _(eng):
            replay("gpsimd", eng)


def host_constants():
    c = {}
    k = np.arange(0, HDIM, 2)[: HDIM // 2].astype(np.float64)
    inv_freq = 1.0 / (10000.0 ** (k / HDIM))
    pos = np.arange(S, dtype=np.float64)
    ang = pos[:, None] * inv_freq[None, :]
    ang2 = np.concatenate([ang, ang], axis=1)
    cos64 = np.cos(ang2).T
    sin64 = np.sin(ang2).T
    sgn = np.ones((64, 1))
    sgn[:32] = -1.0
    sins = sin64 * sgn
    c["cosT"] = np.ascontiguousarray(np.concatenate([cos64, cos64], 0).astype(np.float32))
    c["sinT"] = np.ascontiguousarray(np.concatenate([sins, sins], 0).astype(np.float32))
    p = np.arange(128)[:, None]
    j = np.arange(896)[None, :] - 384
    m01 = (j >= p).astype(np.float32)
    c["m01"] = np.ascontiguousarray(m01)
    c["m01p"] = np.ascontiguousarray(
        np.concatenate([m01[:, 384:640], m01[:, 256:512]], axis=1))
    c["m01d384"] = np.ascontiguousarray(np.tile(m01[:, 384:640], (1, 2)))
    c["m01d256"] = np.ascontiguousarray(np.tile(m01[:, 256:512], (1, 2)))
    c["ident"] = np.eye(128, dtype=np.float32)
    pm = np.zeros((128, 128), dtype=np.float32)
    idx = np.arange(128)
    pm[idx ^ 32, idx] = 1.0
    c["permswap"] = pm
    c["lamcost"] = (np.float32(COST_LAMBDA) * np.array(
        [np.float32(LD * h + h * LD) for h in HID], dtype=np.float32)).reshape(1, 8)
    return c


def col_layout(b):
    return np.ascontiguousarray(np.asarray(b, np.float32).reshape(-1, 128).T)


def qplace(h):
    g = h // 4
    return (g // 2) * 4 + (h % 4), 64 * (g % 2)


def q_perm():
    pos = np.empty(D, dtype=np.int64)
    for h in range(NQ):
        t, off = qplace(h)
        pos[t * 128 + off: t * 128 + off + 64] = np.arange(h * 64, (h + 1) * 64)
    return pos


def build_program():
    nc = bass.Bass("TRN2", target_bir_lowering=False)
    s = Sched()

    def _ckpt(name):
        if UPTO == name:
            raise _StopBuild()
    top = contextlib.ExitStack()

    def sb(stk, name, shape, dtype):
        return stk.enter_context(nc.sbuf_tensor(name, shape, dtype))

    def dma(q, out, in_, reads=(), writes=(), stream=None):
        return s.op(q, (lambda o, i: lambda e: e.dma_start(out=o, in_=i))(out, in_),
                    reads=reads, writes=writes, stream=stream)

    def mm(out, lhsT, rhs, start, stop, reads=(), writes=()):
        return s.op("tensor",
                    (lambda o, l, r, st, sp: lambda e: e.matmul(o, l, r, start=st, stop=sp))(
                        out, lhsT, rhs, start, stop), reads=reads, writes=writes)

    def tr(out, in_, ident, reads=(), writes=()):
        return s.op("tensor", (lambda o, i, d: lambda e: e.transpose(o, i, d))(out, in_, ident),
                    reads=reads, writes=writes)

    def act(out, in_, func, bias=0.0, scale=1.0, reads=(), writes=()):
        return s.op("scalar",
                    (lambda o, i, f, b, sc: lambda e: e.activation(o, i, f, bias=b, scale=sc))(
                        out, in_, func, bias, scale), reads=reads, writes=writes)

    def scopy(out, in_, reads=(), writes=()):
        return s.op("scalar", (lambda o, i: lambda e: e.copy(o, i))(out, in_),
                    reads=reads, writes=writes)

    def vop(name, *args, reads=(), writes=()):
        return s.op("vector", (lambda n, a: lambda e: getattr(e, n)(*a))(name, args),
                    reads=reads, writes=writes)

    def memset(ap, val, writes=()):
        return s.op("vector", (lambda a, v: lambda e: e.memset(a, v))(ap, val), writes=writes)

    def din(name, shape):
        return nc.dram_tensor(name, list(shape), F32, kind="ExternalInput").ap()

    def dint(name, shape, dt):
        return nc.dram_tensor(name, list(shape), dt, kind="Internal").ap()

    x_d = din("x", (S, D))
    y_d = nc.dram_tensor("y", [S, D], F32, kind="ExternalOutput").ap()
    wqT_d = din("wqT", (D, D))
    wkT_d = din("wkT", (D, NKV * HDIM))
    wvT_d = din("wvT", (D, NKV * HDIM))
    woT_d = din("woT", (D, D))
    down_d = din("down_w", (D, LD))
    up_d = din("up_w", (LD, D))
    core_d = din("core_w", (D, D))
    sw1_d = din("sw1", (LD, LD))
    sw2_d = din("sw2", (LD, LD))
    w1_d = [din(f"w1_{e}", (LD, HID[e])) for e in range(NEXP)]
    w2_d = [din(f"w2_{e}", (HID[e], LD)) for e in range(NEXP)]
    downb_d = din("down_b_col", (128, LD // 128))
    upb_d = din("up_b_col", (128, D // 128))
    coreb_d = din("core_b_col", (128, D // 128))
    b1all_d = din("b1all_col", (128, sum(NH) + 8))
    b2row_d = din("b2_cat_row", (1, NEXP * LD))
    sb2row_d = din("sb2_row", (1, LD))
    rbrow_d = din("rb_row", (1, NEXP))
    lam_d = din("lamcost", (1, NEXP))
    rwt_d = din("router_w_t", (128, 16 * NEXP))
    cosT_d = din("cosT", (128, S))
    sinT_d = din("sinT", (128, S))
    m01_d = din("m01", (128, 896))
    m01p_d = din("m01p", (128, 512))
    m01d384_d = din("m01d384", (128, 512))
    m01d256_d = din("m01d256", (128, 512))
    ident_d = din("ident", (128, 128))
    perm_d = din("permswap", (128, 128))
    consts_d = din("constpool", (128, len(CONST_VALS)))

    x1_dram = dint("x1_scr", (S, D), F32)
    ln1T_dram = dint("ln1T_scr", (16, 128, S), BF16)
    ln3T_dram = dint("ln3T_scr", (16, 128, S), BF16)
    wq_c = dint("wq_c", (16, 128, 16, 128), BF16)
    wo_c = dint("wo_c", (16, 128, 16, 128), BF16)
    down_c = dint("down_c", (8, 128, 16, 128), BF16)
    up_c = dint("up_c", (16, 128, 8, 128), BF16)
    core_c = dint("core_c", (16, 128, 16, 128), BF16)
    sw1_c = dint("sw1_c", (8, 128, 8, 128), BF16)
    w1_c = [dint(f"w1c_{e}", (NH[e], 128, 8, 128), BF16) for e in range(NEXP)]
    sw2_c = dint("sw2_c", (LD, LD), BF16)
    w2_c = [dint(f"w2c_{e}", (HID[e], LD), BF16) for e in range(NEXP)]
    ustack_dram = dint("ustack_scr", (10, LD), BF16)

    cpool = sb(top, "cpool", [128, len(CONST_VALS)], F32)
    for i, v in enumerate(CONST_VALS):
        nc.const_aps.register(F32, float(v), cpool[:, i:i + 1])
    ident_b = sb(top, "ident_b", [128, 128], BF16)
    ident_f = sb(top, "ident_f", [128, 128], F32)
    perm_b = sb(top, "perm_b", [128, 128], BF16)
    ones1 = sb(top, "ones1", [1, 128], BF16)
    onesc = sb(top, "onesc", [128, 1], BF16)
    m01_b = sb(top, "m01_b", [128, 896], BF16)
    m01p_b = sb(top, "m01p_b", [128, 512], BF16)
    m01d_b = [sb(top, f"m01d{i}_b", [128, 512], BF16) for i in range(2)]
    cvt = [sb(top, f"cvt{i}", [128, 2048], BF16) for i in range(2)]
    cbst = [sb(top, f"cbst{i}", [128, 16 * 128], BF16) for i in range(3)]
    w2st = [sb(top, f"w2st{i}", [128, LD], BF16) for i in range(4)]
    downb = sb(top, "downb", [128, LD // 128], F32)
    bcol = sb(top, "bcol", [128, D // 128], F32)
    colt = sb(top, "colt", [128, 16], F32)
    b1all = sb(top, "b1all", [128, sum(NH) + 8], F32)
    gb1 = sb(top, "gb1", [128, sum(NH) + 8], BF16)
    rw_sb = sb(top, "rw_sb", [128, 16 * NEXP], F32)
    rw_hi = sb(top, "rw_hi", [128, 16 * NEXP], BF16)
    rw_lo = sb(top, "rw_lo", [128, 16 * NEXP], BF16)
    rbl = sb(top, "rbl", [1, 4 * NEXP], F32)
    rbl_hi = sb(top, "rbl_hi", [1, NEXP], BF16)
    rbl_lo = sb(top, "rbl_lo", [1, NEXP], BF16)
    w_tm = sb(top, "w_tm", [128, 16 * 9], F32)
    wT_sb = sb(top, "wT_sb", [10, S], BF16)
    ustack = sb(top, "ustack", [10, LD], BF16)
    crow = sb(top, "crow", [1, LD], F32)
    urow = sb(top, "urow", [1, LD], F32)

    barc = sb(top, "barc", [1, 2], F32)
    s.barrier_fns = {
        "vector": (lambda en: en.memset(barc[0:1, 0:1], 0.0)),
        "scalar": (lambda en: en.memzero(barc[0:1, 1:2])),
    }
    P0 = top.enter_context(nc.psum_tensor("P0", [128, 512], F32))
    P1 = top.enter_context(nc.psum_tensor("P1", [128, 512], F32))
    P2 = top.enter_context(nc.psum_tensor("P2", [128, 512], F32))
    P3 = top.enter_context(nc.psum_tensor("P3", [128, 512], F32))
    PW0 = top.enter_context(nc.psum_tensor("PW0", [128, 1024], F32))
    PW1 = top.enter_context(nc.psum_tensor("PW1", [128, 1024], F32))
    PS = [P0, P1, P2, P3]
    PN = [("P0",), ("P1",), ("P2",), ("P3",)]

    print('SBUF free after static:', nc.sbuf_bytes_remaining)
    _open = []

    def _mk():
        st = contextlib.ExitStack()
        _open.append(st)
        return st

    def _close(st):
        assert _open and _open[-1] is st
        _open.pop()
        st.close()

    try:
        init = _mk()
        rowtmp = sb(init, "rowtmp", [1, NEXP * LD], F32)
        dma("gpsimd", ident_b[:], ident_d, writes=[("identb",)], stream=("gi", "group"))
        dma("sync", ident_f[:], ident_d, writes=[("identf",)], stream=("si", "group"))
        dma("gpsimd", perm_b[:], perm_d, writes=[("permb",)], stream=("gi", "group"))
        dma("gpsimd", m01_b[:], m01_d, writes=[("m01",)], stream=("gi", "group"))
        dma("gpsimd", m01p_b[:], m01p_d, writes=[("m01p",)], stream=("gi", "group"))
        dma("gpsimd", m01d_b[0][:], m01d384_d, writes=[("m01d", 0)], stream=("gi", "group"))
        dma("gpsimd", m01d_b[1][:], m01d256_d, writes=[("m01d", 1)], stream=("gi", "group"))
        dma("sync", cpool[:], consts_d, writes=[("cpool",)], stream=("si", "group"))
        dma("sync", downb[:], downb_d, writes=[("downb",)], stream=("si", "group"))
        dma("sync", bcol[:], upb_d, writes=[("bcol",)], stream=("si", "group"))
        dma("sync", colt[:], coreb_d, writes=[("coltc",)], stream=("si", "group"))
        vop("tensor_add", bcol[:], bcol[:], colt[:], reads=[("bcol",), ("coltc",)],
            writes=[("bcol",)])
        dma("sync", b1all[:], b1all_d, writes=[("b1all",)], stream=("si", "group"))
        act(gb1[:], b1all[:], AF.Gelu, reads=[("b1all",)], writes=[("gb1",)])
        dma("sync", rowtmp[:], b2row_d, writes=[("rowtmp",)], stream=("si", "group"))
        dma("sync", rbl[0:1, 0:NEXP], rbrow_d, writes=[("rbl",)], stream=("si", "group"))
        dma("sync", rbl[0:1, 16:16 + NEXP], lam_d, writes=[("rbllam",)], stream=("si", "group"))
        vop("tensor_sub", rbl[0:1, 0:NEXP], rbl[0:1, 0:NEXP], rbl[0:1, 16:16 + NEXP],
            reads=[("rbl",), ("rbllam",)], writes=[("rbl",)])
        scopy(rbl_hi[:], rbl[0:1, 0:NEXP], reads=[("rbl",)], writes=[("rblhi",)])
        scopy(rbl[0:1, 8:8 + NEXP], rbl_hi[:], reads=[("rblhi",)], writes=[("rblhf",)])
        vop("tensor_sub", rbl_lo[:], rbl[0:1, 0:NEXP], rbl[0:1, 8:8 + NEXP],
            reads=[("rbl",), ("rblhf",)], writes=[("rbllo",)])
        dma("sync", rw_sb[:], rwt_d, writes=[("rw",)], stream=("si", "group"))
        scopy(rw_hi[:], rw_sb[:], reads=[("rw",)], writes=[("rwhi",)])
        vop("tensor_sub", rw_lo[:], rw_sb[:], rw_hi[:], reads=[("rw",), ("rwhi",)],
            writes=[("rwlo",)])
        memset(ones1[:], 1.0, writes=[("ones1",)])
        memset(onesc[:], 1.0, writes=[("onesc",)])
        memset(wT_sb[0:10, :], 1.0, writes=[("wTones",)])
        vop("tensor_add", rowtmp[0:1, 0:4096], rowtmp[0:1, 0:4096], rowtmp[0:1, 4096:8192],
            reads=[("rowtmp",)], writes=[("rowtmp",)])
        vop("tensor_add", rowtmp[0:1, 0:2048], rowtmp[0:1, 0:2048], rowtmp[0:1, 2048:4096],
            reads=[("rowtmp",)], writes=[("rowtmp",)])
        vop("tensor_add", crow[0:1, :], rowtmp[0:1, 0:1024], rowtmp[0:1, 1024:2048],
            reads=[("rowtmp",)], writes=[("crow",)])
        cvt_i = [0]

        def convert_tiled(src, dst4, R, C):
            for c0 in range(R // 128):
                for cc in range(0, C, 2048):
                    w = min(2048, C - cc)
                    sl = cvt_i[0] % 2
                    cvt_i[0] += 1
                    rid = ("cvt", sl)
                    dma("gpsimd", cvt[sl][:, 0:w], src[c0 * 128:(c0 + 1) * 128, cc:cc + w],
                        writes=[rid], stream=(f"cvt{sl}", "slot"))
                    dma("gpsimd", dst4[cc // 128:(cc + w) // 128, :, c0, :].transpose([1, 0, 2]),
                        cvt[sl][:, 0:w].rearrange("p (j o) -> p j o", o=128),
                        reads=[rid], writes=[("c_" + dst4.name,)],
                        stream=("c_" + dst4.name, "group"))

        def convert_plain(src, dst, R, C):
            for c0 in range(R // 128):
                sl = cvt_i[0] % 2
                cvt_i[0] += 1
                rid = ("cvt", sl)
                dma("gpsimd", cvt[sl][:, 0:C], src[c0 * 128:(c0 + 1) * 128, :], writes=[rid],
                    stream=(f"cvt{sl}", "slot"))
                dma("gpsimd", dst[c0 * 128:(c0 + 1) * 128, :], cvt[sl][:, 0:C],
                    reads=[rid], writes=[("c_" + dst.name,)], stream=("c_" + dst.name, "group"))

        convert_tiled(wqT_d, wq_c, D, D)
        convert_tiled(woT_d, wo_c, D, D)
        for e in range(NEXP):
            convert_plain(w2_d[e], w2_c[e], HID[e], LD)
        convert_tiled(down_d, down_c, D, LD)
        convert_plain(sw2_d, sw2_c, LD, LD)
        convert_tiled(sw1_d, sw1_c, LD, LD)
        for e in range(NEXP):
            convert_tiled(w1_d[e], w1_c[e], LD, HID[e])
        convert_tiled(up_d, up_c, LD, D)
        convert_tiled(core_d, core_c, D, D)
        s.barrier()
        _close(init)

        spanKV = _mk()
        K_sb = sb(spanKV, "K_sb", [128, 4 * S], BF16)
        V_sb = sb(spanKV, "V_sb", [128, 16 * 520], BF16)

        span01 = _mk()
        ln1T = sb(span01, "ln1T", [128, 16 * S], BF16)

        a0 = _mk()
        x_t = [sb(a0, f"x_t{i}", [128, D], F32) for i in range(2)]
        ln1f = [sb(a0, f"ln1f{i}", [128, D], BF16) for i in range(2)]
        sqt = sb(a0, "sqt", [128, D], F32)
        colsA = sb(a0, "colsA", [128, 8], F32)

        def ln_token(xap, sqap, mcol, rcol, m2col, statres, xtag, sqtag):
            vop("reduce_sum", mcol, xap, AX.X, reads=[xtag], writes=[statres])
            act(sqap, xap, AF.Square, reads=[xtag], writes=[sqtag])
            vop("reduce_sum", rcol, sqap, AX.X, reads=[sqtag], writes=[statres])
            vop("tensor_scalar", mcol, mcol, 1.0 / D, None, ALU.mult,
                reads=[statres], writes=[statres])
            vop("tensor_scalar", rcol, rcol, 1.0 / D, None, ALU.mult,
                reads=[statres], writes=[statres])
            vop("tensor_tensor", m2col, mcol, mcol, ALU.mult, reads=[statres], writes=[statres])
            vop("tensor_sub", rcol, rcol, m2col, reads=[statres], writes=[statres])
            act(rcol, rcol, AF.Ln, bias=EPS, reads=[statres], writes=[statres])
            act(rcol, rcol, AF.Exp, scale=-0.5, reads=[statres], writes=[statres])

        lnv = ln1T[:].rearrange("p (c t) -> p c t", c=16)
        for tt in range(16):
            sl = tt % 2
            xa, la = x_t[sl], ln1f[sl]
            xr = ("x_t", sl)
            dma("sync", xa[:], x_d[tt * 128:(tt + 1) * 128, :], writes=[xr],
                stream=(f"xt{sl}", "slot"))
            stat = ("lnA", sl)
            mcol, rcol, m2col = colsA[:, 3 * sl:3 * sl + 1], colsA[:, 3 * sl + 1:3 * sl + 2], \
                colsA[:, 3 * sl + 2:3 * sl + 3]
            ln_token(xa[:], sqt[:], mcol, rcol, m2col, stat, xr, ("sqA",))
            vop("tensor_scalar", la[:], xa[:], mcol, rcol, ALU.subtract, ALU.mult,
                reads=[xr, stat], writes=[("ln1f", sl)])
            for half in range(2):
                pp = PS[2 + half]
                ppb = pp[:].bitcast(BF16)
                pres = PN[2 + half]
                for j in range(8):
                    dc = half * 8 + j
                    tr(ppb[:, j * 128:(j + 1) * 128], la[:, dc * 128:(dc + 1) * 128], ident_b[:],
                       reads=[("ln1f", sl), ("identb",)], writes=[pres])
                scopy(lnv[:, half * 8:(half + 1) * 8, tt * 128:(tt + 1) * 128],
                      ppb[:, 0:1024].rearrange("p (c t) -> p c t", c=8),
                      reads=[pres], writes=[("ln1T",)])

        s.barrier()
        _close(a0)
        _ckpt("a0")

        print('SBUF free at A1(before allocs):', nc.sbuf_bytes_remaining)
        a1 = _mk()
        wvst = sb(a1, "wvst", [128, 16 * 512], BF16)
        wvstf = [sb(a1, f"wvstf{i}", [128, 512], F32) for i in range(2)]
        kstf = [sb(a1, f"kstf{i}", [128, 16 * 128], F32) for i in range(2)]
        cosc = sb(a1, "cosc", [128, 512], F32)
        sinc = sb(a1, "sinc", [128, 512], F32)
        kpb = [sb(a1, f"kpb{i}", [128, 512], BF16) for i in range(2)]
        rp1 = [sb(a1, f"rp1_{i}", [128, 512], F32) for i in range(2)]
        rp2 = [sb(a1, f"rp2_{i}", [128, 512], F32) for i in range(2)]

        memset(V_sb[:, 64::65], 1.0, writes=[("Vones",)])
        for dc in range(16):
            dma("sync", wvstf[dc % 2][:], wvT_d[dc * 128:(dc + 1) * 128, :],
                writes=[("wvstf", dc % 2)], stream=(f"wvst{dc % 2}", "slot"))
            vop("tensor_copy", wvst[:, dc * 512:(dc + 1) * 512], wvstf[dc % 2][:],
                reads=[("wvstf", dc % 2)], writes=[("wvst", dc)])
        vv = V_sb[:].rearrange("p (c x) -> p c x", c=16)
        for tt in range(16):
            pp = PS[tt % 2]
            pres = PN[tt % 2]
            for dc in range(16):
                mm(pp[:], lnv[:, dc, tt * 128:(tt + 1) * 128], wvst[:, dc * 512:(dc + 1) * 512],
                   dc == 0, dc == 15, reads=[("ln1T",), ("wvst", dc)], writes=[pres])
            scopy(vv[:, tt, :].rearrange("p (h z) -> p h z", h=8)[:, :, 0:64],
                  pp[:].rearrange("p (h z) -> p h z", h=8),
                  reads=[pres], writes=[("V", tt)])

        _ckpt("a1v")
        for ot in range(4):
            cb = cbst[ot % 2]
            cbres = ("cbst", ot % 2)
            dma("sync", kstf[ot % 2][:].rearrange("p (c o) -> p c o", c=16),
                wkT_d[:, ot * 128:(ot + 1) * 128].rearrange("(c p) o -> p c o", p=128),
                writes=[("kstf", ot % 2)], stream=(f"kcb{ot % 2}", "slot"))
            vop("tensor_copy", cb[:], kstf[ot % 2][:],
                reads=[("kstf", ot % 2)], writes=[cbres])
            _ckpt("a1kd")
            for tq in range(4):
                dma("sync", cosc[:], cosT_d[:, tq * 512:(tq + 1) * 512], writes=[("cosc",)],
                    stream=("cosc", "slot"))
                _ckpt("a1cA")
                dma("sync", sinc[:], sinT_d[:, tq * 512:(tq + 1) * 512], writes=[("sinc",)],
                    stream=("sinc", "slot"))
                _ckpt("a1c")
                pp = PS[tq % 2]
                pres = PN[tq % 2]
                psw = PS[2 + tq % 2]
                psres = PN[2 + tq % 2]
                for dc in range(16):
                    mm(pp[:], cb[:, dc * 128:(dc + 1) * 128], lnv[:, dc, tq * 512:(tq + 1) * 512],
                       dc == 0, dc == 15, reads=[cbres, ("ln1T",)], writes=[pres])
                sl = tq % 2
                _ckpt("a1m")
                scopy(kpb[sl][:], pp[:], reads=[pres], writes=[("kpb", sl)])
                _ckpt("a1s")
                mm(psw[:], perm_b[:], kpb[sl][:], True, True,
                   reads=[("permb",), ("kpb", sl)], writes=[psres])
                _ckpt("a1p")
                vop("tensor_mul", rp1[sl][:], pp[:], cosc[:], reads=[pres, ("cosc",)],
                    writes=[("rp1", sl)])
                _ckpt("a1r1")
                vop("tensor_mul", rp2[sl][:], psw[:], sinc[:], reads=[psres, ("sinc",)],
                    writes=[("rp2", sl)])
                _ckpt("a1r2")
                vop("tensor_add", K_sb[:, ot * S + tq * 512: ot * S + (tq + 1) * 512],
                    rp1[sl][:], rp2[sl][:], reads=[("rp1", sl), ("rp2", sl)], writes=[("K",)])
                _ckpt("a1r")

            _ckpt("a1k1")
        _ckpt("a1k")
        for q4 in range(4):
            dma("sync", ln1T_dram[q4 * 4:(q4 + 1) * 4, :, :].transpose([1, 0, 2]),
                lnv[:, q4 * 4:(q4 + 1) * 4, :], reads=[("ln1T",)], writes=[("ln1Td",)],
                stream=("ln1Td", "group"))
        s.barrier()
        _close(a1)
        _close(span01)
        _ckpt("a1")

        print('SBUF free at A2(before allocs):', nc.sbuf_bytes_remaining)
        a2 = _mk()
        ln1Tc = [sb(a2, "ln1Tc0", [128, 16 * ACH], BF16)] * 2
        cosc2 = [sb(a2, f"cosc2_{i}", [128, ACH], F32) for i in range(2)]
        sinc2 = [sb(a2, f"sinc2_{i}", [128, ACH], F32) for i in range(2)]
        QTc = sb(a2, "QTc", [128, 16 * ACH], BF16)
        qpb = [sb(a2, f"qpb{i}", [128, ACH], BF16) for i in range(2)]
        qr1 = [sb(a2, "qr1_0", [128, ACH], F32)] * 2
        qr2 = [sb(a2, "qr2_0", [128, ACH], F32)] * 2
        E_sl = [sb(a2, f"E{i}", [128, 2 * ACH], BF16) for i in range(4)]
        OTc = sb(a2, "OTc", [128, 16 * ACH], BF16)
        tmpO = [sb(a2, f"tmpO{i}", [64, 512], F32) for i in range(2)]
        aoTc = sb(a2, "aoTc", [128, 16 * ACH], F32)
        sqb = sb(a2, "sqb", [128, 16 * ACH], BF16)
        ntmp = [sb(a2, f"ntmp{i}", [128, ACH], F32) for i in range(2)]
        xa_t = sb(a2, "xa_t", [128, D], F32)
        x1_t = sb(a2, "x1_t", [128, D], F32)
        ln3f = sb(a2, "ln3f", [128, D], F32)
        ln3h = sb(a2, "ln3h", [128, D], BF16)
        ln3lo = sb(a2, "ln3lo", [128, D], BF16)
        lst = [sb(a2, f"lst{i}", [128, 8 * 128], BF16) for i in range(2)]
        lstl = [sb(a2, f"lstl{i}", [128, 8 * 128], BF16) for i in range(2)]
        colsB = sb(a2, "colsB", [128, 8], F32)
        rr = sb(a2, "rr", [1, 512], F32)
        rr2 = sb(a2, "rr2", [1, 512], F32)
        logit = sb(a2, "logit", [128, 8], F32)
        prob = sb(a2, "prob", [128, 8], F32)
        gsc = sb(a2, "gsc", [128, 24], F32)
        rows = sb(a2, "rows", [1, 2304], F32)
        rows_b = sb(a2, "rows_b", [1, 3072], BF16)

        def rows_hilo(src, n, hi_b, lo_b, ftmp, base, ftag):
            scopy(hi_b, src, reads=[base], writes=[(base, "hi")])
            scopy(ftmp, hi_b, reads=[(base, "hi")], writes=[(base, "hf"), ftag])
            vop("tensor_sub", lo_b, src, ftmp,
                reads=[base, (base, "hf"), ftag], writes=[(base, "lo")])

        def epilogue_stages(c):
            stages = []
            for tb in range(ACH // 128):
                tba = c * (ACH // 128) + tb

                def st_x1h(half, tba=tba, tb=tb):
                    def go(pw, pwn):
                        if half == 0:
                            dma("sync", xa_t[:], x_d[tba * 128:(tba + 1) * 128, :],
                                writes=[("xa",)], stream=("xa", "slot"))
                        for j in range(8):
                            dt = half * 8 + j
                            tr(pw[:, j * 128:(j + 1) * 128],
                               aoTc[:, dt * ACH + tb * 128: dt * ACH + tb * 128 + 128],
                               ident_f[:],
                               reads=[("aoTc",), ("identf",)], writes=[(pwn, j // 4)])
                        vop("tensor_add", x1_t[:, half * 1024:(half + 1) * 1024], pw[:],
                            xa_t[:, half * 1024:(half + 1) * 1024],
                            reads=[(pwn, 0), (pwn, 1), ("xa",)], writes=[("x1",)])
                        if half == 1:
                            dma("sync", x1_dram[tba * 128:(tba + 1) * 128, :], x1_t[:],
                                reads=[("x1",)], writes=[("x1d",)], stream=("x1d", "slot"))
                            mcol, rcol, m2col = colsB[:, 0:1], colsB[:, 1:2], colsB[:, 2:3]
                            ln_token(x1_t[:], xa_t[:], mcol, rcol, m2col, ("lnB",),
                                     ("x1",), ("xa",))
                            vop("tensor_scalar", ln3f[:], x1_t[:], mcol, rcol,
                                ALU.subtract, ALU.mult,
                                reads=[("x1",), ("lnB",)], writes=[("ln3f",)])
                            scopy(ln3h[:], ln3f[:], reads=[("ln3f",)], writes=[("ln3h",)])
                            scopy(xa_t[:], ln3h[:], reads=[("ln3h",)], writes=[("xa",)])
                            vop("tensor_sub", ln3lo[:], ln3f[:], xa_t[:],
                                reads=[("ln3f",), ("xa",)], writes=[("ln3lo",)])
                    return go
                stages.append(st_x1h(0))
                stages.append(st_x1h(1))

                def st_l3(half, lo, tba=tba):
                    def go(pw, pwn):
                        ppb = pw[:].bitcast(BF16)
                        src = ln3lo if lo else ln3h
                        stag = ("ln3lo",) if lo else ("ln3h",)
                        for j in range(8):
                            dc = half * 8 + j
                            tr(ppb[:, j * 128:(j + 1) * 128], src[:, dc * 128:(dc + 1) * 128],
                               ident_b[:], reads=[stag, ("identb",)], writes=[(pwn, 0)])
                        dstb = lstl[half] if lo else lst[half]
                        dtag = ("lstl", half) if lo else ("lst", half)
                        scopy(dstb[:], ppb[:, 0:1024], reads=[(pwn, 0)], writes=[dtag])
                        if not lo:
                            dma("sync",
                                ln3T_dram[half * 8:(half + 1) * 8, :,
                                          tba * 128:(tba + 1) * 128].transpose([1, 0, 2]),
                                lst[half][:].rearrange("p (c t) -> p c t", c=8),
                                reads=[("lst", half)], writes=[("ln3Td", half)],
                                stream=(f"l3d{half}", "slot"))
                    return go
                stages.append(st_l3(0, False))
                stages.append(st_l3(0, True))
                stages.append(st_l3(1, False))
                stages.append(st_l3(1, True))

                def st_router(tba=tba):
                    def go(pw, pwn):
                        rwv_hi = rw_hi[:].rearrange("p (c e) -> p c e", c=16)
                        rwv_lo = rw_lo[:].rearrange("p (c e) -> p c e", c=16)
                        for half in range(2):
                            for j in range(8):
                                dc = half * 8 + j
                                first = (half == 0 and j == 0)
                                mm(pw[:, 0:8], lst[half][:, j * 128:(j + 1) * 128],
                                   rwv_hi[:, dc, :], first, False,
                                   reads=[("lst", half), ("rwhi",)], writes=[(pwn, 0)])
                                mm(pw[:, 0:8], lst[half][:, j * 128:(j + 1) * 128],
                                   rwv_lo[:, dc, :], False, False,
                                   reads=[("lst", half), ("rwlo",)], writes=[(pwn, 0)])
                                mm(pw[:, 0:8], lstl[half][:, j * 128:(j + 1) * 128],
                                   rwv_hi[:, dc, :], False, False,
                                   reads=[("lstl", half), ("rwhi",)], writes=[(pwn, 0)])
                        mm(pw[:, 0:8], ones1[:], rbl_hi[:], False, False,
                           reads=[("ones1",), ("rblhi",)], writes=[(pwn, 0)])
                        mm(pw[:, 0:8], ones1[:], rbl_lo[:], False, True,
                           reads=[("ones1",), ("rbllo",)], writes=[(pwn, 0)])
                        scopy(logit[:], pw[:, 0:8], reads=[(pwn, 0)], writes=[("logit",)])
                        g0, g1 = gsc[:, 0:1], gsc[:, 1:2]
                        m1, m2 = gsc[:, 8:16], gsc[:, 16:24]
                        vop("reduce_max", g0, logit[:], AX.X, reads=[("logit",)],
                            writes=[("g0",)])
                        vop("tensor_scalar", prob[:], logit[:], g0, None, ALU.subtract,
                            reads=[("logit",), ("g0",)], writes=[("prob",)])
                        act(prob[:], prob[:], AF.Exp, reads=[("prob",)], writes=[("prob",)])
                        vop("reduce_sum", g1, prob[:], AX.X, reads=[("prob",)],
                            writes=[("g1",)])
                        vop("reciprocal", g1, g1, reads=[("g1",)], writes=[("g1",)])
                        vop("tensor_scalar", prob[:], prob[:], g1, None, ALU.mult,
                            reads=[("prob",), ("g1",)], writes=[("prob",)])
                        vop("reduce_max", g0, prob[:], AX.X, reads=[("prob",)],
                            writes=[("g0",)])
                        vop("tensor_scalar", m1, prob[:], g0, None, ALU.is_ge,
                            reads=[("prob",), ("g0",)], writes=[("m1",)])
                        vop("scalar_tensor_tensor", m2, m1, -1000.0, prob[:],
                            ALU.mult, ALU.add,
                            reads=[("m1",), ("prob",)], writes=[("m2",)])
                        vop("reduce_max", g0, m2, AX.X, reads=[("m2",)], writes=[("g0",)])
                        vop("tensor_scalar", m2, m2, g0, None, ALU.is_ge,
                            reads=[("m2",), ("g0",)], writes=[("m2",)])
                        vop("tensor_add", m1, m1, m2, reads=[("m1",), ("m2",)],
                            writes=[("m1",)])
                        act(prob[:], prob[:], AF.Exp, reads=[("prob",)], writes=[("prob",)])
                        vop("tensor_mul", prob[:], prob[:], m1,
                            reads=[("prob",), ("m1",)], writes=[("prob",)])
                        vop("reduce_sum", g1, prob[:], AX.X, reads=[("prob",)],
                            writes=[("g1",)])
                        vop("reciprocal", g1, g1, reads=[("g1",)], writes=[("g1",)])
                        vop("tensor_scalar", w_tm[:, tba * 9: tba * 9 + 8], prob[:], g1,
                            None, ALU.mult,
                            reads=[("prob",), ("g1",)], writes=[("w_tm",)])
                    return go
                stages.append(st_router())

                def st_wT(tba=tba):
                    def go(pw, pwn):
                        tr(pw[0:8, 0:128], w_tm[:, tba * 9: tba * 9 + 8], ident_f[:],
                           reads=[("w_tm",), ("identf",)], writes=[(pwn, 0)])
                        scopy(wT_sb[0:8, tba * 128:(tba + 1) * 128], pw[0:8, 0:128],
                              reads=[(pwn, 0)], writes=[("wT",)])
                    return go
                stages.append(st_wT())
            return stages


        for c in range(NACH):
            t0 = c * ACH
            cpar = 0
            l1vc = ln1Tc[cpar][:].rearrange("p (c t) -> p c t", c=16)
            cosv, sinv = cosc2[cpar], sinc2[cpar]
            dma("sync", l1vc, ln1T_dram[:, :, t0:t0 + ACH].transpose([1, 0, 2]),
                reads=[("ln1Td",)], writes=[("ln1Tc", cpar)], stream=(f"l1c{cpar}", "slot"))
            dma("sync", cosv[:], cosT_d[:, t0:t0 + ACH], writes=[("cosc", cpar)],
                stream=(f"cosc{cpar}", "slot"))
            dma("sync", sinv[:], sinT_d[:, t0:t0 + ACH], writes=[("sinc", cpar)],
                stream=(f"sinc{cpar}", "slot"))

            def jit_mms(ot):
                cb = cbst[ot % 3]
                cbres = ("cbst", ot % 3)
                dma("sync", cb[:].rearrange("p (c o) -> p c o", c=16),
                    wq_c[ot, :, :, :],
                    reads=[("c_wq_c",)], writes=[cbres], stream=(f"cb{ot % 3}", "slot"))
                pp = PS[ot % 2]
                pres = PN[ot % 2]
                for dc in range(16):
                    mm(pp[:, 0:ACH], cb[:, dc * 128:(dc + 1) * 128], l1vc[:, dc, :],
                       dc == 0, dc == 15, reads=[cbres, ("ln1Tc", cpar)], writes=[pres])

            def jit_rope(ot):
                sl = ot % 2
                pp = PS[sl]
                pres = PN[sl]
                scopy(qpb[sl][:], pp[:, 0:ACH], reads=[pres], writes=[("qpb", sl)])
                psw = PS[2 + sl]
                psres = PN[2 + sl]
                mm(psw[:, 0:ACH], perm_b[:], qpb[sl][:], True, True,
                   reads=[("permb",), ("qpb", sl)], writes=[psres])
                vop("tensor_mul", qr1[sl][:], pp[:, 0:ACH], cosv[:],
                    reads=[pres, ("cosc", cpar)], writes=[("qr1",)])
                vop("tensor_mul", qr2[sl][:], psw[:, 0:ACH], sinv[:],
                    reads=[psres, ("sinc", cpar)], writes=[("qr2",)])
                vop("tensor_add", QTc[:, ot * ACH:(ot + 1) * ACH], qr1[sl][:], qr2[sl][:],
                    reads=[("qr1",), ("qr2",)], writes=[("QTc",)])

            for ot in range(16):
                jit_mms(ot)
                if ot > 0:
                    jit_rope(ot - 1)
            jit_rope(15)
            stq = epilogue_stages(c - 1) if c > 0 else []
            spop = [0]

            def pop_stage():
                if stq:
                    b = spop[0] % 2
                    spop[0] += 1
                    stq.pop(0)(PW0 if b == 0 else PW1, "PW0" if b == 0 else "PW1")

            nkt = 2 * (c + 1)
            npair = nkt // 2

            def pair_scores_pv(p2i):
                pp2 = p2i % 2
                h0 = 2 * p2i
                kvh = h0 // 4
                ktile, koff = kvh // 2, 64 * (kvh % 2)
                pw = PW0 if pp2 == 0 else PW1
                pwn = "PW0" if pp2 == 0 else "PW1"
                qtile, qoff = qplace(h0)
                qrhs = QTc[qoff:qoff + 64, qtile * ACH:(qtile + 2) * ACH]

                def emit_s(kt):
                    pp = PS[kt % 4]
                    pres = PN[kt % 4]
                    mm(pp[:],
                       K_sb[koff:koff + 64,
                            ktile * S + kt * 128: ktile * S + (kt + 1) * 128],
                       qrhs, True, True, reads=[("K",), ("QTc",)], writes=[pres])
                    esl = kt % 4
                    act(E_sl[esl][:], pp[:], AF.Exp, bias=-ESHIFT, scale=0.125,
                        reads=[pres], writes=[("E", esl)])
                    if kt >= nkt - 2:
                        di = 0 if kt == nkt - 2 else 1
                        vop("tensor_mul", E_sl[esl][:], E_sl[esl][:], m01d_b[di][:],
                            reads=[("E", esl), ("m01d", di)], writes=[("E", esl)])

                def emit_pv(kt):
                    esl = kt % 4
                    mm(pw[0:65, 0:512],
                       V_sb[:, kt * 520 + kvh * 65: kt * 520 + (kvh + 1) * 65],
                       E_sl[esl][:], kt == 0, kt == nkt - 1,
                       reads=[("V", kt), ("Vones",), ("E", esl)], writes=[(pwn, 0)])

                return emit_s, emit_pv

            def pair_norm_calc(p2i):
                hp = p2i % 2
                pw = PW0 if hp == 0 else PW1
                pwn = "PW0" if hp == 0 else "PW1"
                rsrc = rr if hp == 0 else rr2
                rscr = rows[0:1, 1792:2304]
                act(rscr, pw[64:65, 0:512], AF.Ln, reads=[(pwn, 0)], writes=[("rscr",)])
                act(rsrc[0:1, 0:512], rscr, AF.Exp, scale=-1.0,
                    reads=[("rscr",)], writes=[("rr", hp)])
                vop("tensor_copy", rows_b[0:1, hp * 512:hp * 512 + 512], rsrc[0:1, 0:512],
                    reads=[("rr", hp)], writes=[(("rr", hp), "hi")])

            def pair_norm_apply(p2i):
                hp = p2i % 2
                h0 = 2 * p2i
                qtile, qoff = qplace(h0)
                pw = PW0 if hp == 0 else PW1
                pwn = "PW0" if hp == 0 else "PW1"
                mm(pw[0:64, 512:1024], ones1[0:1, 0:64],
                   rows_b[0:1, hp * 512:hp * 512 + 512], True, True,
                   reads=[("ones1",), (("rr", hp), "hi")], writes=[(pwn, 1)])
                scopy(tmpO[hp][:], pw[0:64, 0:512], reads=[(pwn, 0)], writes=[("tmpO", hp)])
                vop("tensor_mul", OTc[qoff:qoff + 64, qtile * ACH:(qtile + 2) * ACH],
                    tmpO[hp][:], pw[0:64, 512:1024],
                    reads=[("tmpO", hp), (pwn, 1)], writes=[("OTc",)])

            LA = 2
            for p2i in range(16):
                emit_s, emit_pv = pair_scores_pv(p2i)
                for kt in range(min(LA + 1, nkt)):
                    emit_s(kt)
                if p2i > 0:
                    pair_norm_calc(p2i - 1)
                late = max(0, nkt - 2)
                for kt in range(nkt):
                    if kt + LA + 1 < nkt:
                        emit_s(kt + LA + 1)
                    if kt == late and p2i > 0:
                        pair_norm_apply(p2i - 1)
                    emit_pv(kt)
                    if kt == late and p2i > 0:
                        pop_stage()
            pair_norm_calc(15)
            pair_norm_apply(15)

            for dt in range(16):
                cb = cbst[dt % 3]
                cbres = ("cbst", dt % 3)
                dma("sync", cb[:].rearrange("p (c o) -> p c o", c=16), wo_c[dt, :, :, :],
                    reads=[("c_wo_c",)], writes=[cbres], stream=(f"cb{dt % 3}", "slot"))
                pp = PS[dt % 4]
                pres = PN[dt % 4]
                for oc in range(16):
                    mm(pp[:, 0:ACH], cb[:, oc * 128:(oc + 1) * 128], OTc[:, oc * ACH:(oc + 1) * ACH],
                       oc == 0, oc == 15, reads=[cbres, ("OTc",)], writes=[pres])
                scopy(aoTc[:, dt * ACH:(dt + 1) * ACH], pp[:, 0:ACH], reads=[pres],
                      writes=[("aoTc",)])
                vop("tensor_copy", QTc[:, dt * ACH:(dt + 1) * ACH],
                    aoTc[:, dt * ACH:(dt + 1) * ACH], reads=[("aoTc",)], writes=[("QTc",)])
                act(sqb[:, dt * ACH:(dt + 1) * ACH], aoTc[:, dt * ACH:(dt + 1) * ACH],
                    AF.Square, reads=[("aoTc",)], writes=[("sqb",)])
                if dt % 4 == 3:
                    pop_stage()

            for dc in range(16):
                mm(P2[0:1, 0:ACH], onesc[:], QTc[:, dc * ACH:(dc + 1) * ACH], dc == 0, dc == 15,
                   reads=[("onesc",), ("QTc",)], writes=[("P2",)])
                mm(P3[0:1, 0:ACH], onesc[:], sqb[:, dc * ACH:(dc + 1) * ACH], dc == 0, dc == 15,
                   reads=[("onesc",), ("sqb",)], writes=[("P3",)])
            mrow = rows[0:1, 1024:1024 + ACH]
            vrow = rows[0:1, 1024 + ACH:1024 + 2 * ACH]
            trow = rows[0:1, 1024 + 2 * ACH:1024 + 3 * ACH]
            vop("tensor_scalar", mrow, P2[0:1, 0:ACH], 1.0 / D, None, ALU.mult,
                reads=[("P2",)], writes=[("mrow",)])
            vop("tensor_scalar", vrow, P3[0:1, 0:ACH], 1.0 / D, None, ALU.mult,
                reads=[("P3",)], writes=[("vrow",)])
            vop("tensor_tensor", trow, mrow, mrow, ALU.mult, reads=[("mrow",)], writes=[("trow",)])
            vop("tensor_sub", vrow, vrow, trow, reads=[("vrow",), ("trow",)], writes=[("vrow",)])
            act(vrow, vrow, AF.Ln, bias=EPS, reads=[("vrow",)], writes=[("vrow",)])
            act(vrow, vrow, AF.Exp, scale=-0.5, reads=[("vrow",)], writes=[("vrow",)])
            rows_hilo(mrow, ACH, rows_b[0:1, 2048:2048 + ACH],
                      rows_b[0:1, 2048 + ACH:2048 + 2 * ACH], trow, ("mrow",), ("ftB",))
            mm(P2[:, 0:ACH], ones1[:], rows_b[0:1, 2048:2048 + ACH], True, False,
               reads=[("ones1",), (("mrow",), "hi")], writes=[("P2",)])
            mm(P2[:, 0:ACH], ones1[:], rows_b[0:1, 2048 + ACH:2048 + 2 * ACH], False, True,
               reads=[("ones1",), (("mrow",), "lo")], writes=[("P2",)])
            rows_hilo(vrow, ACH, rows_b[0:1, 2560:2560 + ACH],
                      rows_b[0:1, 2560 + ACH:2560 + 2 * ACH], trow, ("vrow",), ("ftB",))
            mm(P3[:, 0:ACH], ones1[:], rows_b[0:1, 2560:2560 + ACH], True, False,
               reads=[("ones1",), (("vrow",), "hi")], writes=[("P3",)])
            mm(P3[:, 0:ACH], ones1[:], rows_b[0:1, 2560 + ACH:2560 + 2 * ACH], False, True,
               reads=[("ones1",), (("vrow",), "lo")], writes=[("P3",)])
            for dt in range(16):
                sl = dt % 2
                vop("tensor_sub", ntmp[sl][:], aoTc[:, dt * ACH:(dt + 1) * ACH], P2[:, 0:ACH],
                    reads=[("aoTc",), ("P2",)], writes=[("ntmp", sl)])
                vop("tensor_mul", aoTc[:, dt * ACH:(dt + 1) * ACH], ntmp[sl][:], P3[:, 0:ACH],
                    reads=[("ntmp", sl), ("P3",)], writes=[("aoTc",)])

        tailq = epilogue_stages(NACH - 1)
        for i, st_ in enumerate(tailq):
            st_(PW0 if i % 2 == 0 else PW1, "PW0" if i % 2 == 0 else "PW1")
        s.barrier()
        _close(a2)
        _close(spanKV)
        _ckpt("a2")

        ups = _mk()
        ustg = sb(ups, "ustg", [1, LD], BF16)
        b1ofs = np.cumsum([0] + NH).tolist()
        for e in range(NEXP):
            for hcx in range(NH[e]):
                slw = hcx % 3
                dma("sync", w2st[slw][:], w2_c[e][hcx * 128:(hcx + 1) * 128, :],
                    reads=[("c_" + w2_c[e].name,)], writes=[("w2st", slw)],
                    stream=(f"w2{slw}", "slot"))
                for half in range(2):
                    pp = PS[half]
                    mm(pp[0:1, 0:512], gb1[:, b1ofs[e] + hcx: b1ofs[e] + hcx + 1],
                       w2st[slw][:, half * 512:(half + 1) * 512], hcx == 0, hcx == NH[e] - 1,
                       reads=[("gb1",), ("w2st", slw)], writes=[PN[half]])
            scopy(urow[0:1, 0:512], P0[0:1, 0:512], reads=[("P0",)], writes=[("urow",)])
            scopy(urow[0:1, 512:1024], P1[0:1, 0:512], reads=[("P1",)], writes=[("urow",)])
            vop("tensor_add", crow[0:1, :], crow[0:1, :], urow[0:1, :],
                reads=[("crow",), ("urow",)], writes=[("crow",)])
            s.op("scalar",
                 (lambda o, i: lambda en: en.activation(o, i, AF.Identity, bias=0.0, scale=-1.0))(
                     ustg[0:1, :], urow[0:1, :]),
                 reads=[("urow",)], writes=[("ustg",)])
            dma("gpsimd", ustack_dram[e:e + 1, :], ustg[0:1, :], reads=[("ustg",)],
                writes=[("ustackd",)], stream=("usd", "slot"))
        dma("sync", urow[0:1, :], sb2row_d, writes=[("sb2r",), ("urow",)],
            stream=("sb2", "group"))
        vop("scalar_tensor_tensor", crow[0:1, :], urow[0:1, :], 0.1, crow[0:1, :],
            ALU.mult, ALU.add, reads=[("sb2r",), ("crow",)], writes=[("crow",)])
        scopy(ustg[0:1, :], crow[0:1, :], reads=[("crow",)], writes=[("ustg",)])
        dma("gpsimd", ustack_dram[8:9, :], ustg[0:1, :], reads=[("ustg",)], writes=[("ustackd",)],
            stream=("usd", "slot"))
        scopy(urow[0:1, :], ustg[0:1, :], reads=[("ustg",)], writes=[("uhf",), ("sb2r",)])
        vop("tensor_sub", urow[0:1, :], crow[0:1, :], urow[0:1, :],
            reads=[("crow",), ("uhf",)], writes=[("ulo",)])
        scopy(ustg[0:1, :], urow[0:1, :], reads=[("ulo",)], writes=[("ustg",)])
        dma("gpsimd", ustack_dram[9:10, :], ustg[0:1, :], reads=[("ustg",)], writes=[("ustackd",)],
            stream=("usd", "slot"))
        dma("sync", ustack[:], ustack_dram, reads=[("ustackd",)], writes=[("ustack",)],
            stream=("usl", "group"))
        s.barrier()
        _close(ups)
        _ckpt("ups")

        print('SBUF free at FFN(before allocs):', nc.sbuf_bytes_remaining)
        ffn = _mk()
        ln3Tc = sb(ffn, "ln3Tc", [128, 16 * FCH], BF16)
        g_sb = sb(ffn, "g_sb", [128, 8 * FCH], BF16)
        gxT = sb(ffn, "gxT", [128, 16 * FCH], BF16)
        sc32 = sb(ffn, "sc32", [128, 32 * FCH], BF16)
        acc_sb = sb(ffn, "acc_sb", [128, 4 * LD], F32)
        accT = sb(ffn, "accT", [128, 8 * FCH], BF16)
        faoT = sb(ffn, "faoT", [128, 16 * FCH], F32)
        fntmp = [sb(ffn, f"fntmp{i}", [128, FCH], F32) for i in range(2)]
        fx1 = sb(ffn, "fx1", [128, D], F32)
        frows = sb(ffn, "frows", [1, 3 * FCH], F32)
        frows_b = sb(ffn, "frows_b", [1, 4 * FCH], BF16)

        l3v = ln3Tc[:].rearrange("p (c t) -> p c t", c=16)
        p2map = [(PW0[:, 0:512], PW0[:, 512:1024]), (PW1[:, 0:512], PW1[:, 512:1024]),
                 (P0[:], P1[:]), (P2[:], P3[:])]
        p2res = [[("PW0", 0), ("PW0", 1)], [("PW1", 0), ("PW1", 1)],
                 [("P0",), ("P1",)], [("P2",), ("P3",)]]
        p2full = [PW0[:], PW1[:], None, None]

        def ffn_final(c):
            for tb in range(4):
                tbg = c * 4 + tb
                dma("sync", fx1[:], x1_dram[tbg * 128:(tbg + 1) * 128, :],
                    reads=[("x1d",)], writes=[("fx1",)], stream=("fx1", "slot"))
                for half in range(2):
                    pw = PW0 if half == 0 else PW1
                    pwn = "PW0" if half == 0 else "PW1"
                    for j in range(8):
                        dt = half * 8 + j
                        tr(pw[:, j * 128:(j + 1) * 128],
                           faoT[:, dt * FCH + tb * 128: dt * FCH + tb * 128 + 128], ident_f[:],
                           reads=[("faoT",), ("identf",)], writes=[(pwn, j // 4)])
                    vop("scalar_tensor_tensor", fx1[:, half * 1024:(half + 1) * 1024],
                        pw[:], 1.0, fx1[:, half * 1024:(half + 1) * 1024], ALU.mult, ALU.add,
                        reads=[(pwn, 0), (pwn, 1), ("fx1",)], writes=[("fx1",)])
                dma("sync", y_d[tbg * 128:(tbg + 1) * 128, :], fx1[:],
                    reads=[("fx1",)], writes=[("y",)], stream=("y", "group"))


        for c in range(NFCH):
            t0 = c * FCH
            dma("sync", l3v, ln3T_dram[:, :, t0:t0 + FCH].transpose([1, 0, 2]),
                reads=[("ln3Td", 0), ("ln3Td", 1)], writes=[("ln3Tc",)], stream=("l3c", "slot"))
            for lt in range(8):
                cb = cbst[lt % 2]
                cbres = ("cbst", lt % 2)
                dma("sync", cb[:].rearrange("p (c o) -> p c o", c=16), down_c[lt, :, :, :],
                    reads=[("c_down_c",)], writes=[cbres], stream=(f"cb{lt % 2}", "slot"))
                pp = PS[lt % 4]
                pres = PN[lt % 4]
                for dc in range(16):
                    mm(pp[:], cb[:, dc * 128:(dc + 1) * 128], l3v[:, dc, :], dc == 0, dc == 15,
                       reads=[cbres, ("ln3Tc",)], writes=[pres])
                act(g_sb[:, lt * FCH:(lt + 1) * FCH], pp[:], AF.Gelu, bias=downb[:, lt:lt + 1],
                    reads=[pres, ("downb",)], writes=[("g", lt)])
            for dc in range(16):
                act(gxT[:, dc * FCH:(dc + 1) * FCH], l3v[:, dc, :], AF.Gelu,
                    reads=[("ln3Tc",)], writes=[("gxT",)])
            if c > 0:
                ffn_final(c - 1)
            for tsub in range(4):
                tbg = c * 4 + tsub
                for half in range(2):
                    mm(p2map[tsub][half], wT_sb[:, tbg * 128:(tbg + 1) * 128],
                       ustack[:, half * 512:(half + 1) * 512], True, True,
                       reads=[("wT",), ("wTones",), ("ustack",)],
                       writes=[p2res[tsub][half]])
                if p2full[tsub] is not None:
                    vop("tensor_copy", acc_sb[:, tsub * LD:(tsub + 1) * LD], p2full[tsub],
                        reads=p2res[tsub], writes=[("acc", tsub)])
                else:
                    vop("tensor_copy", acc_sb[:, tsub * LD:tsub * LD + 512], p2map[tsub][0],
                        reads=[p2res[tsub][0]], writes=[("acc", tsub)])
                    vop("tensor_copy", acc_sb[:, tsub * LD + 512:(tsub + 1) * LD], p2map[tsub][1],
                        reads=[p2res[tsub][1]], writes=[("acc", tsub)])

            def expert(nh, w1c4, w2cp, b1c0, gate_col, gate_const):
                for hcx in range(nh):
                    cb = cbst[hcx % 2]
                    cbres = ("cbst", hcx % 2)
                    dma("sync", cb[:, 0:8 * 128].rearrange("p (c o) -> p c o", c=8),
                        w1c4[hcx, :, :, :], reads=[("c_" + w1c4.name,)], writes=[cbres],
                        stream=(f"cb{hcx % 2}", "slot"))
                    pp = PS[hcx % 2]
                    pres = PN[hcx % 2]
                    for lt in range(8):
                        mm(pp[:], cb[:, lt * 128:(lt + 1) * 128], g_sb[:, lt * FCH:(lt + 1) * FCH],
                           lt == 0, lt == 7, reads=[cbres, ("g", lt)], writes=[pres])
                    act(sc32[:, hcx * FCH:(hcx + 1) * FCH], pp[:], AF.Gelu,
                        bias=b1all[:, b1c0 + hcx:b1c0 + hcx + 1],
                        reads=[pres, ("b1all",)], writes=[("sc32", hcx)])
                for hcx in range(nh):
                    slw = hcx % 4
                    dma("sync", w2st[slw][:], w2cp[hcx * 128:(hcx + 1) * 128, :],
                        reads=[("c_" + w2cp.name,)], writes=[("w2st", slw)],
                        stream=(f"w2{slw}", "slot"))
                    for tsub in range(4):
                        for half in range(2):
                            mm(p2map[tsub][half],
                               sc32[:, hcx * FCH + tsub * 128: hcx * FCH + tsub * 128 + 128],
                               w2st[slw][:, half * 512:(half + 1) * 512],
                               hcx == 0, hcx == nh - 1,
                               reads=[("sc32", hcx), ("w2st", slw)],
                               writes=[p2res[tsub][half]])
                for tsub in (2, 3, 0, 1):
                    tbg = c * 4 + tsub
                    gc = (w_tm[:, tbg * 9 + gate_col: tbg * 9 + gate_col + 1]
                          if gate_col is not None else gate_const)
                    grd = [("w_tm",)] if gate_col is not None else []
                    if p2full[tsub] is not None:
                        vop("scalar_tensor_tensor", acc_sb[:, tsub * LD:(tsub + 1) * LD],
                            p2full[tsub], gc, acc_sb[:, tsub * LD:(tsub + 1) * LD],
                            ALU.mult, ALU.add,
                            reads=p2res[tsub] + grd + [("acc", tsub)], writes=[("acc", tsub)])
                    else:
                        for half in range(2):
                            vop("scalar_tensor_tensor",
                                acc_sb[:, tsub * LD + half * 512: tsub * LD + (half + 1) * 512],
                                p2map[tsub][half], gc,
                                acc_sb[:, tsub * LD + half * 512: tsub * LD + (half + 1) * 512],
                                ALU.mult, ALU.add,
                                reads=[p2res[tsub][half]] + grd + [("acc", tsub)],
                                writes=[("acc", tsub)])

            for e in range(NEXP):
                expert(NH[e], w1_c[e], w2_c[e], b1ofs[e], e, None)
            expert(8, sw1_c, sw2_c, b1ofs[NEXP], None, 0.1)

            for lt in range(8):
                pp = PS[2 + lt % 2]
                pres = PN[2 + lt % 2]
                for tsub in range(4):
                    tr(pp[:, tsub * 128:(tsub + 1) * 128],
                       acc_sb[:, tsub * LD + lt * 128: tsub * LD + lt * 128 + 128], ident_f[:],
                       reads=[("acc", tsub), ("identf",)], writes=[pres])
                scopy(accT[:, lt * FCH:(lt + 1) * FCH], pp[:], reads=[pres], writes=[("accT",)])

            for dt in range(16):
                cbu = cbst[dt % 2]
                cbures = ("cbst", dt % 2)
                dma("sync", cbu[:, 0:8 * 128].rearrange("p (c o) -> p c o", c=8),
                    up_c[dt, :, :, :], reads=[("c_up_c",)], writes=[cbures],
                    stream=(f"cb{dt % 2}", "slot"))
                cbc = cbst[2]
                cbcres = ("cbst", 2)
                dma("sync", cbc[:].rearrange("p (c o) -> p c o", c=16), core_c[dt, :, :, :],
                    reads=[("c_core_c",)], writes=[cbcres], stream=("cb2", "slot"))
                pp = PS[dt % 4]
                pres = PN[dt % 4]
                for lt in range(8):
                    mm(pp[:], cbu[:, lt * 128:(lt + 1) * 128], accT[:, lt * FCH:(lt + 1) * FCH],
                       lt == 0, False, reads=[cbures, ("accT",)], writes=[pres])
                for dc in range(16):
                    mm(pp[:], cbc[:, dc * 128:(dc + 1) * 128], gxT[:, dc * FCH:(dc + 1) * FCH],
                       False, dc == 15, reads=[cbcres, ("gxT",)], writes=[pres])
                s.op("scalar",
                     (lambda o, i, b: lambda en: en.activation(o, i, AF.Identity, bias=b, scale=1.0))(
                         faoT[:, dt * FCH:(dt + 1) * FCH], pp[:], bcol[:, dt:dt + 1]),
                     reads=[pres, ("bcol",)], writes=[("faoT",)])
                act(sc32[:, (16 + dt) * FCH:(17 + dt) * FCH],
                    faoT[:, dt * FCH:(dt + 1) * FCH], AF.Square,
                    reads=[("faoT",)], writes=[("sc32", 16 + dt)])
                vop("tensor_copy", sc32[:, dt * FCH:(dt + 1) * FCH],
                    faoT[:, dt * FCH:(dt + 1) * FCH], reads=[("faoT",)], writes=[("sc32", dt)])

            for dc in range(16):
                mm(P2[0:1, 0:FCH], onesc[:], sc32[:, dc * FCH:(dc + 1) * FCH], dc == 0, dc == 15,
                   reads=[("onesc",), ("sc32", dc)], writes=[("P2",)])
                mm(P3[0:1, 0:FCH], onesc[:], sc32[:, (16 + dc) * FCH:(17 + dc) * FCH],
                   dc == 0, dc == 15, reads=[("onesc",), ("sc32", 16 + dc)], writes=[("P3",)])
            fm = frows[0:1, 0:FCH]
            fv = frows[0:1, FCH:2 * FCH]
            ft = frows[0:1, 2 * FCH:3 * FCH]
            vop("tensor_scalar", fm, P2[0:1, 0:FCH], 1.0 / D, None, ALU.mult,
                reads=[("P2",)], writes=[("fm",)])
            vop("tensor_scalar", fv, P3[0:1, 0:FCH], 1.0 / D, None, ALU.mult,
                reads=[("P3",)], writes=[("fv",)])
            vop("tensor_tensor", ft, fm, fm, ALU.mult, reads=[("fm",)], writes=[("ft",)])
            vop("tensor_sub", fv, fv, ft, reads=[("fv",), ("ft",)], writes=[("fv",)])
            act(fv, fv, AF.Ln, bias=EPS, reads=[("fv",)], writes=[("fv",)])
            act(fv, fv, AF.Exp, scale=-0.5, reads=[("fv",)], writes=[("fv",)])
            rows_hilo(fm, FCH, frows_b[0:1, 0:FCH], frows_b[0:1, FCH:2 * FCH], ft, ("fm",), ("ftF",))
            mm(P2[:, 0:FCH], ones1[:], frows_b[0:1, 0:FCH], True, False,
               reads=[("ones1",), (("fm",), "hi")], writes=[("P2",)])
            mm(P2[:, 0:FCH], ones1[:], frows_b[0:1, FCH:2 * FCH], False, True,
               reads=[("ones1",), (("fm",), "lo")], writes=[("P2",)])
            rows_hilo(fv, FCH, frows_b[0:1, 2 * FCH:3 * FCH], frows_b[0:1, 3 * FCH:4 * FCH],
                      ft, ("fv",), ("ftF",))
            mm(P3[:, 0:FCH], ones1[:], frows_b[0:1, 2 * FCH:3 * FCH], True, False,
               reads=[("ones1",), (("fv",), "hi")], writes=[("P3",)])
            mm(P3[:, 0:FCH], ones1[:], frows_b[0:1, 3 * FCH:4 * FCH], False, True,
               reads=[("ones1",), (("fv",), "lo")], writes=[("P3",)])
            for dt in range(16):
                sl = dt % 2
                vop("tensor_sub", fntmp[sl][:], faoT[:, dt * FCH:(dt + 1) * FCH], P2[:, 0:FCH],
                    reads=[("faoT",), ("P2",)], writes=[("fntmp", sl)])
                vop("tensor_mul", faoT[:, dt * FCH:(dt + 1) * FCH], fntmp[sl][:], P3[:, 0:FCH],
                    reads=[("fntmp", sl), ("P3",)], writes=[("faoT",)])

        ffn_final(NFCH - 1)
        _close(ffn)
    except _StopBuild:
        while _open:
            _open.pop().close()

    with nc.Block() as block:
        s.emit(block, nc, top)
    top.close()
    return nc


_NC_CACHE = []


def kernel(**inputs):
    ii = {k: np.asarray(v) for k, v in inputs.items()}
    hcs = host_constants()
    x = np.ascontiguousarray(ii["L0_attn_x"], dtype=np.float32)

    inm = {}
    qp = q_perm()
    inm["wqT"] = np.ascontiguousarray(np.asarray(ii["L0_attn_Wq"], np.float32).T[:, qp])
    inm["wkT"] = np.ascontiguousarray(ii["L0_attn_Wk"].T)
    inm["wvT"] = np.ascontiguousarray(ii["L0_attn_Wv"].T)
    inm["woT"] = np.ascontiguousarray(np.asarray(ii["L0_attn_Wo"], np.float32).T[qp, :])
    inm["down_w"] = np.ascontiguousarray(ii["L0_ffn_down_w"])
    inm["up_w"] = np.ascontiguousarray(ii["L0_ffn_up_w"])
    inm["core_w"] = np.ascontiguousarray(ii["L0_ffn_core_w"])
    inm["sw1"] = np.ascontiguousarray(ii["L0_ffn_shared_w1"])
    inm["sw2"] = np.ascontiguousarray(ii["L0_ffn_shared_w2"])
    for e in range(NEXP):
        inm[f"w1_{e}"] = np.ascontiguousarray(ii[f"L0_ffn_e{e}_w1"])
        inm[f"w2_{e}"] = np.ascontiguousarray(ii[f"L0_ffn_e{e}_w2"])
    inm["down_b_col"] = col_layout(ii["L0_ffn_down_b"])
    inm["up_b_col"] = col_layout(ii["L0_ffn_up_b"])
    inm["core_b_col"] = col_layout(ii["L0_ffn_core_b"])
    b1cols = [col_layout(ii[f"L0_ffn_e{e}_b1"]) for e in range(NEXP)]
    b1cols.append(col_layout(ii["L0_ffn_shared_b1"]))
    inm["b1all_col"] = np.ascontiguousarray(np.concatenate(b1cols, axis=1))
    inm["b2_cat_row"] = np.ascontiguousarray(
        np.concatenate([np.asarray(ii[f"L0_ffn_e{e}_b2"], np.float32)
                        for e in range(NEXP)]).reshape(1, -1))
    inm["sb2_row"] = np.ascontiguousarray(
        np.asarray(ii["L0_ffn_shared_b2"], np.float32).reshape(1, -1))
    inm["rb_row"] = np.ascontiguousarray(
        np.asarray(ii["L0_ffn_router_b"], np.float32).reshape(1, -1))
    inm["lamcost"] = hcs["lamcost"]
    rw = np.asarray(ii["L0_ffn_router_w"], np.float32)
    inm["router_w_t"] = np.ascontiguousarray(
        rw.reshape(16, 128, NEXP).transpose(1, 0, 2).reshape(128, 16 * NEXP))
    inm["cosT"] = hcs["cosT"]
    inm["sinT"] = hcs["sinT"]
    inm["m01"] = hcs["m01"]
    inm["m01p"] = hcs["m01p"]
    inm["m01d384"] = hcs["m01d384"]
    inm["m01d256"] = hcs["m01d256"]
    inm["ident"] = hcs["ident"]
    inm["permswap"] = hcs["permswap"]
    inm["constpool"] = np.ascontiguousarray(
        np.tile(np.array(CONST_VALS, np.float32)[None, :], (128, 1)))

    if not _NC_CACHE:
        _NC_CACHE.append(build_program())
    nc = _NC_CACHE[0]
    in_maps = []
    for core in range(8):
        m = dict(inm)
        m["x"] = np.ascontiguousarray(x[core])
        in_maps.append(m)
    res = run_bass_kernel_spmd(nc, in_maps, core_ids=list(range(8)))
    out = np.stack([np.asarray(res.results[i]["y"]) for i in range(8)], axis=0)
    return out.astype(np.float32)


# version 28
# speedup vs baseline: 1.2072x; 1.0999x over previous
import contextlib
import numpy as np
import concourse.bass as bass
import concourse.mybir as mybir
from concourse.bass_utils import run_bass_kernel_spmd

F32 = mybir.dt.float32
BF16 = mybir.dt.bfloat16
AF = mybir.ActivationFunctionType
ALU = mybir.AluOpType
AX = mybir.AxisListType

S = 2048
D = 2048
NQ, NKV, HDIM = 32, 8, 64
LD = 1024
HID = [1024, 2048, 3072, 4096, 1024, 2048, 3072, 4096]
NH = [h // 128 for h in HID]
NEXP = 8
EPS = 1e-5
ESHIFT = 24.0
ACH = 256
NACH = S // ACH
FCH = 512
NFCH = S // FCH
COST_LAMBDA = 1e-7
UPTO = "all"
NCORES = 8


class _StopBuild(Exception):
    pass

ENGINES = ("tensor", "vector", "scalar", "sync", "gpsimd")
COMPUTE = ("tensor", "vector", "scalar")
QUEUES = ("sync", "gpsimd")
PSUM_RES = {("P0",), ("P1",), ("P2",), ("P3",), ("PW0",), ("PW1",)}

CONST_VALS = [0.0, 1.0, -1.0, 0.125, -ESHIFT, EPS, 1.0 / D, -0.5, 0.1, -1000.0]


class Sched:

    def __init__(self):
        self.barrier_fns = {}
        self.ops = {e: [] for e in ENGINES}
        self.counts = {e: 0 for e in ENGINES}
        self.know = {e: {} for e in ENGINES}
        self.snap = {e: [] for e in ENGINES}
        self.res = {}
        self.pending = {e: {} for e in ENGINES}
        self.wait_targets = {e: set() for e in COMPUTE}
        self.streams = {}
        self.meta = {}

    def _stream_add(self, eng, idx, sid, kind):
        st = self.streams.setdefault(sid, {"kind": kind, "ops": [], "queue": eng})
        assert st["kind"] == kind, sid
        if kind == "group":
            assert st["queue"] == eng, (sid, "group streams must be single-queue")
        k = len(st["ops"])
        st["ops"].append((eng, idx))
        self.meta[(eng, idx)] = (sid, k)

    def _need(self, eng, waits, te, ti, raw=False):
        if te in COMPUTE:
            if te == eng and eng == "tensor":
                return
            if self.know[eng].get(te, 0) >= ti + 1:
                return
            waits[te] = max(waits.get(te, 0), ti + 1)
        else:
            sid, k = self.meta[(te, ti)]
            if self.know[eng].get(sid, 0) >= k + 1:
                return
            waits[sid] = max(waits.get(sid, 0), k + 1)

    def _learn(self, eng, waits):
        for key, c in list(waits.items()):
            if self.know[eng].get(key, 0) >= c:
                del waits[key]
                continue
            self.know[eng][key] = c
            if key in COMPUTE:
                st = self.snap[key][c - 1]
            else:
                we, wi = self.streams[key]["ops"][c - 1]
                st = self.snap[we][wi]
            for f, v in st.items():
                if f != eng and v > self.know[eng].get(f, 0):
                    self.know[eng][f] = v
            if key in COMPUTE:
                self.wait_targets[key].add(c - 1)

    def op(self, eng, fn, reads=(), writes=(), stream=None):
        waits = dict(self.pending[eng])
        exact = set(waits.keys())
        self.pending[eng] = {}
        for r in reads:
            rec = self.res.get(r)
            if rec and rec["w"] is not None:
                self._need(eng, waits, *rec["w"], raw=True)
            if rec and r in PSUM_RES and eng in ("scalar", "vector"):
                for (re_, ri) in rec["r"]:
                    if re_ != eng and re_ in ("scalar", "vector"):
                        self._need(eng, waits, re_, ri)
        for r in writes:
            rec = self.res.get(r)
            if rec:
                if rec["w"] is not None:
                    self._need(eng, waits, *rec["w"])
                for (re_, ri) in rec["r"]:
                    self._need(eng, waits, re_, ri)
        exact &= set(waits.keys())
        self._learn(eng, waits)
        exact &= set(waits.keys())
        idx = self.counts[eng]
        self.ops[eng].append((fn, waits, exact))
        self.counts[eng] = idx + 1
        if eng in QUEUES:
            assert stream is not None, "queue ops need a stream"
            self._stream_add(eng, idx, stream[0], stream[1])
            st = dict(self.know[eng])
            sid, k = self.meta[(eng, idx)]
            if self.streams[sid]["kind"] == "slot":
                st[sid] = max(st.get(sid, 0), k + 1)
        else:
            st = dict(self.know[eng])
            st[eng] = idx + 1
        self.snap[eng].append(st)
        for r in reads:
            rec = self.res.setdefault(r, {"w": None, "r": []})
            rec["r"].append((eng, idx))
        for r in writes:
            self.res[r] = {"w": (eng, idx), "r": []}
        return (eng, idx)

    def barrier(self, engines=("tensor", "vector", "scalar", "sync")):
        ccur = {e: self.counts[e] for e in COMPUTE if self.counts[e] > 0}
        scur = {sid: len(st["ops"]) for sid, st in self.streams.items()
                if st["ops"] and st["queue"] in engines}
        for e in engines:
            if e in QUEUES:
                continue
            waits = {}
            for f, c in ccur.items():
                if f != e and self.know[e].get(f, 0) < c:
                    waits[f] = c
            for sid, c in scur.items():
                if self.know[e].get(sid, 0) < c:
                    waits[sid] = c
            exact = set(k for k in waits if k not in COMPUTE)
            self._learn(e, waits)
            exact &= set(waits.keys())
            idx = self.counts[e]
            bfn = self.barrier_fns.get(e, lambda en: en.nop())
            self.ops[e].append((bfn, waits, exact))
            self.counts[e] = idx + 1
            st = dict(self.know[e])
            st[e] = idx + 1
            self.snap[e].append(st)
        post = {e: self.counts[e] for e in COMPUTE if self.counts[e] > 0}
        for e in QUEUES:
            for f, c in post.items():
                self.pending[e][f] = max(self.pending[e].get(f, 0), c)

    def mark_final(self, eng_idx):
        pass

    def emit(self, block, nc, top):
        sems = {}
        for e in COMPUTE:
            sems[e] = top.enter_context(nc.semaphore(f"sem_{e}"))
        for sid in self.streams:
            sems[sid] = top.enter_context(nc.semaphore(f"sem_s_{sid}"))
        incmap = {}
        for e in COMPUTE:
            m = {}
            for j, oi in enumerate(sorted(self.wait_targets[e])):
                m[oi] = j + 1
            incmap[e] = m
        totals = {sid: len(st["ops"]) for sid, st in self.streams.items()}

        def replay(e, eng):
            for i, (fn, waits, exact) in enumerate(self.ops[e]):
                for key in sorted(waits, key=str):
                    c = waits[key]
                    if key in COMPUTE:
                        eng.wait_ge(sems[key], incmap[key][c - 1])
                    else:
                        st = self.streams[key]
                        if st["kind"] == "slot" or key in exact:
                            eng.wait_ge(sems[key], 16 * c)
                        elif st["queue"] == e:
                            m = sum(1 for (se, si) in st["ops"] if si < i)
                            assert m >= c, (key, c, m)
                            eng.wait_ge(sems[key], 16 * m)
                        else:
                            eng.wait_ge(sems[key], 16 * totals[key])
                inst = fn(eng)
                if e in COMPUTE:
                    if i in incmap[e]:
                        inst.then_inc(sems[e], 1)
                else:
                    sid, k = self.meta[(e, i)]
                    inst.then_inc(sems[sid], 16)
            if e in QUEUES:
                done = set()
                for i in range(len(self.ops[e])):
                    sid, k = self.meta[(e, i)]
                    done.add(sid)
                for sid in sorted(done, key=str):
                    eng.wait_ge(sems[sid], 16 * totals[sid])

        @block.tensor
        def _(eng):
            replay("tensor", eng)

        @block.vector
        def _(eng):
            replay("vector", eng)

        @block.scalar
        def _(eng):
            replay("scalar", eng)

        @block.sync
        def _(eng):
            replay("sync", eng)

        @block.gpsimd
        def _(eng):
            replay("gpsimd", eng)


def host_constants():
    c = {}
    k = np.arange(0, HDIM, 2)[: HDIM // 2].astype(np.float64)
    inv_freq = 1.0 / (10000.0 ** (k / HDIM))
    pos = np.arange(S, dtype=np.float64)
    ang = pos[:, None] * inv_freq[None, :]
    ang2 = np.concatenate([ang, ang], axis=1)
    cos64 = np.cos(ang2).T
    sin64 = np.sin(ang2).T
    sgn = np.ones((64, 1))
    sgn[:32] = -1.0
    sins = sin64 * sgn
    c["cosT"] = np.ascontiguousarray(np.concatenate([cos64, cos64], 0).astype(np.float32))
    c["sinT"] = np.ascontiguousarray(np.concatenate([sins, sins], 0).astype(np.float32))
    p = np.arange(128)[:, None]
    j = np.arange(896)[None, :] - 384
    m01 = (j >= p).astype(np.float32)
    c["m01"] = np.ascontiguousarray(m01)
    c["m01p"] = np.ascontiguousarray(
        np.concatenate([m01[:, 384:640], m01[:, 256:512]], axis=1))
    c["m01d384"] = np.ascontiguousarray(np.tile(m01[:, 384:640], (1, 2)))
    c["m01d256"] = np.ascontiguousarray(np.tile(m01[:, 256:512], (1, 2)))
    c["ident"] = np.eye(128, dtype=np.float32)
    pm = np.zeros((128, 128), dtype=np.float32)
    idx = np.arange(128)
    pm[idx ^ 32, idx] = 1.0
    c["permswap"] = pm
    c["lamcost"] = (np.float32(COST_LAMBDA) * np.array(
        [np.float32(LD * h + h * LD) for h in HID], dtype=np.float32)).reshape(1, 8)
    return c


def col_layout(b):
    return np.ascontiguousarray(np.asarray(b, np.float32).reshape(-1, 128).T)


def qplace(h):
    g = h // 4
    return (g // 2) * 4 + (h % 4), 64 * (g % 2)


def q_perm():
    pos = np.empty(D, dtype=np.int64)
    for h in range(NQ):
        t, off = qplace(h)
        pos[t * 128 + off: t * 128 + off + 64] = np.arange(h * 64, (h + 1) * 64)
    return pos


def build_program():
    nc = bass.Bass("TRN2", target_bir_lowering=False)
    s = Sched()

    def _ckpt(name):
        if UPTO == name:
            raise _StopBuild()
    top = contextlib.ExitStack()

    def sb(stk, name, shape, dtype):
        return stk.enter_context(nc.sbuf_tensor(name, shape, dtype))

    def dma(q, out, in_, reads=(), writes=(), stream=None):
        return s.op(q, (lambda o, i: lambda e: e.dma_start(out=o, in_=i))(out, in_),
                    reads=reads, writes=writes, stream=stream)

    def mm(out, lhsT, rhs, start, stop, reads=(), writes=()):
        return s.op("tensor",
                    (lambda o, l, r, st, sp: lambda e: e.matmul(o, l, r, start=st, stop=sp))(
                        out, lhsT, rhs, start, stop), reads=reads, writes=writes)

    def tr(out, in_, ident, reads=(), writes=()):
        return s.op("tensor", (lambda o, i, d: lambda e: e.transpose(o, i, d))(out, in_, ident),
                    reads=reads, writes=writes)

    def act(out, in_, func, bias=0.0, scale=1.0, reads=(), writes=()):
        return s.op("scalar",
                    (lambda o, i, f, b, sc: lambda e: e.activation(o, i, f, bias=b, scale=sc))(
                        out, in_, func, bias, scale), reads=reads, writes=writes)

    def scopy(out, in_, reads=(), writes=()):
        return s.op("scalar", (lambda o, i: lambda e: e.copy(o, i))(out, in_),
                    reads=reads, writes=writes)

    def vop(name, *args, reads=(), writes=()):
        return s.op("vector", (lambda n, a: lambda e: getattr(e, n)(*a))(name, args),
                    reads=reads, writes=writes)

    def memset(ap, val, writes=()):
        return s.op("vector", (lambda a, v: lambda e: e.memset(a, v))(ap, val), writes=writes)

    def din(name, shape):
        return nc.dram_tensor(name, list(shape), F32, kind="ExternalInput").ap()

    def dint(name, shape, dt):
        return nc.dram_tensor(name, list(shape), dt, kind="Internal").ap()

    x_d = din("x", (S, D))
    y_d = nc.dram_tensor("y", [S, D], F32, kind="ExternalOutput").ap()
    wqT_d = din("wqT", (D, D))
    wkT_d = din("wkT", (D, NKV * HDIM))
    wvT_d = din("wvT", (D, NKV * HDIM))
    woT_d = din("woT", (D, D))
    down_d = din("down_w", (D, LD))
    up_d = din("up_w", (LD, D))
    core_d = din("core_w", (D, D))
    sw1_d = din("sw1", (LD, LD))
    sw2_d = din("sw2", (LD, LD))
    w1_d = [din(f"w1_{e}", (LD, HID[e])) for e in range(NEXP)]
    w2_d = [din(f"w2_{e}", (HID[e], LD)) for e in range(NEXP)]
    downb_d = din("down_b_col", (128, LD // 128))
    upb_d = din("up_b_col", (128, D // 128))
    coreb_d = din("core_b_col", (128, D // 128))
    b1all_d = din("b1all_col", (128, sum(NH) + 8))
    b2row_d = din("b2_cat_row", (1, NEXP * LD))
    sb2row_d = din("sb2_row", (1, LD))
    rbrow_d = din("rb_row", (1, NEXP))
    lam_d = din("lamcost", (1, NEXP))
    rwt_d = din("router_w_t", (128, 16 * NEXP))
    cosT_d = din("cosT", (128, S))
    sinT_d = din("sinT", (128, S))
    m01_d = din("m01", (128, 896))
    m01p_d = din("m01p", (128, 512))
    m01d384_d = din("m01d384", (128, 512))
    m01d256_d = din("m01d256", (128, 512))
    ident_d = din("ident", (128, 128))
    perm_d = din("permswap", (128, 128))
    consts_d = din("constpool", (128, len(CONST_VALS)))

    x1_dram = dint("x1_scr", (S, D), F32)
    ln1T_dram = dint("ln1T_scr", (16, 128, S), BF16)
    ln3T_dram = dint("ln3T_scr", (16, 128, S), BF16)
    wq_c = dint("wq_c", (16, 128, 16, 128), BF16)
    wo_c = dint("wo_c", (16, 128, 16, 128), BF16)
    down_c = dint("down_c", (8, 128, 16, 128), BF16)
    up_c = dint("up_c", (16, 128, 8, 128), BF16)
    core_c = dint("core_c", (16, 128, 16, 128), BF16)
    sw1_c = dint("sw1_c", (8, 128, 8, 128), BF16)
    w1_c = [dint(f"w1c_{e}", (NH[e], 128, 8, 128), BF16) for e in range(NEXP)]
    sw2_c = dint("sw2_c", (LD, LD), BF16)
    w2_c = [dint(f"w2c_{e}", (HID[e], LD), BF16) for e in range(NEXP)]
    ustack_dram = dint("ustack_scr", (10, LD), BF16)

    cpool = sb(top, "cpool", [128, len(CONST_VALS)], F32)
    for i, v in enumerate(CONST_VALS):
        nc.const_aps.register(F32, float(v), cpool[:, i:i + 1])
    ident_b = sb(top, "ident_b", [128, 128], BF16)
    ident_f = sb(top, "ident_f", [128, 128], F32)
    perm_b = sb(top, "perm_b", [128, 128], BF16)
    ones1 = sb(top, "ones1", [1, 128], BF16)
    onesc = sb(top, "onesc", [128, 1], BF16)
    m01_b = sb(top, "m01_b", [128, 896], BF16)
    m01p_b = sb(top, "m01p_b", [128, 512], BF16)
    m01d_b = [sb(top, f"m01d{i}_b", [128, 512], BF16) for i in range(2)]
    cvt = [sb(top, f"cvt{i}", [128, 2048], BF16) for i in range(2)]
    cbst = [sb(top, f"cbst{i}", [128, 16 * 128], BF16) for i in range(4)]
    w2st = [sb(top, f"w2st{i}", [128, LD], BF16) for i in range(4)]
    downb = sb(top, "downb", [128, LD // 128], F32)
    bcol = sb(top, "bcol", [128, D // 128], F32)
    colt = sb(top, "colt", [128, 16], F32)
    b1all = sb(top, "b1all", [128, sum(NH) + 8], F32)
    gb1 = sb(top, "gb1", [128, sum(NH) + 8], BF16)
    rw_sb = sb(top, "rw_sb", [128, 16 * NEXP], F32)
    rw_hi = sb(top, "rw_hi", [128, 16 * NEXP], BF16)
    rw_lo = sb(top, "rw_lo", [128, 16 * NEXP], BF16)
    rbl = sb(top, "rbl", [1, 4 * NEXP], F32)
    rbl_hi = sb(top, "rbl_hi", [1, NEXP], BF16)
    rbl_lo = sb(top, "rbl_lo", [1, NEXP], BF16)
    w_tm = sb(top, "w_tm", [128, 16 * 9], F32)
    wT_sb = sb(top, "wT_sb", [10, S], BF16)
    ustack = sb(top, "ustack", [10, LD], BF16)
    crow = sb(top, "crow", [1, LD], F32)
    urow = sb(top, "urow", [1, LD], F32)

    barc = sb(top, "barc", [1, 2], F32)
    s.barrier_fns = {
        "vector": (lambda en: en.memset(barc[0:1, 0:1], 0.0)),
        "scalar": (lambda en: en.memzero(barc[0:1, 1:2])),
    }
    P0 = top.enter_context(nc.psum_tensor("P0", [128, 512], F32))
    P1 = top.enter_context(nc.psum_tensor("P1", [128, 512], F32))
    P2 = top.enter_context(nc.psum_tensor("P2", [128, 512], F32))
    P3 = top.enter_context(nc.psum_tensor("P3", [128, 512], F32))
    PW0 = top.enter_context(nc.psum_tensor("PW0", [128, 1024], F32))
    PW1 = top.enter_context(nc.psum_tensor("PW1", [128, 1024], F32))
    PS = [P0, P1, P2, P3]
    PN = [("P0",), ("P1",), ("P2",), ("P3",)]

    print('SBUF free after static:', nc.sbuf_bytes_remaining)
    _open = []

    def _mk():
        st = contextlib.ExitStack()
        _open.append(st)
        return st

    def _close(st):
        assert _open and _open[-1] is st
        _open.pop()
        st.close()

    try:
        init = _mk()
        rowtmp = sb(init, "rowtmp", [1, NEXP * LD], F32)
        dma("gpsimd", ident_b[:], ident_d, writes=[("identb",)], stream=("gi", "group"))
        dma("sync", ident_f[:], ident_d, writes=[("identf",)], stream=("si", "group"))
        dma("gpsimd", perm_b[:], perm_d, writes=[("permb",)], stream=("gi", "group"))
        dma("gpsimd", m01_b[:], m01_d, writes=[("m01",)], stream=("gi", "group"))
        dma("gpsimd", m01p_b[:], m01p_d, writes=[("m01p",)], stream=("gi", "group"))
        dma("gpsimd", m01d_b[0][:], m01d384_d, writes=[("m01d", 0)], stream=("gi", "group"))
        dma("gpsimd", m01d_b[1][:], m01d256_d, writes=[("m01d", 1)], stream=("gi", "group"))
        dma("sync", cpool[:], consts_d, writes=[("cpool",)], stream=("si", "group"))
        dma("sync", downb[:], downb_d, writes=[("downb",)], stream=("si", "group"))
        dma("sync", bcol[:], upb_d, writes=[("bcol",)], stream=("si", "group"))
        dma("sync", colt[:], coreb_d, writes=[("coltc",)], stream=("si", "group"))
        vop("tensor_add", bcol[:], bcol[:], colt[:], reads=[("bcol",), ("coltc",)],
            writes=[("bcol",)])
        dma("sync", b1all[:], b1all_d, writes=[("b1all",)], stream=("si", "group"))
        act(gb1[:], b1all[:], AF.Gelu, reads=[("b1all",)], writes=[("gb1",)])
        dma("sync", rowtmp[:], b2row_d, writes=[("rowtmp",)], stream=("si", "group"))
        dma("sync", rbl[0:1, 0:NEXP], rbrow_d, writes=[("rbl",)], stream=("si", "group"))
        dma("sync", rbl[0:1, 16:16 + NEXP], lam_d, writes=[("rbllam",)], stream=("si", "group"))
        vop("tensor_sub", rbl[0:1, 0:NEXP], rbl[0:1, 0:NEXP], rbl[0:1, 16:16 + NEXP],
            reads=[("rbl",), ("rbllam",)], writes=[("rbl",)])
        scopy(rbl_hi[:], rbl[0:1, 0:NEXP], reads=[("rbl",)], writes=[("rblhi",)])
        scopy(rbl[0:1, 8:8 + NEXP], rbl_hi[:], reads=[("rblhi",)], writes=[("rblhf",)])
        vop("tensor_sub", rbl_lo[:], rbl[0:1, 0:NEXP], rbl[0:1, 8:8 + NEXP],
            reads=[("rbl",), ("rblhf",)], writes=[("rbllo",)])
        dma("sync", rw_sb[:], rwt_d, writes=[("rw",)], stream=("si", "group"))
        scopy(rw_hi[:], rw_sb[:], reads=[("rw",)], writes=[("rwhi",)])
        vop("tensor_sub", rw_lo[:], rw_sb[:], rw_hi[:], reads=[("rw",), ("rwhi",)],
            writes=[("rwlo",)])
        memset(ones1[:], 1.0, writes=[("ones1",)])
        memset(onesc[:], 1.0, writes=[("onesc",)])
        memset(wT_sb[0:10, :], 1.0, writes=[("wTones",)])
        vop("tensor_add", rowtmp[0:1, 0:4096], rowtmp[0:1, 0:4096], rowtmp[0:1, 4096:8192],
            reads=[("rowtmp",)], writes=[("rowtmp",)])
        vop("tensor_add", rowtmp[0:1, 0:2048], rowtmp[0:1, 0:2048], rowtmp[0:1, 2048:4096],
            reads=[("rowtmp",)], writes=[("rowtmp",)])
        vop("tensor_add", crow[0:1, :], rowtmp[0:1, 0:1024], rowtmp[0:1, 1024:2048],
            reads=[("rowtmp",)], writes=[("crow",)])
        cvt_i = [0]

        def convert_tiled(src, dst4, R, C):
            for c0 in range(R // 128):
                for cc in range(0, C, 2048):
                    w = min(2048, C - cc)
                    sl = cvt_i[0] % 2
                    cvt_i[0] += 1
                    rid = ("cvt", sl)
                    dma("gpsimd", cvt[sl][:, 0:w], src[c0 * 128:(c0 + 1) * 128, cc:cc + w],
                        writes=[rid], stream=(f"cvt{sl}", "slot"))
                    dma("gpsimd", dst4[cc // 128:(cc + w) // 128, :, c0, :].transpose([1, 0, 2]),
                        cvt[sl][:, 0:w].rearrange("p (j o) -> p j o", o=128),
                        reads=[rid], writes=[("c_" + dst4.name,)],
                        stream=("c_" + dst4.name, "group"))

        def convert_plain(src, dst, R, C):
            for c0 in range(R // 128):
                sl = cvt_i[0] % 2
                cvt_i[0] += 1
                rid = ("cvt", sl)
                dma("gpsimd", cvt[sl][:, 0:C], src[c0 * 128:(c0 + 1) * 128, :], writes=[rid],
                    stream=(f"cvt{sl}", "slot"))
                dma("gpsimd", dst[c0 * 128:(c0 + 1) * 128, :], cvt[sl][:, 0:C],
                    reads=[rid], writes=[("c_" + dst.name,)], stream=("c_" + dst.name, "group"))

        convert_tiled(wqT_d, wq_c, D, D)
        convert_tiled(woT_d, wo_c, D, D)
        for e in range(NEXP):
            convert_plain(w2_d[e], w2_c[e], HID[e], LD)
        convert_tiled(down_d, down_c, D, LD)
        convert_plain(sw2_d, sw2_c, LD, LD)
        convert_tiled(sw1_d, sw1_c, LD, LD)
        convert_tiled(up_d, up_c, LD, D)
        convert_tiled(core_d, core_c, D, D)
        for e in range(NEXP):
            convert_tiled(w1_d[e], w1_c[e], LD, HID[e])
        s.barrier()
        _close(init)

        spanKV = _mk()
        K_sb = sb(spanKV, "K_sb", [128, 4 * S], BF16)
        V_sb = sb(spanKV, "V_sb", [128, 16 * 520], BF16)

        span01 = _mk()
        ln1T = sb(span01, "ln1T", [128, 16 * S], BF16)

        a0 = _mk()
        x_t = [sb(a0, f"x_t{i}", [128, D], F32) for i in range(2)]
        ln1f = [sb(a0, f"ln1f{i}", [128, D], BF16) for i in range(2)]
        sqt = sb(a0, "sqt", [128, D], F32)
        colsA = sb(a0, "colsA", [128, 8], F32)

        def ln_token(xap, sqap, mcol, rcol, m2col, statres, xtag, sqtag):
            vop("reduce_sum", mcol, xap, AX.X, reads=[xtag], writes=[statres])
            act(sqap, xap, AF.Square, reads=[xtag], writes=[sqtag])
            vop("reduce_sum", rcol, sqap, AX.X, reads=[sqtag], writes=[statres])
            vop("tensor_scalar", mcol, mcol, 1.0 / D, None, ALU.mult,
                reads=[statres], writes=[statres])
            vop("tensor_scalar", rcol, rcol, 1.0 / D, None, ALU.mult,
                reads=[statres], writes=[statres])
            vop("tensor_tensor", m2col, mcol, mcol, ALU.mult, reads=[statres], writes=[statres])
            vop("tensor_sub", rcol, rcol, m2col, reads=[statres], writes=[statres])
            act(rcol, rcol, AF.Ln, bias=EPS, reads=[statres], writes=[statres])
            act(rcol, rcol, AF.Exp, scale=-0.5, reads=[statres], writes=[statres])

        lnv = ln1T[:].rearrange("p (c t) -> p c t", c=16)
        for tt in range(16):
            sl = tt % 2
            xa, la = x_t[sl], ln1f[sl]
            xr = ("x_t", sl)
            dma("sync", xa[:], x_d[tt * 128:(tt + 1) * 128, :], writes=[xr],
                stream=(f"xt{sl}", "slot"))
            stat = ("lnA", sl)
            mcol, rcol, m2col = colsA[:, 3 * sl:3 * sl + 1], colsA[:, 3 * sl + 1:3 * sl + 2], \
                colsA[:, 3 * sl + 2:3 * sl + 3]
            ln_token(xa[:], sqt[:], mcol, rcol, m2col, stat, xr, ("sqA",))
            vop("tensor_scalar", la[:], xa[:], mcol, rcol, ALU.subtract, ALU.mult,
                reads=[xr, stat], writes=[("ln1f", sl)])
            for half in range(2):
                pp = PS[2 + half]
                ppb = pp[:].bitcast(BF16)
                pres = PN[2 + half]
                for j in range(8):
                    dc = half * 8 + j
                    tr(ppb[:, j * 128:(j + 1) * 128], la[:, dc * 128:(dc + 1) * 128], ident_b[:],
                       reads=[("ln1f", sl), ("identb",)], writes=[pres])
                scopy(lnv[:, half * 8:(half + 1) * 8, tt * 128:(tt + 1) * 128],
                      ppb[:, 0:1024].rearrange("p (c t) -> p c t", c=8),
                      reads=[pres], writes=[("ln1T",)])

        s.barrier()
        _close(a0)
        _ckpt("a0")

        print('SBUF free at A1(before allocs):', nc.sbuf_bytes_remaining)
        a1 = _mk()
        wvst = sb(a1, "wvst", [128, 16 * 512], BF16)
        wvstf = [sb(a1, f"wvstf{i}", [128, 512], F32) for i in range(2)]
        kstf = [sb(a1, f"kstf{i}", [128, 16 * 128], F32) for i in range(2)]
        cosc = sb(a1, "cosc", [128, 512], F32)
        sinc = sb(a1, "sinc", [128, 512], F32)
        kpb = [sb(a1, f"kpb{i}", [128, 512], BF16) for i in range(2)]
        rp1 = [sb(a1, f"rp1_{i}", [128, 512], F32) for i in range(2)]
        rp2 = [sb(a1, f"rp2_{i}", [128, 512], F32) for i in range(2)]

        memset(V_sb[:, 64::65], 1.0, writes=[("Vones",)])
        for dc in range(16):
            dma("sync", wvstf[dc % 2][:], wvT_d[dc * 128:(dc + 1) * 128, :],
                writes=[("wvstf", dc % 2)], stream=(f"wvst{dc % 2}", "slot"))
            vop("tensor_copy", wvst[:, dc * 512:(dc + 1) * 512], wvstf[dc % 2][:],
                reads=[("wvstf", dc % 2)], writes=[("wvst", dc)])
        vv = V_sb[:].rearrange("p (c x) -> p c x", c=16)
        for tt in range(16):
            pp = PS[tt % 2]
            pres = PN[tt % 2]
            for dc in range(16):
                mm(pp[:], lnv[:, dc, tt * 128:(tt + 1) * 128], wvst[:, dc * 512:(dc + 1) * 512],
                   dc == 0, dc == 15, reads=[("ln1T",), ("wvst", dc)], writes=[pres])
            scopy(vv[:, tt, :].rearrange("p (h z) -> p h z", h=8)[:, :, 0:64],
                  pp[:].rearrange("p (h z) -> p h z", h=8),
                  reads=[pres], writes=[("V", tt)])

        _ckpt("a1v")
        for ot in range(4):
            cb = cbst[ot % 2]
            cbres = ("cbst", ot % 2)
            dma("sync", kstf[ot % 2][:].rearrange("p (c o) -> p c o", c=16),
                wkT_d[:, ot * 128:(ot + 1) * 128].rearrange("(c p) o -> p c o", p=128),
                writes=[("kstf", ot % 2)], stream=(f"kcb{ot % 2}", "slot"))
            vop("tensor_copy", cb[:], kstf[ot % 2][:],
                reads=[("kstf", ot % 2)], writes=[cbres])
            _ckpt("a1kd")
            for tq in range(4):
                dma("sync", cosc[:], cosT_d[:, tq * 512:(tq + 1) * 512], writes=[("cosc",)],
                    stream=("cosc", "slot"))
                _ckpt("a1cA")
                dma("sync", sinc[:], sinT_d[:, tq * 512:(tq + 1) * 512], writes=[("sinc",)],
                    stream=("sinc", "slot"))
                _ckpt("a1c")
                pp = PS[tq % 2]
                pres = PN[tq % 2]
                psw = PS[2 + tq % 2]
                psres = PN[2 + tq % 2]
                for dc in range(16):
                    mm(pp[:], cb[:, dc * 128:(dc + 1) * 128], lnv[:, dc, tq * 512:(tq + 1) * 512],
                       dc == 0, dc == 15, reads=[cbres, ("ln1T",)], writes=[pres])
                sl = tq % 2
                _ckpt("a1m")
                scopy(kpb[sl][:], pp[:], reads=[pres], writes=[("kpb", sl)])
                _ckpt("a1s")
                mm(psw[:], perm_b[:], kpb[sl][:], True, True,
                   reads=[("permb",), ("kpb", sl)], writes=[psres])
                _ckpt("a1p")
                vop("tensor_mul", rp1[sl][:], pp[:], cosc[:], reads=[pres, ("cosc",)],
                    writes=[("rp1", sl)])
                _ckpt("a1r1")
                vop("tensor_mul", rp2[sl][:], psw[:], sinc[:], reads=[psres, ("sinc",)],
                    writes=[("rp2", sl)])
                _ckpt("a1r2")
                vop("tensor_add", K_sb[:, ot * S + tq * 512: ot * S + (tq + 1) * 512],
                    rp1[sl][:], rp2[sl][:], reads=[("rp1", sl), ("rp2", sl)], writes=[("K",)])
                _ckpt("a1r")

            _ckpt("a1k1")
        _ckpt("a1k")
        for q4 in range(4):
            dma("sync", ln1T_dram[q4 * 4:(q4 + 1) * 4, :, :].transpose([1, 0, 2]),
                lnv[:, q4 * 4:(q4 + 1) * 4, :], reads=[("ln1T",)], writes=[("ln1Td",)],
                stream=("ln1Td", "group"))
        s.barrier()
        _close(a1)
        _close(span01)
        _ckpt("a1")

        print('SBUF free at A2(before allocs):', nc.sbuf_bytes_remaining)
        a2 = _mk()
        ln1Tc = [sb(a2, "ln1Tc0", [128, 16 * ACH], BF16)] * 2
        cosc2 = [sb(a2, f"cosc2_{i}", [128, ACH], F32) for i in range(2)]
        sinc2 = [sb(a2, f"sinc2_{i}", [128, ACH], F32) for i in range(2)]
        QTc = sb(a2, "QTc", [128, 16 * ACH], BF16)
        qpb = [sb(a2, f"qpb{i}", [128, ACH], BF16) for i in range(2)]
        qr1 = [sb(a2, "qr1_0", [128, ACH], F32)] * 2
        qr2 = [sb(a2, "qr2_0", [128, ACH], F32)] * 2
        E_sl = [sb(a2, f"E{i}", [128, 2 * ACH], BF16) for i in range(4)]
        OTc = sb(a2, "OTc", [128, 16 * ACH], BF16)
        tmpO = [sb(a2, f"tmpO{i}", [64, 512], F32) for i in range(2)]
        aoTc = sb(a2, "aoTc", [128, 16 * ACH], F32)
        sqb = sb(a2, "sqb", [128, 16 * ACH], BF16)
        ntmp = [sb(a2, f"ntmp{i}", [128, ACH], F32) for i in range(2)]
        xa_t = sb(a2, "xa_t", [128, D], F32)
        x1_t = sb(a2, "x1_t", [128, D], F32)
        ln3f = sb(a2, "ln3f", [128, D], F32)
        ln3h = sb(a2, "ln3h", [128, D], BF16)
        ln3lo = sb(a2, "ln3lo", [128, D], BF16)
        lst = [sb(a2, f"lst{i}", [128, 8 * 128], BF16) for i in range(2)]
        lstl = [sb(a2, f"lstl{i}", [128, 8 * 128], BF16) for i in range(2)]
        colsB = sb(a2, "colsB", [128, 8], F32)
        logit = sb(a2, "logit", [128, 8], F32)
        prob = sb(a2, "prob", [128, 8], F32)
        gsc = sb(a2, "gsc", [128, 24], F32)
        rows = sb(a2, "rows", [1, 2304], F32)
        rows_b = sb(a2, "rows_b", [1, 3072], BF16)

        def rows_hilo(src, n, hi_b, lo_b, ftmp, base, ftag):
            scopy(hi_b, src, reads=[base], writes=[(base, "hi")])
            scopy(ftmp, hi_b, reads=[(base, "hi")], writes=[(base, "hf"), ftag])
            vop("tensor_sub", lo_b, src, ftmp,
                reads=[base, (base, "hf"), ftag], writes=[(base, "lo")])

        def epilogue_stages(c):
            stages = []
            for tb in range(ACH // 128):
                tba = c * (ACH // 128) + tb

                def st_x1h(half, tba=tba, tb=tb):
                    def go(pw, pwn):
                        if half == 0:
                            dma("sync", xa_t[:], x_d[tba * 128:(tba + 1) * 128, :],
                                writes=[("xa",)], stream=("xa", "slot"))
                        for j in range(8):
                            dt = half * 8 + j
                            tr(pw[:, j * 128:(j + 1) * 128],
                               aoTc[:, dt * ACH + tb * 128: dt * ACH + tb * 128 + 128],
                               ident_f[:],
                               reads=[("aoTc",), ("identf",)], writes=[(pwn, j // 4)])
                        vop("tensor_add", x1_t[:, half * 1024:(half + 1) * 1024], pw[:],
                            xa_t[:, half * 1024:(half + 1) * 1024],
                            reads=[(pwn, 0), (pwn, 1), ("xa",)], writes=[("x1",)])
                        if half == 1:
                            dma("sync", x1_dram[tba * 128:(tba + 1) * 128, :], x1_t[:],
                                reads=[("x1",)], writes=[("x1d",)], stream=("x1d", "slot"))
                            mcol, rcol, m2col = colsB[:, 0:1], colsB[:, 1:2], colsB[:, 2:3]
                            ln_token(x1_t[:], xa_t[:], mcol, rcol, m2col, ("lnB",),
                                     ("x1",), ("xa",))
                            vop("tensor_scalar", ln3f[:], x1_t[:], mcol, rcol,
                                ALU.subtract, ALU.mult,
                                reads=[("x1",), ("lnB",)], writes=[("ln3f",)])
                            scopy(ln3h[:], ln3f[:], reads=[("ln3f",)], writes=[("ln3h",)])
                            scopy(xa_t[:], ln3h[:], reads=[("ln3h",)], writes=[("xa",)])
                            vop("tensor_sub", ln3lo[:], ln3f[:], xa_t[:],
                                reads=[("ln3f",), ("xa",)], writes=[("ln3lo",)])
                    return go
                stages.append(st_x1h(0))
                stages.append(st_x1h(1))

                def st_l3(half, lo, tba=tba):
                    def go(pw, pwn):
                        ppb = pw[:].bitcast(BF16)
                        src = ln3lo if lo else ln3h
                        stag = ("ln3lo",) if lo else ("ln3h",)
                        for j in range(8):
                            dc = half * 8 + j
                            tr(ppb[:, j * 128:(j + 1) * 128], src[:, dc * 128:(dc + 1) * 128],
                               ident_b[:], reads=[stag, ("identb",)], writes=[(pwn, 0)])
                        dstb = lstl[half] if lo else lst[half]
                        dtag = ("lstl", half) if lo else ("lst", half)
                        scopy(dstb[:], ppb[:, 0:1024], reads=[(pwn, 0)], writes=[dtag])
                        if not lo:
                            dma("sync",
                                ln3T_dram[half * 8:(half + 1) * 8, :,
                                          tba * 128:(tba + 1) * 128].transpose([1, 0, 2]),
                                lst[half][:].rearrange("p (c t) -> p c t", c=8),
                                reads=[("lst", half)], writes=[("ln3Td", half)],
                                stream=(f"l3d{half}", "slot"))
                    return go
                stages.append(st_l3(0, False))
                stages.append(st_l3(0, True))
                stages.append(st_l3(1, False))
                stages.append(st_l3(1, True))

                def st_router(tba=tba):
                    def go(pw, pwn):
                        rwv_hi = rw_hi[:].rearrange("p (c e) -> p c e", c=16)
                        rwv_lo = rw_lo[:].rearrange("p (c e) -> p c e", c=16)
                        for half in range(2):
                            for j in range(8):
                                dc = half * 8 + j
                                first = (half == 0 and j == 0)
                                mm(pw[:, 0:8], lst[half][:, j * 128:(j + 1) * 128],
                                   rwv_hi[:, dc, :], first, False,
                                   reads=[("lst", half), ("rwhi",)], writes=[(pwn, 0)])
                                mm(pw[:, 0:8], lst[half][:, j * 128:(j + 1) * 128],
                                   rwv_lo[:, dc, :], False, False,
                                   reads=[("lst", half), ("rwlo",)], writes=[(pwn, 0)])
                                mm(pw[:, 0:8], lstl[half][:, j * 128:(j + 1) * 128],
                                   rwv_hi[:, dc, :], False, False,
                                   reads=[("lstl", half), ("rwhi",)], writes=[(pwn, 0)])
                        mm(pw[:, 0:8], ones1[:], rbl_hi[:], False, False,
                           reads=[("ones1",), ("rblhi",)], writes=[(pwn, 0)])
                        mm(pw[:, 0:8], ones1[:], rbl_lo[:], False, True,
                           reads=[("ones1",), ("rbllo",)], writes=[(pwn, 0)])
                        scopy(logit[:], pw[:, 0:8], reads=[(pwn, 0)], writes=[("logit",)])
                        g0, g1 = gsc[:, 0:1], gsc[:, 1:2]
                        m1, m2 = gsc[:, 8:16], gsc[:, 16:24]
                        vop("reduce_max", g0, logit[:], AX.X, reads=[("logit",)],
                            writes=[("g0",)])
                        vop("tensor_scalar", prob[:], logit[:], g0, None, ALU.subtract,
                            reads=[("logit",), ("g0",)], writes=[("prob",)])
                        act(prob[:], prob[:], AF.Exp, reads=[("prob",)], writes=[("prob",)])
                        vop("reduce_sum", g1, prob[:], AX.X, reads=[("prob",)],
                            writes=[("g1",)])
                        vop("reciprocal", g1, g1, reads=[("g1",)], writes=[("g1",)])
                        vop("tensor_scalar", prob[:], prob[:], g1, None, ALU.mult,
                            reads=[("prob",), ("g1",)], writes=[("prob",)])
                        vop("reduce_max", g0, prob[:], AX.X, reads=[("prob",)],
                            writes=[("g0",)])
                        vop("tensor_scalar", m1, prob[:], g0, None, ALU.is_ge,
                            reads=[("prob",), ("g0",)], writes=[("m1",)])
                        vop("scalar_tensor_tensor", m2, m1, -1000.0, prob[:],
                            ALU.mult, ALU.add,
                            reads=[("m1",), ("prob",)], writes=[("m2",)])
                        vop("reduce_max", g0, m2, AX.X, reads=[("m2",)], writes=[("g0",)])
                        vop("tensor_scalar", m2, m2, g0, None, ALU.is_ge,
                            reads=[("m2",), ("g0",)], writes=[("m2",)])
                        vop("tensor_add", m1, m1, m2, reads=[("m1",), ("m2",)],
                            writes=[("m1",)])
                        act(prob[:], prob[:], AF.Exp, reads=[("prob",)], writes=[("prob",)])
                        vop("tensor_mul", prob[:], prob[:], m1,
                            reads=[("prob",), ("m1",)], writes=[("prob",)])
                        vop("reduce_sum", g1, prob[:], AX.X, reads=[("prob",)],
                            writes=[("g1",)])
                        vop("reciprocal", g1, g1, reads=[("g1",)], writes=[("g1",)])
                        vop("tensor_scalar", w_tm[:, tba * 9: tba * 9 + 8], prob[:], g1,
                            None, ALU.mult,
                            reads=[("prob",), ("g1",)], writes=[("w_tm",)])
                    return go
                stages.append(st_router())

                def st_wT(tba=tba):
                    def go(pw, pwn):
                        tr(pw[0:8, 0:128], w_tm[:, tba * 9: tba * 9 + 8], ident_f[:],
                           reads=[("w_tm",), ("identf",)], writes=[(pwn, 0)])
                        scopy(wT_sb[0:8, tba * 128:(tba + 1) * 128], pw[0:8, 0:128],
                              reads=[(pwn, 0)], writes=[("wT",)])
                    return go
                stages.append(st_wT())
            return stages


        for c in range(NACH):
            t0 = c * ACH
            cpar = 0
            l1vc = ln1Tc[cpar][:].rearrange("p (c t) -> p c t", c=16)
            cosv, sinv = cosc2[cpar], sinc2[cpar]
            dma("sync", l1vc, ln1T_dram[:, :, t0:t0 + ACH].transpose([1, 0, 2]),
                reads=[("ln1Td",)], writes=[("ln1Tc", cpar)], stream=(f"l1c{cpar}", "slot"))
            dma("sync", cosv[:], cosT_d[:, t0:t0 + ACH], writes=[("cosc", cpar)],
                stream=(f"cosc{cpar}", "slot"))
            dma("sync", sinv[:], sinT_d[:, t0:t0 + ACH], writes=[("sinc", cpar)],
                stream=(f"sinc{cpar}", "slot"))

            def jit_mms(ot):
                cb = cbst[ot % 4]
                cbres = ("cbst", ot % 4)
                dma("sync", cb[:].rearrange("p (c o) -> p c o", c=16),
                    wq_c[ot, :, :, :],
                    reads=[("c_wq_c",)], writes=[cbres], stream=(f"cb{ot % 4}", "slot"))
                pp = PS[ot % 2]
                pres = PN[ot % 2]
                for dc in range(16):
                    mm(pp[:, 0:ACH], cb[:, dc * 128:(dc + 1) * 128], l1vc[:, dc, :],
                       dc == 0, dc == 15, reads=[cbres, ("ln1Tc", cpar)], writes=[pres])

            def jit_rope(ot):
                sl = ot % 2
                pp = PS[sl]
                pres = PN[sl]
                scopy(qpb[sl][:], pp[:, 0:ACH], reads=[pres], writes=[("qpb", sl)])
                psw = PS[2 + sl]
                psres = PN[2 + sl]
                mm(psw[:, 0:ACH], perm_b[:], qpb[sl][:], True, True,
                   reads=[("permb",), ("qpb", sl)], writes=[psres])
                vop("tensor_mul", qr1[sl][:], pp[:, 0:ACH], cosv[:],
                    reads=[pres, ("cosc", cpar)], writes=[("qr1",)])
                vop("tensor_mul", qr2[sl][:], psw[:, 0:ACH], sinv[:],
                    reads=[psres, ("sinc", cpar)], writes=[("qr2",)])
                vop("tensor_add", QTc[:, ot * ACH:(ot + 1) * ACH], qr1[sl][:], qr2[sl][:],
                    reads=[("qr1",), ("qr2",)], writes=[("QTc",)])

            for ot in range(16):
                jit_mms(ot)
                if ot > 0:
                    jit_rope(ot - 1)
            jit_rope(15)
            stq = epilogue_stages(c - 1) if c > 0 else []
            spop = [0]

            def pop_stage():
                if stq:
                    b = spop[0] % 2
                    spop[0] += 1
                    stq.pop(0)(PW0 if b == 0 else PW1, "PW0" if b == 0 else "PW1")

            nkt = 2 * (c + 1)
            npair = nkt // 2

            def pair_scores_pv(p2i):
                pp2 = p2i % 2
                h0 = 2 * p2i
                kvh = h0 // 4
                ktile, koff = kvh // 2, 64 * (kvh % 2)
                pw = PW0 if pp2 == 0 else PW1
                pwn = "PW0" if pp2 == 0 else "PW1"
                qtile, qoff = qplace(h0)
                qrhs = QTc[qoff:qoff + 64, qtile * ACH:(qtile + 2) * ACH]

                def emit_s(kt):
                    pp = PS[kt % 4]
                    pres = PN[kt % 4]
                    mm(pp[:],
                       K_sb[koff:koff + 64,
                            ktile * S + kt * 128: ktile * S + (kt + 1) * 128],
                       qrhs, True, True, reads=[("K",), ("QTc",)], writes=[pres])
                    esl = kt % 4
                    act(E_sl[esl][:], pp[:], AF.Exp, bias=-ESHIFT, scale=0.125,
                        reads=[pres], writes=[("E", esl)])
                    if kt >= nkt - 2:
                        di = 0 if kt == nkt - 2 else 1
                        vop("tensor_mul", E_sl[esl][:], E_sl[esl][:], m01d_b[di][:],
                            reads=[("E", esl), ("m01d", di)], writes=[("E", esl)])

                def emit_pv(kt):
                    esl = kt % 4
                    mm(pw[0:65, 0:512],
                       V_sb[:, kt * 520 + kvh * 65: kt * 520 + (kvh + 1) * 65],
                       E_sl[esl][:], kt == 0, kt == nkt - 1,
                       reads=[("V", kt), ("Vones",), ("E", esl)], writes=[(pwn, 0)])

                return emit_s, emit_pv

            def pair_norm_calc(p2i):
                hp = p2i % 2
                pw = PW0 if hp == 0 else PW1
                pwn = "PW0" if hp == 0 else "PW1"
                rscr = rows[0:1, 1792:2304]
                act(rscr, pw[64:65, 0:512], AF.Ln, reads=[(pwn, 0)], writes=[("rscr",)])
                act(rows_b[0:1, hp * 512:hp * 512 + 512], rscr, AF.Exp, scale=-1.0,
                    reads=[("rscr",)], writes=[(("rr", hp), "hi")])

            def pair_norm_apply(p2i):
                hp = p2i % 2
                h0 = 2 * p2i
                qtile, qoff = qplace(h0)
                pw = PW0 if hp == 0 else PW1
                pwn = "PW0" if hp == 0 else "PW1"
                mm(pw[0:64, 512:1024], ones1[0:1, 0:64],
                   rows_b[0:1, hp * 512:hp * 512 + 512], True, True,
                   reads=[("ones1",), (("rr", hp), "hi")], writes=[(pwn, 1)])
                scopy(tmpO[hp][:], pw[0:64, 0:512], reads=[(pwn, 0)], writes=[("tmpO", hp)])
                vop("tensor_mul", OTc[qoff:qoff + 64, qtile * ACH:(qtile + 2) * ACH],
                    tmpO[hp][:], pw[0:64, 512:1024],
                    reads=[("tmpO", hp), (pwn, 1)], writes=[("OTc",)])

            LA = 2
            for p2i in range(16):
                emit_s, emit_pv = pair_scores_pv(p2i)
                for kt in range(min(LA + 1, nkt)):
                    emit_s(kt)
                if p2i > 0:
                    pair_norm_calc(p2i - 1)
                late = max(0, nkt - 2)
                for kt in range(nkt):
                    if kt + LA + 1 < nkt:
                        emit_s(kt + LA + 1)
                    if kt == late and p2i > 0:
                        pair_norm_apply(p2i - 1)
                    emit_pv(kt)
                    if kt == late and p2i > 0:
                        pop_stage()
            pair_norm_calc(15)
            pair_norm_apply(15)

            for dt in range(16):
                cb = cbst[dt % 4]
                cbres = ("cbst", dt % 4)
                dma("sync", cb[:].rearrange("p (c o) -> p c o", c=16), wo_c[dt, :, :, :],
                    reads=[("c_wo_c",)], writes=[cbres], stream=(f"cb{dt % 4}", "slot"))
                pp = PS[dt % 4]
                pres = PN[dt % 4]
                for oc in range(16):
                    mm(pp[:, 0:ACH], cb[:, oc * 128:(oc + 1) * 128], OTc[:, oc * ACH:(oc + 1) * ACH],
                       oc == 0, oc == 15, reads=[cbres, ("OTc",)], writes=[pres])
                scopy(aoTc[:, dt * ACH:(dt + 1) * ACH], pp[:, 0:ACH], reads=[pres],
                      writes=[("aoTc",)])
                vop("tensor_copy", QTc[:, dt * ACH:(dt + 1) * ACH],
                    aoTc[:, dt * ACH:(dt + 1) * ACH], reads=[("aoTc",)], writes=[("QTc",)])
                act(sqb[:, dt * ACH:(dt + 1) * ACH], aoTc[:, dt * ACH:(dt + 1) * ACH],
                    AF.Square, reads=[("aoTc",)], writes=[("sqb",)])
                if dt % 4 == 3:
                    pop_stage()

            for dc in range(16):
                mm(P2[0:1, 0:ACH], onesc[:], QTc[:, dc * ACH:(dc + 1) * ACH], dc == 0, dc == 15,
                   reads=[("onesc",), ("QTc",)], writes=[("P2",)])
                mm(P3[0:1, 0:ACH], onesc[:], sqb[:, dc * ACH:(dc + 1) * ACH], dc == 0, dc == 15,
                   reads=[("onesc",), ("sqb",)], writes=[("P3",)])
            mrow = rows[0:1, 1024:1024 + ACH]
            vrow = rows[0:1, 1024 + ACH:1024 + 2 * ACH]
            trow = rows[0:1, 1024 + 2 * ACH:1024 + 3 * ACH]
            vop("tensor_scalar", mrow, P2[0:1, 0:ACH], 1.0 / D, None, ALU.mult,
                reads=[("P2",)], writes=[("mrow",)])
            vop("tensor_scalar", vrow, P3[0:1, 0:ACH], 1.0 / D, None, ALU.mult,
                reads=[("P3",)], writes=[("vrow",)])
            vop("tensor_tensor", trow, mrow, mrow, ALU.mult, reads=[("mrow",)], writes=[("trow",)])
            vop("tensor_sub", vrow, vrow, trow, reads=[("vrow",), ("trow",)], writes=[("vrow",)])
            act(vrow, vrow, AF.Ln, bias=EPS, reads=[("vrow",)], writes=[("vrow",)])
            act(vrow, vrow, AF.Exp, scale=-0.5, reads=[("vrow",)], writes=[("vrow",)])
            rows_hilo(mrow, ACH, rows_b[0:1, 2048:2048 + ACH],
                      rows_b[0:1, 2048 + ACH:2048 + 2 * ACH], trow, ("mrow",), ("ftB",))
            mm(PW0[:, 512:512 + ACH], ones1[:], rows_b[0:1, 2048:2048 + ACH], True, False,
               reads=[("ones1",), (("mrow",), "hi")], writes=[("PW0", 1)])
            mm(PW0[:, 512:512 + ACH], ones1[:], rows_b[0:1, 2048 + ACH:2048 + 2 * ACH],
               False, True, reads=[("ones1",), (("mrow",), "lo")], writes=[("PW0", 1)])
            rows_hilo(vrow, ACH, rows_b[0:1, 2560:2560 + ACH],
                      rows_b[0:1, 2560 + ACH:2560 + 2 * ACH], trow, ("vrow",), ("ftB",))
            mm(PW1[:, 512:512 + ACH], ones1[:], rows_b[0:1, 2560:2560 + ACH], True, False,
               reads=[("ones1",), (("vrow",), "hi")], writes=[("PW1", 1)])
            mm(PW1[:, 512:512 + ACH], ones1[:], rows_b[0:1, 2560 + ACH:2560 + 2 * ACH],
               False, True, reads=[("ones1",), (("vrow",), "lo")], writes=[("PW1", 1)])
            for dt in range(16):
                sl = dt % 2
                vop("tensor_sub", ntmp[sl][:], aoTc[:, dt * ACH:(dt + 1) * ACH],
                    PW0[:, 512:512 + ACH],
                    reads=[("aoTc",), ("PW0", 1)], writes=[("ntmp", sl)])
                vop("tensor_mul", aoTc[:, dt * ACH:(dt + 1) * ACH], ntmp[sl][:],
                    PW1[:, 512:512 + ACH],
                    reads=[("ntmp", sl), ("PW1", 1)], writes=[("aoTc",)])

        tailq = epilogue_stages(NACH - 1)
        for i, st_ in enumerate(tailq):
            st_(PW0 if i % 2 == 0 else PW1, "PW0" if i % 2 == 0 else "PW1")
        s.barrier()
        _close(a2)
        _close(spanKV)
        _ckpt("a2")

        ups = _mk()
        ustg = sb(ups, "ustg", [1, LD], BF16)
        b1ofs = np.cumsum([0] + NH).tolist()
        for e in range(NEXP):
            for hcx in range(NH[e]):
                slw = hcx % 4
                dma("sync" if slw % 2 == 0 else "gpsimd",
                    w2st[slw][:], w2_c[e][hcx * 128:(hcx + 1) * 128, :],
                    reads=[("c_" + w2_c[e].name,)], writes=[("w2st", slw)],
                    stream=(f"w2{slw}", "slot"))
                for half in range(2):
                    pp = PS[half]
                    mm(pp[0:1, 0:512], gb1[:, b1ofs[e] + hcx: b1ofs[e] + hcx + 1],
                       w2st[slw][:, half * 512:(half + 1) * 512], hcx == 0, hcx == NH[e] - 1,
                       reads=[("gb1",), ("w2st", slw)], writes=[PN[half]])
            scopy(urow[0:1, 0:512], P0[0:1, 0:512], reads=[("P0",)], writes=[("urow",)])
            scopy(urow[0:1, 512:1024], P1[0:1, 0:512], reads=[("P1",)], writes=[("urow",)])
            vop("tensor_add", crow[0:1, :], crow[0:1, :], urow[0:1, :],
                reads=[("crow",), ("urow",)], writes=[("crow",)])
            s.op("scalar",
                 (lambda o, i: lambda en: en.activation(o, i, AF.Identity, bias=0.0, scale=-1.0))(
                     ustg[0:1, :], urow[0:1, :]),
                 reads=[("urow",)], writes=[("ustg",)])
            dma("gpsimd", ustack_dram[e:e + 1, :], ustg[0:1, :], reads=[("ustg",)],
                writes=[("ustackd",)], stream=("usd", "slot"))
        dma("sync", urow[0:1, :], sb2row_d, writes=[("sb2r",), ("urow",)],
            stream=("sb2", "group"))
        vop("scalar_tensor_tensor", crow[0:1, :], urow[0:1, :], 0.1, crow[0:1, :],
            ALU.mult, ALU.add, reads=[("sb2r",), ("crow",)], writes=[("crow",)])
        scopy(ustg[0:1, :], crow[0:1, :], reads=[("crow",)], writes=[("ustg",)])
        dma("gpsimd", ustack_dram[8:9, :], ustg[0:1, :], reads=[("ustg",)], writes=[("ustackd",)],
            stream=("usd", "slot"))
        scopy(urow[0:1, :], ustg[0:1, :], reads=[("ustg",)], writes=[("uhf",), ("sb2r",)])
        vop("tensor_sub", urow[0:1, :], crow[0:1, :], urow[0:1, :],
            reads=[("crow",), ("uhf",)], writes=[("ulo",)])
        scopy(ustg[0:1, :], urow[0:1, :], reads=[("ulo",)], writes=[("ustg",)])
        dma("gpsimd", ustack_dram[9:10, :], ustg[0:1, :], reads=[("ustg",)], writes=[("ustackd",)],
            stream=("usd", "slot"))
        dma("sync", ustack[:], ustack_dram, reads=[("ustackd",)], writes=[("ustack",)],
            stream=("usl", "group"))
        s.barrier()
        _close(ups)
        _ckpt("ups")

        print('SBUF free at FFN(before allocs):', nc.sbuf_bytes_remaining)
        ffn = _mk()
        ln3Tc = sb(ffn, "ln3Tc", [128, 16 * FCH], BF16)
        g_sb = sb(ffn, "g_sb", [128, 8 * FCH], BF16)
        gxT = sb(ffn, "gxT", [128, 16 * FCH], BF16)
        sc32 = sb(ffn, "sc32", [128, 32 * FCH], BF16)
        acc_sb = sb(ffn, "acc_sb", [128, 4 * LD], F32)
        accT = sb(ffn, "accT", [128, 8 * FCH], BF16)
        faoT = sb(ffn, "faoT", [128, 16 * FCH], F32)
        fntmp = [sb(ffn, f"fntmp{i}", [128, FCH], F32) for i in range(2)]
        fx1 = sb(ffn, "fx1", [128, D], F32)
        frows = sb(ffn, "frows", [1, 3 * FCH], F32)
        frows_b = sb(ffn, "frows_b", [1, 4 * FCH], BF16)

        l3v = ln3Tc[:].rearrange("p (c t) -> p c t", c=16)
        p2map = [(PW0[:, 0:512], PW0[:, 512:1024]), (PW1[:, 0:512], PW1[:, 512:1024]),
                 (P0[:], P1[:]), (P2[:], P3[:])]
        p2res = [[("PW0", 0), ("PW0", 1)], [("PW1", 0), ("PW1", 1)],
                 [("P0",), ("P1",)], [("P2",), ("P3",)]]
        p2full = [PW0[:], PW1[:], None, None]

        def ffn_final(c):
            for tb in range(4):
                tbg = c * 4 + tb
                dma("sync", fx1[:], x1_dram[tbg * 128:(tbg + 1) * 128, :],
                    reads=[("x1d",)], writes=[("fx1",)], stream=("fx1", "slot"))
                for half in range(2):
                    pw = PW0 if half == 0 else PW1
                    pwn = "PW0" if half == 0 else "PW1"
                    for j in range(8):
                        dt = half * 8 + j
                        tr(pw[:, j * 128:(j + 1) * 128],
                           faoT[:, dt * FCH + tb * 128: dt * FCH + tb * 128 + 128], ident_f[:],
                           reads=[("faoT",), ("identf",)], writes=[(pwn, j // 4)])
                    vop("scalar_tensor_tensor", fx1[:, half * 1024:(half + 1) * 1024],
                        pw[:], 1.0, fx1[:, half * 1024:(half + 1) * 1024], ALU.mult, ALU.add,
                        reads=[(pwn, 0), (pwn, 1), ("fx1",)], writes=[("fx1",)])
                dma("sync", y_d[tbg * 128:(tbg + 1) * 128, :], fx1[:],
                    reads=[("fx1",)], writes=[("y",)], stream=("y", "group"))


        for c in range(NFCH):
            t0 = c * FCH
            dma("sync", l3v, ln3T_dram[:, :, t0:t0 + FCH].transpose([1, 0, 2]),
                reads=[("ln3Td", 0), ("ln3Td", 1)], writes=[("ln3Tc",)], stream=("l3c", "slot"))
            for lt in range(8):
                slc = lt % 4
                cb = cbst[slc]
                cbres = ("cbst", slc)
                dma("sync" if slc % 2 == 0 else "gpsimd",
                    cb[:].rearrange("p (c o) -> p c o", c=16), down_c[lt, :, :, :],
                    reads=[("c_down_c",)], writes=[cbres], stream=(f"cb{slc}", "slot"))
                pp = PS[lt % 4]
                pres = PN[lt % 4]
                for dc in range(16):
                    mm(pp[:], cb[:, dc * 128:(dc + 1) * 128], l3v[:, dc, :], dc == 0, dc == 15,
                       reads=[cbres, ("ln3Tc",)], writes=[pres])
                act(g_sb[:, lt * FCH:(lt + 1) * FCH], pp[:], AF.Gelu, bias=downb[:, lt:lt + 1],
                    reads=[pres, ("downb",)], writes=[("g", lt)])
            for dc in range(16):
                act(gxT[:, dc * FCH:(dc + 1) * FCH], l3v[:, dc, :], AF.Gelu,
                    reads=[("ln3Tc",)], writes=[("gxT",)])
            if c > 0:
                ffn_final(c - 1)
            for tsub in range(4):
                tbg = c * 4 + tsub
                for half in range(2):
                    mm(p2map[tsub][half], wT_sb[:, tbg * 128:(tbg + 1) * 128],
                       ustack[:, half * 512:(half + 1) * 512], True, True,
                       reads=[("wT",), ("wTones",), ("ustack",)],
                       writes=[p2res[tsub][half]])
                if p2full[tsub] is not None:
                    vop("tensor_copy", acc_sb[:, tsub * LD:(tsub + 1) * LD], p2full[tsub],
                        reads=p2res[tsub], writes=[("acc", tsub)])
                else:
                    vop("tensor_copy", acc_sb[:, tsub * LD:tsub * LD + 512], p2map[tsub][0],
                        reads=[p2res[tsub][0]], writes=[("acc", tsub)])
                    vop("tensor_copy", acc_sb[:, tsub * LD + 512:(tsub + 1) * LD], p2map[tsub][1],
                        reads=[p2res[tsub][1]], writes=[("acc", tsub)])

            def expert(nh, w1c4, w2cp, b1c0, gate_col, gate_const):
                for hcx in range(nh):
                    slc = hcx % 4
                    cb = cbst[slc]
                    cbres = ("cbst", slc)
                    dma("sync" if slc % 2 == 0 else "gpsimd",
                        cb[:, 0:8 * 128].rearrange("p (c o) -> p c o", c=8),
                        w1c4[hcx, :, :, :], reads=[("c_" + w1c4.name,)], writes=[cbres],
                        stream=(f"cb{slc}", "slot"))
                    pp = PS[hcx % 2]
                    pres = PN[hcx % 2]
                    for lt in range(8):
                        mm(pp[:], cb[:, lt * 128:(lt + 1) * 128], g_sb[:, lt * FCH:(lt + 1) * FCH],
                           lt == 0, lt == 7, reads=[cbres, ("g", lt)], writes=[pres])
                    act(sc32[:, hcx * FCH:(hcx + 1) * FCH], pp[:], AF.Gelu,
                        bias=b1all[:, b1c0 + hcx:b1c0 + hcx + 1],
                        reads=[pres, ("b1all",)], writes=[("sc32", hcx)])
                for hcx in range(nh):
                    slw = hcx % 4
                    dma("sync" if slw % 2 == 0 else "gpsimd",
                        w2st[slw][:], w2cp[hcx * 128:(hcx + 1) * 128, :],
                        reads=[("c_" + w2cp.name,)], writes=[("w2st", slw)],
                        stream=(f"w2{slw}", "slot"))
                    for tsub in range(4):
                        for half in range(2):
                            mm(p2map[tsub][half],
                               sc32[:, hcx * FCH + tsub * 128: hcx * FCH + tsub * 128 + 128],
                               w2st[slw][:, half * 512:(half + 1) * 512],
                               hcx == 0, hcx == nh - 1,
                               reads=[("sc32", hcx), ("w2st", slw)],
                               writes=[p2res[tsub][half]])
                for tsub in (2, 3, 0, 1):
                    tbg = c * 4 + tsub
                    gc = (w_tm[:, tbg * 9 + gate_col: tbg * 9 + gate_col + 1]
                          if gate_col is not None else gate_const)
                    grd = [("w_tm",)] if gate_col is not None else []
                    if p2full[tsub] is not None:
                        vop("scalar_tensor_tensor", acc_sb[:, tsub * LD:(tsub + 1) * LD],
                            p2full[tsub], gc, acc_sb[:, tsub * LD:(tsub + 1) * LD],
                            ALU.mult, ALU.add,
                            reads=p2res[tsub] + grd + [("acc", tsub)], writes=[("acc", tsub)])
                    else:
                        for half in range(2):
                            vop("scalar_tensor_tensor",
                                acc_sb[:, tsub * LD + half * 512: tsub * LD + (half + 1) * 512],
                                p2map[tsub][half], gc,
                                acc_sb[:, tsub * LD + half * 512: tsub * LD + (half + 1) * 512],
                                ALU.mult, ALU.add,
                                reads=[p2res[tsub][half]] + grd + [("acc", tsub)],
                                writes=[("acc", tsub)])

            for e in range(NEXP):
                expert(NH[e], w1_c[e], w2_c[e], b1ofs[e], e, None)
            expert(8, sw1_c, sw2_c, b1ofs[NEXP], None, 0.1)

            for lt in range(8):
                pp = PS[2 + lt % 2]
                pres = PN[2 + lt % 2]
                for tsub in range(4):
                    tr(pp[:, tsub * 128:(tsub + 1) * 128],
                       acc_sb[:, tsub * LD + lt * 128: tsub * LD + lt * 128 + 128], ident_f[:],
                       reads=[("acc", tsub), ("identf",)], writes=[pres])
                scopy(accT[:, lt * FCH:(lt + 1) * FCH], pp[:], reads=[pres], writes=[("accT",)])

            for dt in range(16):
                cbu = cbst[dt % 2]
                cbures = ("cbst", dt % 2)
                dma("sync" if dt % 2 == 0 else "gpsimd",
                    cbu[:, 0:8 * 128].rearrange("p (c o) -> p c o", c=8),
                    up_c[dt, :, :, :], reads=[("c_up_c",)], writes=[cbures],
                    stream=(f"cb{dt % 2}", "slot"))
                cbc = cbst[2 + dt % 2]
                cbcres = ("cbst", 2 + dt % 2)
                dma("gpsimd" if dt % 2 == 0 else "sync",
                    cbc[:].rearrange("p (c o) -> p c o", c=16), core_c[dt, :, :, :],
                    reads=[("c_core_c",)], writes=[cbcres],
                    stream=(f"cb{2 + dt % 2}", "slot"))
                pp = PS[dt % 4]
                pres = PN[dt % 4]
                for lt in range(8):
                    mm(pp[:], cbu[:, lt * 128:(lt + 1) * 128], accT[:, lt * FCH:(lt + 1) * FCH],
                       lt == 0, False, reads=[cbures, ("accT",)], writes=[pres])
                for dc in range(16):
                    mm(pp[:], cbc[:, dc * 128:(dc + 1) * 128], gxT[:, dc * FCH:(dc + 1) * FCH],
                       False, dc == 15, reads=[cbcres, ("gxT",)], writes=[pres])
                s.op("scalar",
                     (lambda o, i, b: lambda en: en.activation(o, i, AF.Identity, bias=b, scale=1.0))(
                         faoT[:, dt * FCH:(dt + 1) * FCH], pp[:], bcol[:, dt:dt + 1]),
                     reads=[pres, ("bcol",)], writes=[("faoT",)])
                act(sc32[:, (16 + dt) * FCH:(17 + dt) * FCH],
                    faoT[:, dt * FCH:(dt + 1) * FCH], AF.Square,
                    reads=[("faoT",)], writes=[("sc32", 16 + dt)])
                vop("tensor_copy", sc32[:, dt * FCH:(dt + 1) * FCH],
                    faoT[:, dt * FCH:(dt + 1) * FCH], reads=[("faoT",)], writes=[("sc32", dt)])

            for dc in range(16):
                mm(P2[0:1, 0:FCH], onesc[:], sc32[:, dc * FCH:(dc + 1) * FCH], dc == 0, dc == 15,
                   reads=[("onesc",), ("sc32", dc)], writes=[("P2",)])
                mm(P3[0:1, 0:FCH], onesc[:], sc32[:, (16 + dc) * FCH:(17 + dc) * FCH],
                   dc == 0, dc == 15, reads=[("onesc",), ("sc32", 16 + dc)], writes=[("P3",)])
            fm = frows[0:1, 0:FCH]
            fv = frows[0:1, FCH:2 * FCH]
            ft = frows[0:1, 2 * FCH:3 * FCH]
            vop("tensor_scalar", fm, P2[0:1, 0:FCH], 1.0 / D, None, ALU.mult,
                reads=[("P2",)], writes=[("fm",)])
            vop("tensor_scalar", fv, P3[0:1, 0:FCH], 1.0 / D, None, ALU.mult,
                reads=[("P3",)], writes=[("fv",)])
            vop("tensor_tensor", ft, fm, fm, ALU.mult, reads=[("fm",)], writes=[("ft",)])
            vop("tensor_sub", fv, fv, ft, reads=[("fv",), ("ft",)], writes=[("fv",)])
            act(fv, fv, AF.Ln, bias=EPS, reads=[("fv",)], writes=[("fv",)])
            act(fv, fv, AF.Exp, scale=-0.5, reads=[("fv",)], writes=[("fv",)])
            rows_hilo(fm, FCH, frows_b[0:1, 0:FCH], frows_b[0:1, FCH:2 * FCH], ft, ("fm",), ("ftF",))
            mm(P2[:, 0:FCH], ones1[:], frows_b[0:1, 0:FCH], True, False,
               reads=[("ones1",), (("fm",), "hi")], writes=[("P2",)])
            mm(P2[:, 0:FCH], ones1[:], frows_b[0:1, FCH:2 * FCH], False, True,
               reads=[("ones1",), (("fm",), "lo")], writes=[("P2",)])
            rows_hilo(fv, FCH, frows_b[0:1, 2 * FCH:3 * FCH], frows_b[0:1, 3 * FCH:4 * FCH],
                      ft, ("fv",), ("ftF",))
            mm(P3[:, 0:FCH], ones1[:], frows_b[0:1, 2 * FCH:3 * FCH], True, False,
               reads=[("ones1",), (("fv",), "hi")], writes=[("P3",)])
            mm(P3[:, 0:FCH], ones1[:], frows_b[0:1, 3 * FCH:4 * FCH], False, True,
               reads=[("ones1",), (("fv",), "lo")], writes=[("P3",)])
            for dt in range(16):
                sl = dt % 2
                vop("tensor_sub", fntmp[sl][:], faoT[:, dt * FCH:(dt + 1) * FCH], P2[:, 0:FCH],
                    reads=[("faoT",), ("P2",)], writes=[("fntmp", sl)])
                vop("tensor_mul", faoT[:, dt * FCH:(dt + 1) * FCH], fntmp[sl][:], P3[:, 0:FCH],
                    reads=[("fntmp", sl), ("P3",)], writes=[("faoT",)])

        ffn_final(NFCH - 1)
        _close(ffn)
    except _StopBuild:
        while _open:
            _open.pop().close()

    with nc.Block() as block:
        s.emit(block, nc, top)
    top.close()
    return nc


_NC_CACHE = []


def kernel(**inputs):
    ii = {k: np.asarray(v) for k, v in inputs.items()}
    hcs = host_constants()
    x = np.ascontiguousarray(ii["L0_attn_x"], dtype=np.float32)

    inm = {}
    qp = q_perm()
    inm["wqT"] = np.ascontiguousarray(np.asarray(ii["L0_attn_Wq"], np.float32).T[:, qp])
    inm["wkT"] = np.ascontiguousarray(ii["L0_attn_Wk"].T)
    inm["wvT"] = np.ascontiguousarray(ii["L0_attn_Wv"].T)
    inm["woT"] = np.ascontiguousarray(np.asarray(ii["L0_attn_Wo"], np.float32).T[qp, :])
    inm["down_w"] = np.ascontiguousarray(ii["L0_ffn_down_w"])
    inm["up_w"] = np.ascontiguousarray(ii["L0_ffn_up_w"])
    inm["core_w"] = np.ascontiguousarray(ii["L0_ffn_core_w"])
    inm["sw1"] = np.ascontiguousarray(ii["L0_ffn_shared_w1"])
    inm["sw2"] = np.ascontiguousarray(ii["L0_ffn_shared_w2"])
    for e in range(NEXP):
        inm[f"w1_{e}"] = np.ascontiguousarray(ii[f"L0_ffn_e{e}_w1"])
        inm[f"w2_{e}"] = np.ascontiguousarray(ii[f"L0_ffn_e{e}_w2"])
    inm["down_b_col"] = col_layout(ii["L0_ffn_down_b"])
    inm["up_b_col"] = col_layout(ii["L0_ffn_up_b"])
    inm["core_b_col"] = col_layout(ii["L0_ffn_core_b"])
    b1cols = [col_layout(ii[f"L0_ffn_e{e}_b1"]) for e in range(NEXP)]
    b1cols.append(col_layout(ii["L0_ffn_shared_b1"]))
    inm["b1all_col"] = np.ascontiguousarray(np.concatenate(b1cols, axis=1))
    inm["b2_cat_row"] = np.ascontiguousarray(
        np.concatenate([np.asarray(ii[f"L0_ffn_e{e}_b2"], np.float32)
                        for e in range(NEXP)]).reshape(1, -1))
    inm["sb2_row"] = np.ascontiguousarray(
        np.asarray(ii["L0_ffn_shared_b2"], np.float32).reshape(1, -1))
    inm["rb_row"] = np.ascontiguousarray(
        np.asarray(ii["L0_ffn_router_b"], np.float32).reshape(1, -1))
    inm["lamcost"] = hcs["lamcost"]
    rw = np.asarray(ii["L0_ffn_router_w"], np.float32)
    inm["router_w_t"] = np.ascontiguousarray(
        rw.reshape(16, 128, NEXP).transpose(1, 0, 2).reshape(128, 16 * NEXP))
    inm["cosT"] = hcs["cosT"]
    inm["sinT"] = hcs["sinT"]
    inm["m01"] = hcs["m01"]
    inm["m01p"] = hcs["m01p"]
    inm["m01d384"] = hcs["m01d384"]
    inm["m01d256"] = hcs["m01d256"]
    inm["ident"] = hcs["ident"]
    inm["permswap"] = hcs["permswap"]
    inm["constpool"] = np.ascontiguousarray(
        np.tile(np.array(CONST_VALS, np.float32)[None, :], (128, 1)))

    if not _NC_CACHE:
        _NC_CACHE.append(build_program())
    nc = _NC_CACHE[0]
    in_maps = []
    for core in range(8):
        m = dict(inm)
        m["x"] = np.ascontiguousarray(x[core])
        in_maps.append(m)
    res = run_bass_kernel_spmd(nc, in_maps, core_ids=list(range(8)))
    out = np.stack([np.asarray(res.results[i]["y"]) for i in range(8)], axis=0)
    return out.astype(np.float32)


# version 32
# speedup vs baseline: 1.2467x; 1.0327x over previous
import contextlib
import numpy as np
import concourse.bass as bass
import concourse.mybir as mybir
from concourse.bass_utils import run_bass_kernel_spmd

F32 = mybir.dt.float32
BF16 = mybir.dt.bfloat16
AF = mybir.ActivationFunctionType
ALU = mybir.AluOpType
AX = mybir.AxisListType

S = 2048
D = 2048
NQ, NKV, HDIM = 32, 8, 64
LD = 1024
HID = [1024, 2048, 3072, 4096, 1024, 2048, 3072, 4096]
NH = [h // 128 for h in HID]
NEXP = 8
EPS = 1e-5
ESHIFT = 24.0
ACH = 256
NACH = S // ACH
FCH = 512
NFCH = S // FCH
COST_LAMBDA = 1e-7
DBG_X1 = False
UPTO = "all"
NCORES = 8


class _StopBuild(Exception):
    pass

ENGINES = ("tensor", "vector", "scalar", "sync", "gpsimd")
COMPUTE = ("tensor", "vector", "scalar")
QUEUES = ("sync", "gpsimd")
PSUM_RES = {("P0",), ("P1",), ("P2",), ("P3",), ("PW0",), ("PW1",)}

CONST_VALS = [0.0, 1.0, -1.0, 0.125, -ESHIFT, EPS, 1.0 / D, -0.5, 0.1, -1000.0]


class Sched:

    def __init__(self):
        self.barrier_fns = {}
        self.ops = {e: [] for e in ENGINES}
        self.counts = {e: 0 for e in ENGINES}
        self.know = {e: {} for e in ENGINES}
        self.snap = {e: [] for e in ENGINES}
        self.res = {}
        self.pending = {e: {} for e in ENGINES}
        self.wait_targets = {e: set() for e in COMPUTE}
        self.streams = {}
        self.meta = {}

    def _stream_add(self, eng, idx, sid, kind):
        st = self.streams.setdefault(sid, {"kind": kind, "ops": [], "queue": eng})
        assert st["kind"] == kind, sid
        if kind == "group":
            assert st["queue"] == eng, (sid, "group streams must be single-queue")
        k = len(st["ops"])
        st["ops"].append((eng, idx))
        self.meta[(eng, idx)] = (sid, k)

    def _need(self, eng, waits, te, ti, raw=False):
        if te in COMPUTE:
            if te == eng and eng == "tensor":
                return
            if self.know[eng].get(te, 0) >= ti + 1:
                return
            waits[te] = max(waits.get(te, 0), ti + 1)
        else:
            sid, k = self.meta[(te, ti)]
            if self.know[eng].get(sid, 0) >= k + 1:
                return
            waits[sid] = max(waits.get(sid, 0), k + 1)

    def _learn(self, eng, waits):
        for key, c in list(waits.items()):
            if self.know[eng].get(key, 0) >= c:
                del waits[key]
                continue
            self.know[eng][key] = c
            if key in COMPUTE:
                st = self.snap[key][c - 1]
            else:
                we, wi = self.streams[key]["ops"][c - 1]
                st = self.snap[we][wi]
            for f, v in st.items():
                if f != eng and v > self.know[eng].get(f, 0):
                    self.know[eng][f] = v
            if key in COMPUTE:
                self.wait_targets[key].add(c - 1)

    def op(self, eng, fn, reads=(), writes=(), stream=None):
        waits = dict(self.pending[eng])
        exact = set(waits.keys())
        self.pending[eng] = {}
        for r in reads:
            rec = self.res.get(r)
            if rec and rec["w"] is not None:
                self._need(eng, waits, *rec["w"], raw=True)
            if rec and r in PSUM_RES and eng in ("scalar", "vector"):
                for (re_, ri) in rec["r"]:
                    if re_ != eng and re_ in ("scalar", "vector"):
                        self._need(eng, waits, re_, ri)
        for r in writes:
            rec = self.res.get(r)
            if rec:
                if rec["w"] is not None:
                    self._need(eng, waits, *rec["w"])
                for (re_, ri) in rec["r"]:
                    self._need(eng, waits, re_, ri)
        exact &= set(waits.keys())
        self._learn(eng, waits)
        exact &= set(waits.keys())
        idx = self.counts[eng]
        self.ops[eng].append((fn, waits, exact))
        self.counts[eng] = idx + 1
        if eng in QUEUES:
            assert stream is not None, "queue ops need a stream"
            self._stream_add(eng, idx, stream[0], stream[1])
            st = dict(self.know[eng])
            sid, k = self.meta[(eng, idx)]
            if self.streams[sid]["kind"] == "slot":
                st[sid] = max(st.get(sid, 0), k + 1)
        else:
            st = dict(self.know[eng])
            st[eng] = idx + 1
        self.snap[eng].append(st)
        for r in reads:
            rec = self.res.setdefault(r, {"w": None, "r": []})
            rec["r"].append((eng, idx))
        for r in writes:
            self.res[r] = {"w": (eng, idx), "r": []}
        return (eng, idx)

    def barrier(self, engines=("tensor", "vector", "scalar", "sync")):
        ccur = {e: self.counts[e] for e in COMPUTE if self.counts[e] > 0}
        scur = {sid: len(st["ops"]) for sid, st in self.streams.items()
                if st["ops"] and st["queue"] in engines}
        for e in engines:
            if e in QUEUES:
                continue
            waits = {}
            for f, c in ccur.items():
                if f != e and self.know[e].get(f, 0) < c:
                    waits[f] = c
            for sid, c in scur.items():
                if self.know[e].get(sid, 0) < c:
                    waits[sid] = c
            exact = set(k for k in waits if k not in COMPUTE)
            self._learn(e, waits)
            exact &= set(waits.keys())
            idx = self.counts[e]
            bfn = self.barrier_fns.get(e, lambda en: en.nop())
            self.ops[e].append((bfn, waits, exact))
            self.counts[e] = idx + 1
            st = dict(self.know[e])
            st[e] = idx + 1
            self.snap[e].append(st)
        post = {e: self.counts[e] for e in COMPUTE if self.counts[e] > 0}
        for e in QUEUES:
            for f, c in post.items():
                self.pending[e][f] = max(self.pending[e].get(f, 0), c)

    def mark_final(self, eng_idx):
        pass

    def emit(self, block, nc, top):
        sems = {}
        for e in COMPUTE:
            sems[e] = top.enter_context(nc.semaphore(f"sem_{e}"))
        for sid in self.streams:
            sems[sid] = top.enter_context(nc.semaphore(f"sem_s_{sid}"))
        incmap = {}
        for e in COMPUTE:
            m = {}
            for j, oi in enumerate(sorted(self.wait_targets[e])):
                m[oi] = j + 1
            incmap[e] = m
        totals = {sid: len(st["ops"]) for sid, st in self.streams.items()}

        def replay(e, eng):
            for i, (fn, waits, exact) in enumerate(self.ops[e]):
                for key in sorted(waits, key=str):
                    c = waits[key]
                    if key in COMPUTE:
                        eng.wait_ge(sems[key], incmap[key][c - 1])
                    else:
                        st = self.streams[key]
                        if st["kind"] == "slot" or key in exact:
                            eng.wait_ge(sems[key], 16 * c)
                        elif st["queue"] == e:
                            m = sum(1 for (se, si) in st["ops"] if si < i)
                            assert m >= c, (key, c, m)
                            eng.wait_ge(sems[key], 16 * m)
                        else:
                            eng.wait_ge(sems[key], 16 * totals[key])
                inst = fn(eng)
                if e in COMPUTE:
                    if i in incmap[e]:
                        inst.then_inc(sems[e], 1)
                else:
                    sid, k = self.meta[(e, i)]
                    inst.then_inc(sems[sid], 16)
            if e in QUEUES:
                done = set()
                for i in range(len(self.ops[e])):
                    sid, k = self.meta[(e, i)]
                    done.add(sid)
                for sid in sorted(done, key=str):
                    eng.wait_ge(sems[sid], 16 * totals[sid])

        @block.tensor
        def _(eng):
            replay("tensor", eng)

        @block.vector
        def _(eng):
            replay("vector", eng)

        @block.scalar
        def _(eng):
            replay("scalar", eng)

        @block.sync
        def _(eng):
            replay("sync", eng)

        @block.gpsimd
        def _(eng):
            replay("gpsimd", eng)


def host_constants():
    c = {}
    k = np.arange(0, HDIM, 2)[: HDIM // 2].astype(np.float64)
    inv_freq = 1.0 / (10000.0 ** (k / HDIM))
    pos = np.arange(S, dtype=np.float64)
    ang = pos[:, None] * inv_freq[None, :]
    ang2 = np.concatenate([ang, ang], axis=1)
    cos64 = np.cos(ang2).T
    sin64 = np.sin(ang2).T
    sgn = np.ones((64, 1))
    sgn[:32] = -1.0
    sins = sin64 * sgn
    c["cosT"] = np.ascontiguousarray(np.concatenate([cos64, cos64], 0).astype(np.float32))
    c["sinT"] = np.ascontiguousarray(np.concatenate([sins, sins], 0).astype(np.float32))
    p = np.arange(128)[:, None]
    j = np.arange(896)[None, :] - 384
    m01 = (j >= p).astype(np.float32)
    c["m01"] = np.ascontiguousarray(m01)
    c["m01p"] = np.ascontiguousarray(
        np.concatenate([m01[:, 384:640], m01[:, 256:512]], axis=1))
    c["m01d384"] = np.ascontiguousarray(np.tile(m01[:, 384:640], (1, 2)))
    c["m01d256"] = np.ascontiguousarray(np.tile(m01[:, 256:512], (1, 2)))
    c["ident"] = np.eye(128, dtype=np.float32)
    pm = np.zeros((128, 128), dtype=np.float32)
    idx = np.arange(128)
    pm[idx ^ 32, idx] = 1.0
    c["permswap"] = pm
    c["lamcost"] = (np.float32(COST_LAMBDA) * np.array(
        [np.float32(LD * h + h * LD) for h in HID], dtype=np.float32)).reshape(1, 8)
    return c


def col_layout(b):
    return np.ascontiguousarray(np.asarray(b, np.float32).reshape(-1, 128).T)


def qplace(h):
    g = h // 4
    return (g // 2) * 4 + (h % 4), 64 * (g % 2)


def q_perm():
    pos = np.empty(D, dtype=np.int64)
    for h in range(NQ):
        t, off = qplace(h)
        pos[t * 128 + off: t * 128 + off + 64] = np.arange(h * 64, (h + 1) * 64)
    return pos


def build_program():
    nc = bass.Bass("TRN2", target_bir_lowering=False)
    s = Sched()

    def _ckpt(name):
        if UPTO == name:
            raise _StopBuild()
    top = contextlib.ExitStack()

    def sb(stk, name, shape, dtype):
        return stk.enter_context(nc.sbuf_tensor(name, shape, dtype))

    def dma(q, out, in_, reads=(), writes=(), stream=None):
        return s.op(q, (lambda o, i: lambda e: e.dma_start(out=o, in_=i))(out, in_),
                    reads=reads, writes=writes, stream=stream)

    def mm(out, lhsT, rhs, start, stop, reads=(), writes=()):
        return s.op("tensor",
                    (lambda o, l, r, st, sp: lambda e: e.matmul(o, l, r, start=st, stop=sp))(
                        out, lhsT, rhs, start, stop), reads=reads, writes=writes)

    def tr(out, in_, ident, reads=(), writes=()):
        return s.op("tensor", (lambda o, i, d: lambda e: e.transpose(o, i, d))(out, in_, ident),
                    reads=reads, writes=writes)

    def act(out, in_, func, bias=0.0, scale=1.0, accum_out=None, reads=(), writes=()):
        return s.op("scalar",
                    (lambda o, i, f, b, sc, ac: lambda e: e.activation(
                        o, i, f, bias=b, scale=sc, accum_out=ac))(
                        out, in_, func, bias, scale, accum_out), reads=reads, writes=writes)

    def scopy(out, in_, reads=(), writes=()):
        return s.op("scalar", (lambda o, i: lambda e: e.copy(o, i))(out, in_),
                    reads=reads, writes=writes)

    def vop(name, *args, reads=(), writes=()):
        return s.op("vector", (lambda n, a: lambda e: getattr(e, n)(*a))(name, args),
                    reads=reads, writes=writes)

    def memset(ap, val, writes=()):
        return s.op("vector", (lambda a, v: lambda e: e.memset(a, v))(ap, val), writes=writes)

    def din(name, shape):
        return nc.dram_tensor(name, list(shape), F32, kind="ExternalInput").ap()

    def dint(name, shape, dt):
        return nc.dram_tensor(name, list(shape), dt, kind="Internal").ap()

    x_d = din("x", (S, D))
    y_d = nc.dram_tensor("y", [S, D], F32, kind="ExternalOutput").ap()
    wqT_d = din("wqT", (D, D))
    wkT_d = din("wkT", (D, NKV * HDIM))
    wvT_d = din("wvT", (D, NKV * HDIM))
    woT_d = din("woT", (D, D))
    down_d = din("down_w", (D, LD))
    up_d = din("up_w", (LD, D))
    core_d = din("core_w", (D, D))
    sw1_d = din("sw1", (LD, LD))
    sw2_d = din("sw2", (LD, LD))
    w1_d = [din(f"w1_{e}", (LD, HID[e])) for e in range(NEXP)]
    w2_d = [din(f"w2_{e}", (HID[e], LD)) for e in range(NEXP)]
    downb_d = din("down_b_col", (128, LD // 128))
    upb_d = din("up_b_col", (128, D // 128))
    coreb_d = din("core_b_col", (128, D // 128))
    b1all_d = din("b1all_col", (128, sum(NH) + 8))
    b2row_d = din("b2_cat_row", (1, NEXP * LD))
    sb2row_d = din("sb2_row", (1, LD))
    rbrow_d = din("rb_row", (1, NEXP))
    lam_d = din("lamcost", (1, NEXP))
    rwt_d = din("router_w_t", (128, 16 * NEXP))
    cosT_d = din("cosT", (128, S))
    sinT_d = din("sinT", (128, S))
    m01_d = din("m01", (128, 896))
    m01p_d = din("m01p", (128, 512))
    m01d384_d = din("m01d384", (128, 512))
    m01d256_d = din("m01d256", (128, 512))
    ident_d = din("ident", (128, 128))
    perm_d = din("permswap", (128, 128))
    consts_d = din("constpool", (128, len(CONST_VALS)))

    x1_dram = dint("x1_scr", (S, D), F32)
    mrA_dram = dint("mrA_scr", (1, 2 * ACH), F32)
    mrF_dram = dint("mrF_scr", (1, 2 * FCH), F32)
    ln1T_dram = dint("ln1T_scr", (16, 128, S), BF16)
    ln3T_dram = dint("ln3T_scr", (16, 128, S), BF16)
    wv_c = dint("wv_c", (D, 512), BF16)
    wk_c = dint("wk_c", (4, 128, 16, 128), BF16)
    wq_c = dint("wq_c", (16, 128, 16, 128), BF16)
    wo_c = dint("wo_c", (16, 128, 16, 128), BF16)
    down_c = dint("down_c", (8, 128, 16, 128), BF16)
    up_c = dint("up_c", (16, 128, 8, 128), BF16)
    core_c = dint("core_c", (16, 128, 16, 128), BF16)
    sw1_c = dint("sw1_c", (8, 128, 8, 128), BF16)
    w1_c = [dint(f"w1c_{e}", (NH[e], 128, 8, 128), BF16) for e in range(NEXP)]
    sw2_c = dint("sw2_c", (LD, LD), BF16)
    w2_c = [dint(f"w2c_{e}", (HID[e], LD), BF16) for e in range(NEXP)]
    ustack_dram = dint("ustack_scr", (10, LD), BF16)

    cpool = sb(top, "cpool", [128, len(CONST_VALS)], F32)
    for i, v in enumerate(CONST_VALS):
        nc.const_aps.register(F32, float(v), cpool[:, i:i + 1])
    ident_b = sb(top, "ident_b", [128, 128], BF16)
    ident_f = sb(top, "ident_f", [128, 128], F32)
    perm_b = sb(top, "perm_b", [128, 128], BF16)
    ones1 = sb(top, "ones1", [1, 128], BF16)
    onesc = sb(top, "onesc", [128, 1], BF16)
    m01_b = sb(top, "m01_b", [128, 896], BF16)
    m01p_b = sb(top, "m01p_b", [128, 512], BF16)
    m01d_b = [sb(top, f"m01d{i}_b", [128, 512], BF16) for i in range(2)]
    cvt = [sb(top, f"cvt{i}", [128, 2048], BF16) for i in range(2)]
    cbst = [sb(top, f"cbst{i}", [128, 16 * 128], BF16) for i in range(4)]
    w2st = [sb(top, f"w2st{i}", [128, LD], BF16) for i in range(4)]
    downb = sb(top, "downb", [128, LD // 128], F32)
    bcol = sb(top, "bcol", [128, D // 128], F32)
    colt = sb(top, "colt", [128, 16], F32)
    b1all = sb(top, "b1all", [128, sum(NH) + 8], F32)
    gb1 = sb(top, "gb1", [128, sum(NH) + 8], BF16)
    rw_sb = sb(top, "rw_sb", [128, 16 * NEXP], F32)
    rw_hi = sb(top, "rw_hi", [128, 16 * NEXP], BF16)
    rw_lo = sb(top, "rw_lo", [128, 16 * NEXP], BF16)
    rbl = sb(top, "rbl", [1, 4 * NEXP], F32)
    rbl_hi = sb(top, "rbl_hi", [1, NEXP], BF16)
    rbl_lo = sb(top, "rbl_lo", [1, NEXP], BF16)
    w_tm = sb(top, "w_tm", [128, 16 * 9], F32)
    wT_sb = sb(top, "wT_sb", [10, S], BF16)
    ustack = sb(top, "ustack", [10, LD], BF16)
    crow = sb(top, "crow", [1, LD], F32)
    urow = sb(top, "urow", [1, LD], F32)

    barc = sb(top, "barc", [1, 2], F32)
    s.barrier_fns = {
        "vector": (lambda en: en.memset(barc[0:1, 0:1], 0.0)),
        "scalar": (lambda en: en.memzero(barc[0:1, 1:2])),
    }
    P0 = top.enter_context(nc.psum_tensor("P0", [128, 512], F32))
    P1 = top.enter_context(nc.psum_tensor("P1", [128, 512], F32))
    P2 = top.enter_context(nc.psum_tensor("P2", [128, 512], F32))
    P3 = top.enter_context(nc.psum_tensor("P3", [128, 512], F32))
    PW0 = top.enter_context(nc.psum_tensor("PW0", [128, 1024], F32))
    PW1 = top.enter_context(nc.psum_tensor("PW1", [128, 1024], F32))
    PS = [P0, P1, P2, P3]
    PN = [("P0",), ("P1",), ("P2",), ("P3",)]

    print('SBUF free after static:', nc.sbuf_bytes_remaining)
    _open = []

    def _mk():
        st = contextlib.ExitStack()
        _open.append(st)
        return st

    def _close(st):
        assert _open and _open[-1] is st
        _open.pop()
        st.close()

    try:
        init = _mk()
        rowtmp = sb(init, "rowtmp", [1, NEXP * LD], F32)
        dma("gpsimd", ident_b[:], ident_d, writes=[("identb",)], stream=("gi", "group"))
        dma("sync", ident_f[:], ident_d, writes=[("identf",)], stream=("si", "group"))
        dma("gpsimd", perm_b[:], perm_d, writes=[("permb",)], stream=("gi", "group"))
        dma("gpsimd", m01_b[:], m01_d, writes=[("m01",)], stream=("gi", "group"))
        dma("gpsimd", m01p_b[:], m01p_d, writes=[("m01p",)], stream=("gi", "group"))
        dma("gpsimd", m01d_b[0][:], m01d384_d, writes=[("m01d", 0)], stream=("gi", "group"))
        dma("gpsimd", m01d_b[1][:], m01d256_d, writes=[("m01d", 1)], stream=("gi", "group"))
        dma("sync", cpool[:], consts_d, writes=[("cpool",)], stream=("si", "group"))
        dma("sync", downb[:], downb_d, writes=[("downb",)], stream=("si", "group"))
        dma("sync", bcol[:], upb_d, writes=[("bcol",)], stream=("si", "group"))
        dma("sync", colt[:], coreb_d, writes=[("coltc",)], stream=("si", "group"))
        vop("tensor_add", bcol[:], bcol[:], colt[:], reads=[("bcol",), ("coltc",)],
            writes=[("bcol",)])
        dma("sync", b1all[:], b1all_d, writes=[("b1all",)], stream=("si", "group"))
        act(gb1[:], b1all[:], AF.Gelu, reads=[("b1all",)], writes=[("gb1",)])
        dma("sync", rowtmp[:], b2row_d, writes=[("rowtmp",)], stream=("si", "group"))
        dma("sync", rbl[0:1, 0:NEXP], rbrow_d, writes=[("rbl",)], stream=("si", "group"))
        dma("sync", rbl[0:1, 16:16 + NEXP], lam_d, writes=[("rbllam",)], stream=("si", "group"))
        vop("tensor_sub", rbl[0:1, 0:NEXP], rbl[0:1, 0:NEXP], rbl[0:1, 16:16 + NEXP],
            reads=[("rbl",), ("rbllam",)], writes=[("rbl",)])
        scopy(rbl_hi[:], rbl[0:1, 0:NEXP], reads=[("rbl",)], writes=[("rblhi",)])
        scopy(rbl[0:1, 8:8 + NEXP], rbl_hi[:], reads=[("rblhi",)], writes=[("rblhf",)])
        vop("tensor_sub", rbl_lo[:], rbl[0:1, 0:NEXP], rbl[0:1, 8:8 + NEXP],
            reads=[("rbl",), ("rblhf",)], writes=[("rbllo",)])
        dma("sync", rw_sb[:], rwt_d, writes=[("rw",)], stream=("si", "group"))
        scopy(rw_hi[:], rw_sb[:], reads=[("rw",)], writes=[("rwhi",)])
        vop("tensor_sub", rw_lo[:], rw_sb[:], rw_hi[:], reads=[("rw",), ("rwhi",)],
            writes=[("rwlo",)])
        memset(ones1[:], 1.0, writes=[("ones1",)])
        memset(onesc[:], 1.0, writes=[("onesc",)])
        memset(wT_sb[0:10, :], 1.0, writes=[("wTones",)])
        vop("tensor_add", rowtmp[0:1, 0:4096], rowtmp[0:1, 0:4096], rowtmp[0:1, 4096:8192],
            reads=[("rowtmp",)], writes=[("rowtmp",)])
        vop("tensor_add", rowtmp[0:1, 0:2048], rowtmp[0:1, 0:2048], rowtmp[0:1, 2048:4096],
            reads=[("rowtmp",)], writes=[("rowtmp",)])
        vop("tensor_add", crow[0:1, :], rowtmp[0:1, 0:1024], rowtmp[0:1, 1024:2048],
            reads=[("rowtmp",)], writes=[("crow",)])
        cvt_i = [0]

        def convert_tiled(src, dst4, R, C):
            for c0 in range(R // 128):
                for cc in range(0, C, 2048):
                    w = min(2048, C - cc)
                    sl = cvt_i[0] % 2
                    cvt_i[0] += 1
                    rid = ("cvt", sl)
                    dma("gpsimd", cvt[sl][:, 0:w], src[c0 * 128:(c0 + 1) * 128, cc:cc + w],
                        writes=[rid], stream=(f"cvt{sl}", "slot"))
                    dma("gpsimd", dst4[cc // 128:(cc + w) // 128, :, c0, :].transpose([1, 0, 2]),
                        cvt[sl][:, 0:w].rearrange("p (j o) -> p j o", o=128),
                        reads=[rid], writes=[("c_" + dst4.name,)],
                        stream=("c_" + dst4.name, "group"))

        def convert_plain(src, dst, R, C):
            for c0 in range(R // 128):
                sl = cvt_i[0] % 2
                cvt_i[0] += 1
                rid = ("cvt", sl)
                dma("gpsimd", cvt[sl][:, 0:C], src[c0 * 128:(c0 + 1) * 128, :], writes=[rid],
                    stream=(f"cvt{sl}", "slot"))
                dma("gpsimd", dst[c0 * 128:(c0 + 1) * 128, :], cvt[sl][:, 0:C],
                    reads=[rid], writes=[("c_" + dst.name,)], stream=("c_" + dst.name, "group"))

        convert_plain(wvT_d, wv_c, D, 512)
        convert_tiled(wkT_d, wk_c, D, 512)
        convert_tiled(wqT_d, wq_c, D, D)
        convert_tiled(woT_d, wo_c, D, D)
        for e in range(NEXP):
            convert_plain(w2_d[e], w2_c[e], HID[e], LD)
        convert_tiled(down_d, down_c, D, LD)
        convert_plain(sw2_d, sw2_c, LD, LD)
        convert_tiled(sw1_d, sw1_c, LD, LD)
        for e in range(NEXP):
            convert_tiled(w1_d[e], w1_c[e], LD, HID[e])
        convert_tiled(up_d, up_c, LD, D)
        convert_tiled(core_d, core_c, D, D)
        s.barrier()
        _close(init)

        spanKV = _mk()
        K_sb = sb(spanKV, "K_sb", [128, 4 * S], BF16)
        V_sb = sb(spanKV, "V_sb", [128, 16 * 520], BF16)

        span01 = _mk()
        ln1T = sb(span01, "ln1T", [128, 16 * S], BF16)

        a0 = _mk()
        x_t = [sb(a0, f"x_t{i}", [128, D], F32) for i in range(2)]
        ln1f = [sb(a0, f"ln1f{i}", [128, D], BF16) for i in range(2)]
        sqt = sb(a0, "sqt", [128, D], F32)
        colsA = sb(a0, "colsA", [128, 8], F32)
        wvst = sb(a0, "wvst", [128, 16 * 512], BF16)

        def ln_token(xap, sqap, mcol, rcol, m2col, statres, xtag, sqtag):
            sm, sr = (statres, "m"), (statres, "r")
            vop("reduce_sum", mcol, xap, AX.X, reads=[xtag], writes=[sm])
            act(sqap, xap, AF.Square, accum_out=rcol, reads=[xtag],
                writes=[sqtag, sr])
            vop("tensor_scalar", mcol, mcol, 1.0 / D, None, ALU.mult,
                reads=[sm], writes=[sm])
            vop("tensor_scalar", rcol, rcol, 1.0 / D, None, ALU.mult,
                reads=[sr], writes=[sr])
            vop("tensor_tensor", m2col, mcol, mcol, ALU.mult, reads=[sm], writes=[sm])
            vop("tensor_sub", rcol, rcol, m2col, reads=[sr, sm], writes=[sr])
            act(rcol, rcol, AF.Ln, bias=EPS, reads=[sr], writes=[sr])
            act(rcol, rcol, AF.Exp, scale=-0.5, reads=[sr], writes=[sr])

        lnv = ln1T[:].rearrange("p (c t) -> p c t", c=16)
        vv = V_sb[:].rearrange("p (c x) -> p c x", c=16)
        memset(V_sb[:, 64::65], 1.0, writes=[("Vones",)])
        for dc in range(16):
            dma("sync", wvst[:, dc * 512:(dc + 1) * 512], wv_c[dc * 128:(dc + 1) * 128, :],
                reads=[("c_wv_c",)], writes=[("wvst",)], stream=("wvl", "group"))
        for tt in range(16):
            sl = tt % 2
            xa, la = x_t[sl], ln1f[sl]
            xr = ("x_t", sl)
            dma("sync", xa[:], x_d[tt * 128:(tt + 1) * 128, :], writes=[xr],
                stream=(f"xt{sl}", "slot"))
            stat = ("lnA", sl)
            mcol, rcol, m2col = colsA[:, 3 * sl:3 * sl + 1], colsA[:, 3 * sl + 1:3 * sl + 2], \
                colsA[:, 3 * sl + 2:3 * sl + 3]
            ln_token(xa[:], sqt[:], mcol, rcol, m2col, stat, xr, ("sqA",))
            vop("tensor_scalar", la[:], xa[:], mcol, rcol, ALU.subtract, ALU.mult,
                reads=[xr, (stat, "m"), (stat, "r")], writes=[("ln1f", sl)])
            for half in range(2):
                pp = PS[2 + half]
                ppb = pp[:].bitcast(BF16)
                pres = PN[2 + half]
                for j in range(8):
                    dc = half * 8 + j
                    tr(ppb[:, j * 128:(j + 1) * 128], la[:, dc * 128:(dc + 1) * 128], ident_b[:],
                       reads=[("ln1f", sl), ("identb",)], writes=[pres])
                scopy(lnv[:, half * 8:(half + 1) * 8, tt * 128:(tt + 1) * 128],
                      ppb[:, 0:1024].rearrange("p (c t) -> p c t", c=8),
                      reads=[pres], writes=[("ln1T",)])
            ppv = PS[sl]
            presv = PN[sl]
            for dc in range(16):
                mm(ppv[:], lnv[:, dc, tt * 128:(tt + 1) * 128],
                   wvst[:, dc * 512:(dc + 1) * 512],
                   dc == 0, dc == 15, reads=[("ln1T",), ("wvst",)], writes=[presv])
            scopy(vv[:, tt, :].rearrange("p (h z) -> p h z", h=8)[:, :, 0:64],
                  ppv[:].rearrange("p (h z) -> p h z", h=8),
                  reads=[presv], writes=[("V", tt)])

        s.barrier()
        _close(a0)
        _ckpt("a0")

        print('SBUF free at A1(before allocs):', nc.sbuf_bytes_remaining)
        a1 = _mk()
        cosc = sb(a1, "cosc", [128, 512], F32)
        sinc = sb(a1, "sinc", [128, 512], F32)
        kpb = [sb(a1, f"kpb{i}", [128, 512], BF16) for i in range(2)]
        rp1 = [sb(a1, f"rp1_{i}", [128, 512], F32) for i in range(2)]
        rp2 = [sb(a1, f"rp2_{i}", [128, 512], F32) for i in range(2)]

        _ckpt("a1v")
        for ot in range(4):
            cb = cbst[ot % 4]
            cbres = ("cbst", ot % 4)
            dma("sync", cb[:].rearrange("p (c o) -> p c o", c=16), wk_c[ot, :, :, :],
                reads=[("c_wk_c",)], writes=[cbres], stream=(f"cb{ot % 4}", "slot"))
            _ckpt("a1kd")
            for tq in range(4):
                dma("sync", cosc[:], cosT_d[:, tq * 512:(tq + 1) * 512], writes=[("cosc",)],
                    stream=("cosc", "slot"))
                _ckpt("a1cA")
                dma("sync", sinc[:], sinT_d[:, tq * 512:(tq + 1) * 512], writes=[("sinc",)],
                    stream=("sinc", "slot"))
                _ckpt("a1c")
                pp = PS[tq % 2]
                pres = PN[tq % 2]
                psw = PS[2 + tq % 2]
                psres = PN[2 + tq % 2]
                for dc in range(16):
                    mm(pp[:], cb[:, dc * 128:(dc + 1) * 128], lnv[:, dc, tq * 512:(tq + 1) * 512],
                       dc == 0, dc == 15, reads=[cbres, ("ln1T",)], writes=[pres])
                sl = tq % 2
                _ckpt("a1m")
                scopy(kpb[sl][:], pp[:], reads=[pres], writes=[("kpb", sl)])
                _ckpt("a1s")
                mm(psw[:], perm_b[:], kpb[sl][:], True, True,
                   reads=[("permb",), ("kpb", sl)], writes=[psres])
                _ckpt("a1p")
                vop("tensor_mul", rp1[sl][:], pp[:], cosc[:], reads=[pres, ("cosc",)],
                    writes=[("rp1", sl)])
                _ckpt("a1r1")
                vop("tensor_mul", rp2[sl][:], psw[:], sinc[:], reads=[psres, ("sinc",)],
                    writes=[("rp2", sl)])
                _ckpt("a1r2")
                vop("tensor_add", K_sb[:, ot * S + tq * 512: ot * S + (tq + 1) * 512],
                    rp1[sl][:], rp2[sl][:], reads=[("rp1", sl), ("rp2", sl)], writes=[("K",)])
                _ckpt("a1r")

            _ckpt("a1k1")
        _ckpt("a1k")
        for q4 in range(4):
            dma("sync", ln1T_dram[q4 * 4:(q4 + 1) * 4, :, :].transpose([1, 0, 2]),
                lnv[:, q4 * 4:(q4 + 1) * 4, :], reads=[("ln1T",)], writes=[("ln1Td",)],
                stream=("ln1Td", "group"))
        s.barrier()
        _close(a1)
        _close(span01)
        _ckpt("a1")

        print('SBUF free at A2(before allocs):', nc.sbuf_bytes_remaining)
        a2 = _mk()
        ln1Tc = [sb(a2, "ln1Tc0", [128, 16 * ACH], BF16)] * 2
        cosc2 = [sb(a2, f"cosc2_{i}", [128, ACH], F32) for i in range(2)]
        sinc2 = [sb(a2, f"sinc2_{i}", [128, ACH], F32) for i in range(2)]
        QTc = sb(a2, "QTc", [128, 16 * ACH], BF16)
        qpb = [sb(a2, f"qpb{i}", [128, ACH], BF16) for i in range(2)]
        qr1 = [sb(a2, "qr1_0", [128, ACH], F32)] * 2
        qr2 = [sb(a2, "qr2_0", [128, ACH], F32)] * 2
        E_sl = [sb(a2, f"E{i}", [128, 2 * ACH], BF16) for i in range(4)]
        OTc = sb(a2, "OTc", [128, 16 * ACH], BF16)
        tmpO = [sb(a2, f"tmpO{i}", [64, 512], F32) for i in range(2)]
        aoTc = sb(a2, "aoTc", [128, 16 * ACH], F32)
        sqb = sb(a2, "sqb", [128, 16 * ACH], BF16)
        arow2 = sb(a2, "arow2", [2, ACH], F32)
        mrcA = sb(a2, "mrcA", [128, 4], F32)
        afu = sb(a2, "afu", [128, 1024], F32)
        xa_t = sb(a2, "xa_t", [128, D], F32)
        x1_t = sb(a2, "x1_t", [128, D], F32)
        ln3f = sb(a2, "ln3f", [128, D], F32)
        ln3h = sb(a2, "ln3h", [128, D], BF16)
        ln3lo = sb(a2, "ln3lo", [128, D], BF16)
        lst = [sb(a2, f"lst{i}", [128, 8 * 128], BF16) for i in range(2)]
        lstl = [sb(a2, f"lstl{i}", [128, 8 * 128], BF16) for i in range(2)]
        colsB = sb(a2, "colsB", [128, 8], F32)
        logit = sb(a2, "logit", [128, 8], F32)
        prob = sb(a2, "prob", [128, 8], F32)
        gsc = sb(a2, "gsc", [128, 24], F32)
        rows = sb(a2, "rows", [1, 2304], F32)
        rows_b = sb(a2, "rows_b", [1, 1024], BF16)

        def epilogue_stages(c):
            stages = []
            for tb in range(ACH // 128):
                tba = c * (ACH // 128) + tb

                def st_x1h(half, tba=tba, tb=tb):
                    def go(pw, pwn):
                        if half == 0:
                            dma("sync", xa_t[:], x_d[tba * 128:(tba + 1) * 128, :],
                                writes=[("xa",)], stream=("xa", "slot"))
                        for j in range(8):
                            dt = half * 8 + j
                            tr(pw[:, j * 128:(j + 1) * 128],
                               aoTc[:, dt * ACH + tb * 128: dt * ACH + tb * 128 + 128],
                               ident_f[:],
                               reads=[("aoTc",), ("identf",)], writes=[(pwn, j // 4)])
                        vop("tensor_scalar", afu[:], pw[:],
                            mrcA[:, 2 * tb:2 * tb + 1], mrcA[:, 2 * tb + 1:2 * tb + 2],
                            ALU.subtract, ALU.mult,
                            reads=[(pwn, 0), (pwn, 1), ("mrcA",)], writes=[("afu",)])
                        vop("tensor_add", x1_t[:, half * 1024:(half + 1) * 1024], afu[:],
                            xa_t[:, half * 1024:(half + 1) * 1024],
                            reads=[("afu",), ("xa",)], writes=[("x1",)])
                        if half == 1:
                            dma("sync", x1_dram[tba * 128:(tba + 1) * 128, :], x1_t[:],
                                reads=[("x1",)], writes=[("x1d",)], stream=("x1d", "slot"))
                            mcol, rcol, m2col = colsB[:, 0:1], colsB[:, 1:2], colsB[:, 2:3]
                            ln_token(x1_t[:], xa_t[:], mcol, rcol, m2col, "lnB",
                                     ("x1",), ("xa",))
                            vop("tensor_scalar", ln3f[:], x1_t[:], mcol, rcol,
                                ALU.subtract, ALU.mult,
                                reads=[("x1",), ("lnB", "m"), ("lnB", "r")],
                                writes=[("ln3f",)])
                            scopy(ln3h[:], ln3f[:], reads=[("ln3f",)], writes=[("ln3h",)])
                            scopy(xa_t[:], ln3h[:], reads=[("ln3h",)], writes=[("xa",)])
                            vop("tensor_sub", ln3lo[:], ln3f[:], xa_t[:],
                                reads=[("ln3f",), ("xa",)], writes=[("ln3lo",)])
                    return go
                stages.append(st_x1h(0))
                stages.append(st_x1h(1))

                def st_l3(half, lo, tba=tba):
                    def go(pw, pwn):
                        ppb = pw[:].bitcast(BF16)
                        src = ln3lo if lo else ln3h
                        stag = ("ln3lo",) if lo else ("ln3h",)
                        for j in range(8):
                            dc = half * 8 + j
                            tr(ppb[:, j * 128:(j + 1) * 128], src[:, dc * 128:(dc + 1) * 128],
                               ident_b[:], reads=[stag, ("identb",)], writes=[(pwn, 0)])
                        dstb = lstl[half] if lo else lst[half]
                        dtag = ("lstl", half) if lo else ("lst", half)
                        scopy(dstb[:], ppb[:, 0:1024], reads=[(pwn, 0)], writes=[dtag])
                        if not lo:
                            dma("sync",
                                ln3T_dram[half * 8:(half + 1) * 8, :,
                                          tba * 128:(tba + 1) * 128].transpose([1, 0, 2]),
                                lst[half][:].rearrange("p (c t) -> p c t", c=8),
                                reads=[("lst", half)], writes=[("ln3Td", half)],
                                stream=(f"l3d{half}", "slot"))
                    return go
                stages.append(st_l3(0, False))
                stages.append(st_l3(0, True))
                stages.append(st_l3(1, False))
                stages.append(st_l3(1, True))

                def st_router(tba=tba):
                    def go(pw, pwn):
                        rwv_hi = rw_hi[:].rearrange("p (c e) -> p c e", c=16)
                        rwv_lo = rw_lo[:].rearrange("p (c e) -> p c e", c=16)
                        for half in range(2):
                            for j in range(8):
                                dc = half * 8 + j
                                first = (half == 0 and j == 0)
                                mm(pw[:, 0:8], lst[half][:, j * 128:(j + 1) * 128],
                                   rwv_hi[:, dc, :], first, False,
                                   reads=[("lst", half), ("rwhi",)], writes=[(pwn, 0)])
                                mm(pw[:, 0:8], lst[half][:, j * 128:(j + 1) * 128],
                                   rwv_lo[:, dc, :], False, False,
                                   reads=[("lst", half), ("rwlo",)], writes=[(pwn, 0)])
                                mm(pw[:, 0:8], lstl[half][:, j * 128:(j + 1) * 128],
                                   rwv_hi[:, dc, :], False, False,
                                   reads=[("lstl", half), ("rwhi",)], writes=[(pwn, 0)])
                        mm(pw[:, 0:8], ones1[:], rbl_hi[:], False, False,
                           reads=[("ones1",), ("rblhi",)], writes=[(pwn, 0)])
                        mm(pw[:, 0:8], ones1[:], rbl_lo[:], False, True,
                           reads=[("ones1",), ("rbllo",)], writes=[(pwn, 0)])
                        scopy(logit[:], pw[:, 0:8], reads=[(pwn, 0)], writes=[("logit",)])
                        g0, g1 = gsc[:, 0:1], gsc[:, 1:2]
                        m1, m2 = gsc[:, 8:16], gsc[:, 16:24]
                        vop("reduce_max", g0, logit[:], AX.X, reads=[("logit",)],
                            writes=[("g0",)])
                        vop("tensor_scalar", prob[:], logit[:], g0, None, ALU.subtract,
                            reads=[("logit",), ("g0",)], writes=[("prob",)])
                        act(prob[:], prob[:], AF.Exp, reads=[("prob",)], writes=[("prob",)])
                        vop("reduce_sum", g1, prob[:], AX.X, reads=[("prob",)],
                            writes=[("g1",)])
                        vop("reciprocal", g1, g1, reads=[("g1",)], writes=[("g1",)])
                        vop("tensor_scalar", prob[:], prob[:], g1, None, ALU.mult,
                            reads=[("prob",), ("g1",)], writes=[("prob",)])
                        vop("reduce_max", g0, prob[:], AX.X, reads=[("prob",)],
                            writes=[("g0",)])
                        vop("tensor_scalar", m1, prob[:], g0, None, ALU.is_ge,
                            reads=[("prob",), ("g0",)], writes=[("m1",)])
                        vop("scalar_tensor_tensor", m2, m1, -1000.0, prob[:],
                            ALU.mult, ALU.add,
                            reads=[("m1",), ("prob",)], writes=[("m2",)])
                        vop("reduce_max", g0, m2, AX.X, reads=[("m2",)], writes=[("g0",)])
                        vop("tensor_scalar", m2, m2, g0, None, ALU.is_ge,
                            reads=[("m2",), ("g0",)], writes=[("m2",)])
                        vop("tensor_add", m1, m1, m2, reads=[("m1",), ("m2",)],
                            writes=[("m1",)])
                        act(prob[:], prob[:], AF.Exp, reads=[("prob",)], writes=[("prob",)])
                        vop("tensor_mul", prob[:], prob[:], m1,
                            reads=[("prob",), ("m1",)], writes=[("prob",)])
                        vop("reduce_sum", g1, prob[:], AX.X, reads=[("prob",)],
                            writes=[("g1",)])
                        vop("reciprocal", g1, g1, reads=[("g1",)], writes=[("g1",)])
                        vop("tensor_scalar", w_tm[:, tba * 9: tba * 9 + 8], prob[:], g1,
                            None, ALU.mult,
                            reads=[("prob",), ("g1",)], writes=[("w_tm",)])
                    return go
                stages.append(st_router())

                def st_wT(tba=tba):
                    def go(pw, pwn):
                        tr(pw[0:8, 0:128], w_tm[:, tba * 9: tba * 9 + 8], ident_f[:],
                           reads=[("w_tm",), ("identf",)], writes=[(pwn, 0)])
                        scopy(wT_sb[0:8, tba * 128:(tba + 1) * 128], pw[0:8, 0:128],
                              reads=[(pwn, 0)], writes=[("wT",)])
                    return go
                stages.append(st_wT())
            return stages


        for c in range(NACH):
            t0 = c * ACH
            cpar = 0
            l1vc = ln1Tc[cpar][:].rearrange("p (c t) -> p c t", c=16)
            cosv, sinv = cosc2[cpar], sinc2[cpar]
            dma("sync", l1vc, ln1T_dram[:, :, t0:t0 + ACH].transpose([1, 0, 2]),
                reads=[("ln1Td",)], writes=[("ln1Tc", cpar)], stream=(f"l1c{cpar}", "slot"))
            dma("sync", cosv[:], cosT_d[:, t0:t0 + ACH], writes=[("cosc", cpar)],
                stream=(f"cosc{cpar}", "slot"))
            dma("sync", sinv[:], sinT_d[:, t0:t0 + ACH], writes=[("sinc", cpar)],
                stream=(f"sinc{cpar}", "slot"))

            def jit_mms(ot):
                cb = cbst[ot % 4]
                cbres = ("cbst", ot % 4)
                dma("sync", cb[:].rearrange("p (c o) -> p c o", c=16),
                    wq_c[ot, :, :, :],
                    reads=[("c_wq_c",)], writes=[cbres], stream=(f"cb{ot % 4}", "slot"))
                pp = PS[ot % 2]
                pres = PN[ot % 2]
                for dc in range(16):
                    mm(pp[:, 0:ACH], cb[:, dc * 128:(dc + 1) * 128], l1vc[:, dc, :],
                       dc == 0, dc == 15, reads=[cbres, ("ln1Tc", cpar)], writes=[pres])

            def jit_rope(ot):
                sl = ot % 2
                pp = PS[sl]
                pres = PN[sl]
                scopy(qpb[sl][:], pp[:, 0:ACH], reads=[pres], writes=[("qpb", sl)])
                psw = PS[2 + sl]
                psres = PN[2 + sl]
                mm(psw[:, 0:ACH], perm_b[:], qpb[sl][:], True, True,
                   reads=[("permb",), ("qpb", sl)], writes=[psres])
                vop("tensor_mul", qr1[sl][:], pp[:, 0:ACH], cosv[:],
                    reads=[pres, ("cosc", cpar)], writes=[("qr1",)])
                vop("tensor_mul", qr2[sl][:], psw[:, 0:ACH], sinv[:],
                    reads=[psres, ("sinc", cpar)], writes=[("qr2",)])
                vop("tensor_add", QTc[:, ot * ACH:(ot + 1) * ACH], qr1[sl][:], qr2[sl][:],
                    reads=[("qr1",), ("qr2",)], writes=[("QTc",)])

            for ot in range(16):
                jit_mms(ot)
                if ot > 0:
                    jit_rope(ot - 1)
            jit_rope(15)
            stq = epilogue_stages(c - 1) if c > 0 else []
            spop = [0]

            def pop_stage():
                if stq:
                    b = spop[0] % 2
                    spop[0] += 1
                    stq.pop(0)(PW0 if b == 0 else PW1, "PW0" if b == 0 else "PW1")

            nkt = 2 * (c + 1)
            npair = nkt // 2

            def pair_scores_pv(p2i):
                pp2 = p2i % 2
                h0 = 2 * p2i
                kvh = h0 // 4
                ktile, koff = kvh // 2, 64 * (kvh % 2)
                pw = PW0 if pp2 == 0 else PW1
                pwn = "PW0" if pp2 == 0 else "PW1"
                qtile, qoff = qplace(h0)
                qrhs = QTc[qoff:qoff + 64, qtile * ACH:(qtile + 2) * ACH]

                def emit_s(kt):
                    pp = PS[kt % 4]
                    pres = PN[kt % 4]
                    mm(pp[:],
                       K_sb[koff:koff + 64,
                            ktile * S + kt * 128: ktile * S + (kt + 1) * 128],
                       qrhs, True, True, reads=[("K",), ("QTc",)], writes=[pres])
                    esl = kt % 4
                    act(E_sl[esl][:], pp[:], AF.Exp, bias=-ESHIFT, scale=0.125,
                        reads=[pres], writes=[("E", esl)])
                    if kt >= nkt - 2:
                        di = 0 if kt == nkt - 2 else 1
                        vop("tensor_mul", E_sl[esl][:], E_sl[esl][:], m01d_b[di][:],
                            reads=[("E", esl), ("m01d", di)], writes=[("E", esl)])

                def emit_pv(kt):
                    esl = kt % 4
                    mm(pw[0:65, 0:512],
                       V_sb[:, kt * 520 + kvh * 65: kt * 520 + (kvh + 1) * 65],
                       E_sl[esl][:], kt == 0, kt == nkt - 1,
                       reads=[("V", kt), ("Vones",), ("E", esl)], writes=[(pwn, 0)])

                return emit_s, emit_pv

            def pair_norm_calc(p2i):
                hp = p2i % 2
                pw = PW0 if hp == 0 else PW1
                pwn = "PW0" if hp == 0 else "PW1"
                rscr = rows[0:1, 1792:2304]
                act(rscr, pw[64:65, 0:512], AF.Ln, reads=[(pwn, 0)], writes=[("rscr",)])
                act(rows_b[0:1, hp * 512:hp * 512 + 512], rscr, AF.Exp, scale=-1.0,
                    reads=[("rscr",)], writes=[(("rr", hp), "hi")])

            def pair_norm_apply(p2i):
                hp = p2i % 2
                h0 = 2 * p2i
                qtile, qoff = qplace(h0)
                pw = PW0 if hp == 0 else PW1
                pwn = "PW0" if hp == 0 else "PW1"
                mm(pw[0:64, 512:1024], ones1[0:1, 0:64],
                   rows_b[0:1, hp * 512:hp * 512 + 512], True, True,
                   reads=[("ones1",), (("rr", hp), "hi")], writes=[(pwn, 1)])
                scopy(tmpO[hp][:], pw[0:64, 0:512], reads=[(pwn, 0)], writes=[("tmpO", hp)])
                vop("tensor_mul", OTc[qoff:qoff + 64, qtile * ACH:(qtile + 2) * ACH],
                    tmpO[hp][:], pw[0:64, 512:1024],
                    reads=[("tmpO", hp), (pwn, 1)], writes=[("OTc",)])

            LA = 2
            for p2i in range(16):
                emit_s, emit_pv = pair_scores_pv(p2i)
                for kt in range(min(LA + 1, nkt)):
                    emit_s(kt)
                if p2i > 0:
                    pair_norm_calc(p2i - 1)
                late = max(0, nkt - 2)
                for kt in range(nkt):
                    if kt + LA + 1 < nkt:
                        emit_s(kt + LA + 1)
                    if kt == late and p2i > 0:
                        pair_norm_apply(p2i - 1)
                    emit_pv(kt)
                    if kt == late and p2i > 0:
                        pop_stage()
            pair_norm_calc(15)
            pair_norm_apply(15)

            for dt in range(16):
                cb = cbst[dt % 4]
                cbres = ("cbst", dt % 4)
                dma("sync", cb[:].rearrange("p (c o) -> p c o", c=16), wo_c[dt, :, :, :],
                    reads=[("c_wo_c",)], writes=[cbres], stream=(f"cb{dt % 4}", "slot"))
                pp = PS[dt % 4]
                pres = PN[dt % 4]
                for oc in range(16):
                    mm(pp[:, 0:ACH], cb[:, oc * 128:(oc + 1) * 128], OTc[:, oc * ACH:(oc + 1) * ACH],
                       oc == 0, oc == 15, reads=[cbres, ("OTc",)], writes=[pres])
                scopy(aoTc[:, dt * ACH:(dt + 1) * ACH], pp[:, 0:ACH], reads=[pres],
                      writes=[("aoTc",)])
                vop("tensor_copy", QTc[:, dt * ACH:(dt + 1) * ACH],
                    aoTc[:, dt * ACH:(dt + 1) * ACH], reads=[("aoTc",)], writes=[("QTc",)])
                act(sqb[:, dt * ACH:(dt + 1) * ACH], aoTc[:, dt * ACH:(dt + 1) * ACH],
                    AF.Square, reads=[("aoTc",)], writes=[("sqb",)])
                if dt % 4 == 3:
                    pop_stage()

            for dc in range(16):
                mm(P2[0:1, 0:ACH], onesc[:], QTc[:, dc * ACH:(dc + 1) * ACH], dc == 0, dc == 15,
                   reads=[("onesc",), ("QTc",)], writes=[("P2",)])
                mm(P3[0:1, 0:ACH], onesc[:], sqb[:, dc * ACH:(dc + 1) * ACH], dc == 0, dc == 15,
                   reads=[("onesc",), ("sqb",)], writes=[("P3",)])
            mrow = rows[0:1, 1024:1024 + ACH]
            vrow = rows[0:1, 1024 + ACH:1024 + 2 * ACH]
            trow = rows[0:1, 1024 + 2 * ACH:1024 + 3 * ACH]
            vop("tensor_scalar", mrow, P2[0:1, 0:ACH], 1.0 / D, None, ALU.mult,
                reads=[("P2",)], writes=[("mrow",)])
            vop("tensor_scalar", vrow, P3[0:1, 0:ACH], 1.0 / D, None, ALU.mult,
                reads=[("P3",)], writes=[("vrow",)])
            vop("tensor_tensor", trow, mrow, mrow, ALU.mult, reads=[("mrow",)], writes=[("trow",)])
            vop("tensor_sub", vrow, vrow, trow, reads=[("vrow",), ("trow",)], writes=[("vrow",)])
            act(vrow, vrow, AF.Ln, bias=EPS, reads=[("vrow",)], writes=[("vrow",)])
            act(vrow, vrow, AF.Exp, scale=-0.5, reads=[("vrow",)], writes=[("vrow",)])
            dma("sync", mrA_dram, rows[0:1, 1024:1024 + 2 * ACH],
                reads=[("mrow",), ("vrow",)], writes=[("mrAd",)], stream=("mrcA", "slot"))
            dma("sync", arow2[0:2, :], mrA_dram.rearrange("p (r c) -> (p r) c", r=2),
                reads=[("mrAd",)], writes=[("arow2",)], stream=("mrcA", "slot"))
            for tb in range(2):
                mm(P2[0:128, 504:506], arow2[0:2, tb * 128:(tb + 1) * 128],
                   ident_f[0:2, 0:2], True, True,
                   reads=[("arow2",), ("identf",)], writes=[("P2",)])
                scopy(mrcA[:, 2 * tb:2 * tb + 2], P2[0:128, 504:506],
                      reads=[("P2",)], writes=[("mrcA",)])

        tailq = epilogue_stages(NACH - 1)
        for i, st_ in enumerate(tailq):
            st_(PW0 if i % 2 == 0 else PW1, "PW0" if i % 2 == 0 else "PW1")
        s.barrier()
        _close(a2)
        _close(spanKV)
        _ckpt("a2")

        ups = _mk()
        ustg = sb(ups, "ustg", [1, LD], BF16)
        b1ofs = np.cumsum([0] + NH).tolist()
        for e in range(NEXP):
            for hcx in range(NH[e]):
                slw = hcx % 4
                dma("sync", w2st[slw][:], w2_c[e][hcx * 128:(hcx + 1) * 128, :],
                    reads=[("c_" + w2_c[e].name,)], writes=[("w2st", slw)],
                    stream=(f"w2{slw}", "slot"))
                for half in range(2):
                    pp = PS[half]
                    mm(pp[0:1, 0:512], gb1[:, b1ofs[e] + hcx: b1ofs[e] + hcx + 1],
                       w2st[slw][:, half * 512:(half + 1) * 512], hcx == 0, hcx == NH[e] - 1,
                       reads=[("gb1",), ("w2st", slw)], writes=[PN[half]])
            scopy(urow[0:1, 0:512], P0[0:1, 0:512], reads=[("P0",)], writes=[("urow",)])
            scopy(urow[0:1, 512:1024], P1[0:1, 0:512], reads=[("P1",)], writes=[("urow",)])
            vop("tensor_add", crow[0:1, :], crow[0:1, :], urow[0:1, :],
                reads=[("crow",), ("urow",)], writes=[("crow",)])
            s.op("scalar",
                 (lambda o, i: lambda en: en.activation(o, i, AF.Identity, bias=0.0, scale=-1.0))(
                     ustg[0:1, :], urow[0:1, :]),
                 reads=[("urow",)], writes=[("ustg",)])
            dma("gpsimd", ustack_dram[e:e + 1, :], ustg[0:1, :], reads=[("ustg",)],
                writes=[("ustackd",)], stream=("usd", "slot"))
        dma("sync", urow[0:1, :], sb2row_d, writes=[("sb2r",), ("urow",)],
            stream=("sb2", "group"))
        vop("scalar_tensor_tensor", crow[0:1, :], urow[0:1, :], 0.1, crow[0:1, :],
            ALU.mult, ALU.add, reads=[("sb2r",), ("crow",)], writes=[("crow",)])
        scopy(ustg[0:1, :], crow[0:1, :], reads=[("crow",)], writes=[("ustg",)])
        dma("gpsimd", ustack_dram[8:9, :], ustg[0:1, :], reads=[("ustg",)], writes=[("ustackd",)],
            stream=("usd", "slot"))
        scopy(urow[0:1, :], ustg[0:1, :], reads=[("ustg",)], writes=[("uhf",), ("sb2r",)])
        vop("tensor_sub", urow[0:1, :], crow[0:1, :], urow[0:1, :],
            reads=[("crow",), ("uhf",)], writes=[("ulo",)])
        scopy(ustg[0:1, :], urow[0:1, :], reads=[("ulo",)], writes=[("ustg",)])
        dma("gpsimd", ustack_dram[9:10, :], ustg[0:1, :], reads=[("ustg",)], writes=[("ustackd",)],
            stream=("usd", "slot"))
        dma("sync", ustack[:], ustack_dram, reads=[("ustackd",)], writes=[("ustack",)],
            stream=("usl", "group"))
        s.barrier()
        _close(ups)
        _ckpt("ups")

        print('SBUF free at FFN(before allocs):', nc.sbuf_bytes_remaining)
        ffn = _mk()
        ln3Tc = sb(ffn, "ln3Tc", [128, 16 * FCH], BF16)
        g_sb = sb(ffn, "g_sb", [128, 8 * FCH], BF16)
        gxT = sb(ffn, "gxT", [128, 16 * FCH], BF16)
        sc32 = sb(ffn, "sc32", [128, 32 * FCH], BF16)
        acc_sb = sb(ffn, "acc_sb", [128, 4 * LD], F32)
        accT = sb(ffn, "accT", [128, 8 * FCH], BF16)
        faoT = sb(ffn, "faoT", [128, 16 * FCH], F32)
        fx1 = sb(ffn, "fx1", [128, D], F32)
        frows = sb(ffn, "frows", [1, 3 * FCH], F32)
        frows2 = sb(ffn, "frows2", [2, FCH], F32)
        mrcF = sb(ffn, "mrcF", [128, 8], F32)
        ffu = sb(ffn, "ffu", [128, 1024], F32)

        l3v = ln3Tc[:].rearrange("p (c t) -> p c t", c=16)
        p2map = [(PW0[:, 0:512], PW0[:, 512:1024]), (PW1[:, 0:512], PW1[:, 512:1024]),
                 (P0[:], P1[:]), (P2[:], P3[:])]
        p2res = [[("PW0", 0), ("PW0", 1)], [("PW1", 0), ("PW1", 1)],
                 [("P0",), ("P1",)], [("P2",), ("P3",)]]
        p2full = [PW0[:], PW1[:], None, None]

        def ffn_final_stages(c):
            stages = []
            for tb in range(4):
                tbg = c * 4 + tb

                def go(tb=tb, tbg=tbg):
                    dma("sync", fx1[:], x1_dram[tbg * 128:(tbg + 1) * 128, :],
                        reads=[("x1d",)], writes=[("fx1",)], stream=("fx1", "slot"))
                    mcol = mrcF[:, 2 * tb:2 * tb + 1]
                    rcol = mrcF[:, 2 * tb + 1:2 * tb + 2]
                    for half in range(2):
                        pw = PW0 if half == 0 else PW1
                        pwn = "PW0" if half == 0 else "PW1"
                        for j in range(8):
                            dt = half * 8 + j
                            tr(pw[:, j * 128:(j + 1) * 128],
                               faoT[:, dt * FCH + tb * 128: dt * FCH + tb * 128 + 128],
                               ident_f[:],
                               reads=[("faoT",), ("identf",)], writes=[(pwn, j // 4)])
                        vop("tensor_scalar", ffu[:], pw[:], mcol, rcol,
                            ALU.subtract, ALU.mult,
                            reads=[(pwn, 0), (pwn, 1), ("mrcF",)], writes=[("ffu",)])
                        vop("tensor_add", fx1[:, half * 1024:(half + 1) * 1024], ffu[:],
                            fx1[:, half * 1024:(half + 1) * 1024],
                            reads=[("ffu",), ("fx1",)], writes=[("fx1",)])
                    dma("sync", y_d[tbg * 128:(tbg + 1) * 128, :], fx1[:],
                        reads=[("fx1",)], writes=[("y",)], stream=("y", "group"))
                stages.append(go)
            return stages

        for c in range(NFCH):
            fstq = ffn_final_stages(c - 1) if c > 0 else []
            if fstq:
                fstq.pop(0)()
            t0 = c * FCH
            dma("sync", l3v, ln3T_dram[:, :, t0:t0 + FCH].transpose([1, 0, 2]),
                reads=[("ln3Td", 0), ("ln3Td", 1)], writes=[("ln3Tc",)], stream=("l3c", "slot"))
            for lt in range(8):
                slc = lt % 4
                cb = cbst[slc]
                cbres = ("cbst", slc)
                dma("sync" if slc % 2 == 0 else "gpsimd",
                    cb[:].rearrange("p (c o) -> p c o", c=16), down_c[lt, :, :, :],
                    reads=[("c_down_c",)], writes=[cbres], stream=(f"cb{slc}", "slot"))
                pp = PS[lt % 4]
                pres = PN[lt % 4]
                for dc in range(16):
                    mm(pp[:], cb[:, dc * 128:(dc + 1) * 128], l3v[:, dc, :], dc == 0, dc == 15,
                       reads=[cbres, ("ln3Tc",)], writes=[pres])
                act(g_sb[:, lt * FCH:(lt + 1) * FCH], pp[:], AF.Gelu, bias=downb[:, lt:lt + 1],
                    reads=[pres, ("downb",)], writes=[("g", lt)])
                if lt % 2 == 1 and fstq:
                    fstq.pop(0)()
            for dc in range(16):
                act(gxT[:, dc * FCH:(dc + 1) * FCH], l3v[:, dc, :], AF.Gelu,
                    reads=[("ln3Tc",)], writes=[("gxT",)])
            for tsub in range(4):
                tbg = c * 4 + tsub
                for half in range(2):
                    mm(p2map[tsub][half], wT_sb[:, tbg * 128:(tbg + 1) * 128],
                       ustack[:, half * 512:(half + 1) * 512], True, True,
                       reads=[("wT",), ("wTones",), ("ustack",)],
                       writes=[p2res[tsub][half]])
                if p2full[tsub] is not None:
                    vop("tensor_copy", acc_sb[:, tsub * LD:(tsub + 1) * LD], p2full[tsub],
                        reads=p2res[tsub], writes=[("acc", tsub)])
                else:
                    vop("tensor_copy", acc_sb[:, tsub * LD:tsub * LD + 512], p2map[tsub][0],
                        reads=[p2res[tsub][0]], writes=[("acc", tsub)])
                    vop("tensor_copy", acc_sb[:, tsub * LD + 512:(tsub + 1) * LD], p2map[tsub][1],
                        reads=[p2res[tsub][1]], writes=[("acc", tsub)])

            def expert(nh, w1c4, w2cp, b1c0, gate_col, gate_const):
                for hcx in range(nh):
                    slc = hcx % 4
                    cb = cbst[slc]
                    cbres = ("cbst", slc)
                    dma("sync" if slc % 2 == 0 else "gpsimd",
                        cb[:, 0:8 * 128].rearrange("p (c o) -> p c o", c=8),
                        w1c4[hcx, :, :, :], reads=[("c_" + w1c4.name,)], writes=[cbres],
                        stream=(f"cb{slc}", "slot"))
                    pp = PS[hcx % 2]
                    pres = PN[hcx % 2]
                    for lt in range(8):
                        mm(pp[:], cb[:, lt * 128:(lt + 1) * 128], g_sb[:, lt * FCH:(lt + 1) * FCH],
                           lt == 0, lt == 7, reads=[cbres, ("g", lt)], writes=[pres])
                    act(sc32[:, hcx * FCH:(hcx + 1) * FCH], pp[:], AF.Gelu,
                        bias=b1all[:, b1c0 + hcx:b1c0 + hcx + 1],
                        reads=[pres, ("b1all",)], writes=[("sc32", hcx)])
                for hcx in range(nh):
                    slw = hcx % 4
                    dma("sync" if slw % 2 == 0 else "gpsimd",
                        w2st[slw][:], w2cp[hcx * 128:(hcx + 1) * 128, :],
                        reads=[("c_" + w2cp.name,)], writes=[("w2st", slw)],
                        stream=(f"w2{slw}", "slot"))
                    for tsub in range(4):
                        for half in range(2):
                            mm(p2map[tsub][half],
                               sc32[:, hcx * FCH + tsub * 128: hcx * FCH + tsub * 128 + 128],
                               w2st[slw][:, half * 512:(half + 1) * 512],
                               hcx == 0, hcx == nh - 1,
                               reads=[("sc32", hcx), ("w2st", slw)],
                               writes=[p2res[tsub][half]])
                for tsub in (2, 3, 0, 1):
                    tbg = c * 4 + tsub
                    gc = (w_tm[:, tbg * 9 + gate_col: tbg * 9 + gate_col + 1]
                          if gate_col is not None else gate_const)
                    grd = [("w_tm",)] if gate_col is not None else []
                    if p2full[tsub] is not None:
                        vop("scalar_tensor_tensor", acc_sb[:, tsub * LD:(tsub + 1) * LD],
                            p2full[tsub], gc, acc_sb[:, tsub * LD:(tsub + 1) * LD],
                            ALU.mult, ALU.add,
                            reads=p2res[tsub] + grd + [("acc", tsub)], writes=[("acc", tsub)])
                    else:
                        for half in range(2):
                            vop("scalar_tensor_tensor",
                                acc_sb[:, tsub * LD + half * 512: tsub * LD + (half + 1) * 512],
                                p2map[tsub][half], gc,
                                acc_sb[:, tsub * LD + half * 512: tsub * LD + (half + 1) * 512],
                                ALU.mult, ALU.add,
                                reads=[p2res[tsub][half]] + grd + [("acc", tsub)],
                                writes=[("acc", tsub)])

            for e in range(NEXP):
                expert(NH[e], w1_c[e], w2_c[e], b1ofs[e], e, None)
            expert(8, sw1_c, sw2_c, b1ofs[NEXP], None, 0.1)

            for lt in range(8):
                pp = PS[2 + lt % 2]
                pres = PN[2 + lt % 2]
                for tsub in range(4):
                    tr(pp[:, tsub * 128:(tsub + 1) * 128],
                       acc_sb[:, tsub * LD + lt * 128: tsub * LD + lt * 128 + 128], ident_f[:],
                       reads=[("acc", tsub), ("identf",)], writes=[pres])
                scopy(accT[:, lt * FCH:(lt + 1) * FCH], pp[:], reads=[pres], writes=[("accT",)])

            for dt in range(16):
                cbu = cbst[dt % 2]
                cbures = ("cbst", dt % 2)
                dma("sync" if dt % 2 == 0 else "gpsimd",
                    cbu[:, 0:8 * 128].rearrange("p (c o) -> p c o", c=8),
                    up_c[dt, :, :, :], reads=[("c_up_c",)], writes=[cbures],
                    stream=(f"cb{dt % 2}", "slot"))
                cbc = cbst[2 + dt % 2]
                cbcres = ("cbst", 2 + dt % 2)
                dma("gpsimd" if dt % 2 == 0 else "sync",
                    cbc[:].rearrange("p (c o) -> p c o", c=16), core_c[dt, :, :, :],
                    reads=[("c_core_c",)], writes=[cbcres],
                    stream=(f"cb{2 + dt % 2}", "slot"))
                pp = PS[dt % 4]
                pres = PN[dt % 4]
                for lt in range(8):
                    mm(pp[:], cbu[:, lt * 128:(lt + 1) * 128], accT[:, lt * FCH:(lt + 1) * FCH],
                       lt == 0, False, reads=[cbures, ("accT",)], writes=[pres])
                for dc in range(16):
                    mm(pp[:], cbc[:, dc * 128:(dc + 1) * 128], gxT[:, dc * FCH:(dc + 1) * FCH],
                       False, dc == 15, reads=[cbcres, ("gxT",)], writes=[pres])
                s.op("scalar",
                     (lambda o, i, b: lambda en: en.activation(o, i, AF.Identity, bias=b, scale=1.0))(
                         faoT[:, dt * FCH:(dt + 1) * FCH], pp[:], bcol[:, dt:dt + 1]),
                     reads=[pres, ("bcol",)], writes=[("faoT",)])
                act(sc32[:, (16 + dt) * FCH:(17 + dt) * FCH],
                    faoT[:, dt * FCH:(dt + 1) * FCH], AF.Square,
                    reads=[("faoT",)], writes=[("sc32", 16 + dt)])
                vop("tensor_copy", sc32[:, dt * FCH:(dt + 1) * FCH],
                    faoT[:, dt * FCH:(dt + 1) * FCH], reads=[("faoT",)], writes=[("sc32", dt)])

            for dc in range(16):
                mm(P2[0:1, 0:FCH], onesc[:], sc32[:, dc * FCH:(dc + 1) * FCH], dc == 0, dc == 15,
                   reads=[("onesc",), ("sc32", dc)], writes=[("P2",)])
                mm(P3[0:1, 0:FCH], onesc[:], sc32[:, (16 + dc) * FCH:(17 + dc) * FCH],
                   dc == 0, dc == 15, reads=[("onesc",), ("sc32", 16 + dc)], writes=[("P3",)])
            fm = frows[0:1, 0:FCH]
            fv = frows[0:1, FCH:2 * FCH]
            ft = frows[0:1, 2 * FCH:3 * FCH]
            vop("tensor_scalar", fm, P2[0:1, 0:FCH], 1.0 / D, None, ALU.mult,
                reads=[("P2",)], writes=[("fm",)])
            vop("tensor_scalar", fv, P3[0:1, 0:FCH], 1.0 / D, None, ALU.mult,
                reads=[("P3",)], writes=[("fv",)])
            vop("tensor_tensor", ft, fm, fm, ALU.mult, reads=[("fm",)], writes=[("ft",)])
            vop("tensor_sub", fv, fv, ft, reads=[("fv",), ("ft",)], writes=[("fv",)])
            act(fv, fv, AF.Ln, bias=EPS, reads=[("fv",)], writes=[("fv",)])
            act(fv, fv, AF.Exp, scale=-0.5, reads=[("fv",)], writes=[("fv",)])
            dma("sync", mrF_dram, frows[0:1, 0:2 * FCH],
                reads=[("fm",), ("fv",)], writes=[("mrFd",)], stream=("mrc", "slot"))
            dma("sync", frows2[0:2, :], mrF_dram.rearrange("p (r c) -> (p r) c", r=2),
                reads=[("mrFd",)], writes=[("frows2",)], stream=("mrc", "slot"))
            for tb in range(4):
                mm(P2[0:128, 504:506], frows2[0:2, tb * 128:(tb + 1) * 128],
                   ident_f[0:2, 0:2], True, True,
                   reads=[("frows2",), ("identf",)], writes=[("P2",)])
                scopy(mrcF[:, 2 * tb:2 * tb + 2], P2[0:128, 504:506],
                      reads=[("P2",)], writes=[("mrcF",)])

        for go in ffn_final_stages(NFCH - 1):
            go()
        if DBG_X1:
            y2_d = nc.dram_tensor("y2", [S, D], F32, kind="ExternalOutput").ap()
            dma("sync", y2_d, x1_dram, reads=[("x1d",)], writes=[("y2",)],
                stream=("y2", "group"))
            y3_d = nc.dram_tensor("y3", [16, 128, S], BF16, kind="ExternalOutput").ap()
            dma("sync", y3_d, ln3T_dram, reads=[("ln3Td", 0), ("ln3Td", 1)],
                writes=[("y3",)], stream=("y2", "group"))
            y4_d = nc.dram_tensor("y4", [10, LD], BF16, kind="ExternalOutput").ap()
            dma("sync", y4_d, ustack_dram, reads=[("ustackd",)], writes=[("y4",)],
                stream=("y2", "group"))
        _close(ffn)
    except _StopBuild:
        while _open:
            _open.pop().close()

    with nc.Block() as block:
        s.emit(block, nc, top)
    top.close()
    return nc


_NC_CACHE = []


def kernel(**inputs):
    ii = {k: np.asarray(v) for k, v in inputs.items()}
    hcs = host_constants()
    x = np.ascontiguousarray(ii["L0_attn_x"], dtype=np.float32)

    inm = {}
    qp = q_perm()
    inm["wqT"] = np.ascontiguousarray(np.asarray(ii["L0_attn_Wq"], np.float32).T[:, qp])
    inm["wkT"] = np.ascontiguousarray(ii["L0_attn_Wk"].T)
    inm["wvT"] = np.ascontiguousarray(ii["L0_attn_Wv"].T)
    inm["woT"] = np.ascontiguousarray(np.asarray(ii["L0_attn_Wo"], np.float32).T[qp, :])
    inm["down_w"] = np.ascontiguousarray(ii["L0_ffn_down_w"])
    inm["up_w"] = np.ascontiguousarray(ii["L0_ffn_up_w"])
    inm["core_w"] = np.ascontiguousarray(ii["L0_ffn_core_w"])
    inm["sw1"] = np.ascontiguousarray(ii["L0_ffn_shared_w1"])
    inm["sw2"] = np.ascontiguousarray(ii["L0_ffn_shared_w2"])
    for e in range(NEXP):
        inm[f"w1_{e}"] = np.ascontiguousarray(ii[f"L0_ffn_e{e}_w1"])
        inm[f"w2_{e}"] = np.ascontiguousarray(ii[f"L0_ffn_e{e}_w2"])
    inm["down_b_col"] = col_layout(ii["L0_ffn_down_b"])
    inm["up_b_col"] = col_layout(ii["L0_ffn_up_b"])
    inm["core_b_col"] = col_layout(ii["L0_ffn_core_b"])
    b1cols = [col_layout(ii[f"L0_ffn_e{e}_b1"]) for e in range(NEXP)]
    b1cols.append(col_layout(ii["L0_ffn_shared_b1"]))
    inm["b1all_col"] = np.ascontiguousarray(np.concatenate(b1cols, axis=1))
    inm["b2_cat_row"] = np.ascontiguousarray(
        np.concatenate([np.asarray(ii[f"L0_ffn_e{e}_b2"], np.float32)
                        for e in range(NEXP)]).reshape(1, -1))
    inm["sb2_row"] = np.ascontiguousarray(
        np.asarray(ii["L0_ffn_shared_b2"], np.float32).reshape(1, -1))
    inm["rb_row"] = np.ascontiguousarray(
        np.asarray(ii["L0_ffn_router_b"], np.float32).reshape(1, -1))
    inm["lamcost"] = hcs["lamcost"]
    rw = np.asarray(ii["L0_ffn_router_w"], np.float32)
    inm["router_w_t"] = np.ascontiguousarray(
        rw.reshape(16, 128, NEXP).transpose(1, 0, 2).reshape(128, 16 * NEXP))
    inm["cosT"] = hcs["cosT"]
    inm["sinT"] = hcs["sinT"]
    inm["m01"] = hcs["m01"]
    inm["m01p"] = hcs["m01p"]
    inm["m01d384"] = hcs["m01d384"]
    inm["m01d256"] = hcs["m01d256"]
    inm["ident"] = hcs["ident"]
    inm["permswap"] = hcs["permswap"]
    inm["constpool"] = np.ascontiguousarray(
        np.tile(np.array(CONST_VALS, np.float32)[None, :], (128, 1)))

    if not _NC_CACHE:
        _NC_CACHE.append(build_program())
    nc = _NC_CACHE[0]
    in_maps = []
    for core in range(8):
        m = dict(inm)
        m["x"] = np.ascontiguousarray(x[core])
        in_maps.append(m)
    res = run_bass_kernel_spmd(nc, in_maps, core_ids=list(range(8)))
    out = np.stack([np.asarray(res.results[i]["y"]) for i in range(8)], axis=0)
    return out.astype(np.float32)
